# version 08
# baseline (speedup 1.0000x reference)
import math
import numpy as np
import ml_dtypes
import concourse.bass as bass
import concourse.mybir as mybir
from concourse.bass_utils import run_bass_kernel_spmd

F32 = mybir.dt.float32
BF16 = mybir.dt.bfloat16
AF = mybir.ActivationFunctionType
ALU = mybir.AluOpType
AX = mybir.AxisListType

N = 4096
D = 1024
NT = N // 128
NCH = D // 128

ENGS = ("tensor", "vector", "scalar", "gpsimd", "sync")

C_ZERO, C_ONE, C_NEGHALF = 0, 1, 2
NCONST = 4


def build_consts():
    c = {}
    c["ident"] = np.eye(128, dtype=np.float32)
    c["ones_row"] = np.ones((1, 512), np.float32)
    cc = np.zeros((128, NCONST), np.float32)
    cc[:, C_ONE] = 1.0
    cc[:, C_NEGHALF] = -0.5
    c["cconst"] = cc
    p = np.arange(128)[:, None]
    f = np.arange(128)[None, :]
    c["tri_lo"] = (p >= f).astype(np.float32)
    c["tri_up"] = (p <= f).astype(np.float32)
    c["tri_lo4"] = np.tile((p >= f), (1, 4)).astype(np.float32)
    c["tri_up4"] = np.tile((p <= f), (1, 4)).astype(np.float32)
    w17 = np.zeros((1, 1040), np.float32)
    w17[0, 64::65] = 1.0
    c["wvu17row"] = w17
    f2 = np.arange(512)[None, :]
    for k, off in enumerate((0, 128, 256, 384)):
        c[f"amask{k}"] = ((p + off) >= f2).astype(np.float32)
    DH, WIN = 64, 256
    half = np.arange(0, DH, 2, dtype=np.float64)
    inv_freq = 1.0 / 10000.0 ** (half / DH)
    t = np.arange(N, dtype=np.float64)
    freqs = t[:, None] * inv_freq[None, :]
    freqs = np.concatenate([freqs, freqs], -1)
    base_scale = (half + 0.4 * DH) / (1.4 * DH)
    power = (t - N // 2) / (WIN // 2)
    scale = base_scale[None, :] ** power[:, None]
    scale = np.concatenate([scale, scale], -1)
    cos, sin = np.cos(freqs), np.sin(freqs)
    sgn = np.concatenate([-np.ones(32), np.ones(32)])[None, :]
    c["rot_all"] = np.concatenate([
        cos * scale * 8.0, sin * scale * 8.0 * sgn,
        cos / scale, sin / scale * sgn], axis=1).astype(np.float32)
    return c


class Arena:

    def __init__(self, nc, nbytes=205 * 1024):
        self.nbytes = nbytes
        self._cm = nc.sbuf_tensor("arena", [128, nbytes], mybir.dt.uint8)
        self.t = self._cm.__enter__()
        self.off = 0
        self.marks = []

    def alloc(self, cols, dtype, parts=128):
        esz = 2 if dtype == BF16 else 4
        nb = cols * esz
        off = (self.off + 63) & ~63
        assert off + nb <= self.nbytes, f"arena overflow at {off}+{nb}"
        self.off = off + nb
        ap = self.t[:, off:off + nb].bitcast(dtype)
        if parts != 128:
            ap = ap[0:parts, :]
        return ap

    def allocN(self, n, cols, dtype, parts=128):
        return [self.alloc(cols, dtype, parts) for _ in range(n)]

    def push(self):
        self.marks.append(self.off)

    def pop(self):
        self.off = self.marks.pop()


class Ctx:
    def __init__(self, sems, eng_name, eng):
        self.sems = sems
        self.cnt = {k: 0 for k in sems}
        self.eng_name = eng_name
        self.eng = eng
        self._waited = {k: 0 for k in sems}

    def op(self, engine, fn, waits=(), inc=None, inc_n=1, drain=False):
        resolved = []
        for w in waits:
            if isinstance(w, str):
                resolved.append((w, self.cnt[w]))
            else:
                resolved.append(w)
        if self.eng_name == engine:
            for s, v in resolved:
                if v > 0 and self._waited[s] < v:
                    self.eng.wait_ge(self.sems[s], v)
                    self._waited[s] = v
            ins = fn(self.eng)
            if inc is not None:
                ins.then_inc(self.sems[inc], inc_n)
            if drain:
                self.eng.drain()
        if inc is not None:
            self.cnt[inc] += inc_n
            return self.cnt[inc]
        return None

    def wait(self, engine, waits):
        resolved = [(w, self.cnt[w]) if isinstance(w, str) else w for w in waits]
        if self.eng_name == engine:
            for s, v in resolved:
                if v > 0 and self._waited[s] < v:
                    self.eng.wait_ge(self.sems[s], v)
                    self._waited[s] = v

    def barrier(self):
        dma_sems = [k for k in self.sems if k in ("gld", "qld", "qst") or k.startswith("d")]
        if self.eng_name is not None:
            for s in dma_sems:
                v = self.cnt[s]
                if v > 0 and self._waited[s] < v:
                    self.eng.wait_ge(self.sems[s], v)
                    self._waited[s] = v
        self.cnt["bar"] += 5
        target = self.cnt["bar"]
        if self.eng_name is not None:
            self.eng.sem_inc(self.sems["bar"], 1)
            self.eng.wait_ge(self.sems["bar"], target)
            self._waited["bar"] = target


SEM_NAMES = ("gld", "qld", "qst", "sT", "sT2", "sV", "sV2", "sC", "sC2", "sG", "bar") + tuple(f"d{i}" for i in range(20))

def dram_table():
    return {
        "x": ([N, D], F32),
        "Wq": ([16, 1024, 64], F32), "bq": ([16, 64], F32),
        "Wk": ([16, 1024, 64], F32), "bk": ([16, 64], F32),
        "Wvd": ([16, 1024, 16], F32), "bvd": ([16, 16], F32),
        "Wvu": ([16, 16, 64], F32), "bvu": ([16, 64], F32),
        "Wqkv": ([6, 1024, 3072], F32), "q_scale": ([6, 64], F32), "k_scale": ([6, 64], F32),
        "Wo": ([6, 1024, 1024], F32), "ln_g": ([6, 1024], F32), "ln_b": ([6, 1024], F32),
        "W1": ([6, 1024, 5460], F32), "b1": ([6, 5460], F32),
        "W2": ([6, 2730, 1024], F32), "b2": ([6, 1024], F32),
        "omega": ([64, 64], F32),
        "wq1": ([1024, 1024], F32), "wk1": ([1024, 1024], F32),
        "wv1": ([1024, 1024], F32), "wo1": ([1024, 1024], F32), "bo1": ([1024], F32),
        "W1q": ([4096, 1024], F32), "b1q": ([4096], F32),
        "W2q": ([1024, 4096], F32), "b2q": ([1024], F32),
        "ident": ([128, 128], F32), "ones_row": ([1, 512], F32), "cconst": ([128, NCONST], F32),
        "tri_lo": ([128, 128], F32), "tri_up": ([128, 128], F32),
        "tri_lo4": ([128, 512], F32), "tri_up4": ([128, 512], F32),
        "wvu17row": ([1, 1040], F32),
        "amask0": ([128, 512], F32), "amask1": ([128, 512], F32),
        "amask2": ([128, 512], F32), "amask3": ([128, 512], F32),
        "rot_all": ([N, 256], F32),
        "r0": ([N, D], F32), "xnT": ([D, N], BF16),
        "s0": ([N, D], F32),
        "x1": ([N, D], F32), "xn1T": ([D, N], BF16),
        "s1": ([N, D], F32),
        "w1tT": ([D, 4096], BF16), "w2tT": ([4096, D], BF16),
        "escr1": ([1, 1], F32), "escr2": ([1, 1], F32),
        "out": ([N, D], F32),
    }


STAGE_IO = {
    "A": (["x", "cconst", "ident"], ["r0", "xnT"], []),
    "B": (["xnT", "Wq", "bq", "Wk", "bk", "Wvd", "bvd", "Wvu", "bvu", "ident", "wvu17row",
           "cconst", "ones_row", "amask0", "amask1", "amask2", "amask3"], ["s0"], []),
    "C": (["Wqkv", "q_scale", "k_scale", "Wo", "ln_g", "ln_b", "W1", "b1", "W2", "b2",
           "cconst", "ones_row", "ident", "tri_lo", "tri_up", "tri_lo4", "tri_up4",
           "rot_all"], [], ["r0"]),
    "D": (["x", "s0", "r0", "cconst", "ident"], ["x1", "xn1T"], []),
    "E": (["W1q", "W2q", "cconst", "ident", "ones_row"], ["w1tT", "w2tT", "escr1", "escr2"], []),
    "F": (["xn1T", "omega", "wq1", "wk1", "wv1", "wo1", "bo1",
           "cconst", "ones_row", "ident", "tri_up"], ["s1"], []),
    "G": (["xn1T", "w1tT", "w2tT", "b1q", "b2q", "x1", "s1", "cconst", "ones_row"], ["out"], []),
}

CONST_NAMES = {"ident", "ones_row", "cconst", "tri_lo", "tri_up", "tri_lo4", "tri_up4",
               "amask0", "amask1", "amask2", "amask3", "rot_all", "wvu17row"}


def alloc_stageA(nc, ar, bufs):
    bufs["A_xa"] = ar.allocN(2, 1024, F32)
    bufs["A_xn"] = ar.allocN(2, 1024, F32)
    bufs["A_xnb"] = ar.allocN(2, 1024, BF16)
    bufs["A_xtv"] = ar.allocN(2, 1024, BF16)
    bufs["A_scr"] = ar.allocN(1, 1024, F32)
    bufs["A_st"] = ar.alloc(8, F32)


def emit_stageA(ctx, bufs, io):
    x, r0, xnT = io["x"], io["r0"], io["xnT"]
    xa, xn, xnb, xtv = bufs["A_xa"], bufs["A_xn"], bufs["A_xnb"], bufs["A_xtv"]
    st = bufs["A_st"]
    scr = bufs["A_scr"][0]
    cconst, psum = bufs["cconst"], bufs["psum"]
    xnT_r = xnT.rearrange("(c p) n -> p c n", p=128)

    ld = [0] * NT; vxn = [0] * NT; ccast = [0] * NT; ttr = [0] * NT
    vev = [0] * NT; st1 = [0] * NT; st2 = [0] * NT

    for i in range(NT):
        b = i % 2
        w = [("sV", vxn[i - 2])] if i >= 2 else []
        ld[i] = ctx.op("sync", lambda e, i=i, b=b: e.dma_start(
            out=xa[b], in_=x[i * 128:(i + 1) * 128, :]), waits=w, inc=f"d{b}", inc_n=16)

        ctx.op("vector", lambda e, b=b: e.tensor_scalar(
            scr, xa[b], 1.0, 0.0, ALU.mult, ALU.add, accum_out=st[:, 0:1]),
            waits=[(f"d{b}", ld[i])], drain=True)
        ctx.op("vector", lambda e, b=b: e.scalar_tensor_tensor(
            scr, xa[b], 1.0, xa[b], ALU.mult, ALU.mult, accum_out=st[:, 1:2]), drain=True)
        ctx.op("vector", lambda e: e.tensor_scalar(st[:, 2:3], st[:, 0:1], 1.0 / D, None, ALU.mult), drain=True)
        ctx.op("vector", lambda e: e.tensor_tensor(st[:, 3:4], st[:, 2:3], st[:, 2:3], ALU.mult), drain=True)
        ctx.op("vector", lambda e: e.tensor_scalar(st[:, 4:5], st[:, 1:2], 1.0 / D, None, ALU.mult), drain=True)
        ctx.op("vector", lambda e: e.tensor_tensor(st[:, 4:5], st[:, 4:5], st[:, 3:4], ALU.subtract), drain=True)
        ctx.op("vector", lambda e: e.tensor_scalar(st[:, 4:5], st[:, 4:5], 1e-5, None, ALU.add),
               inc="sV2", drain=True)
        ctx.op("scalar", lambda e: e.activation(st[:, 5:6], st[:, 4:5], AF.Ln,
               bias=cconst[:, C_ZERO:C_ZERO + 1], scale=cconst[:, C_ONE:C_ONE + 1]),
               waits=["sV2"], drain=True)
        ctx.op("scalar", lambda e: e.activation(st[:, 6:7], st[:, 5:6], AF.Exp,
               bias=cconst[:, C_ZERO:C_ZERO + 1], scale=cconst[:, C_NEGHALF:C_NEGHALF + 1]),
               inc="sC", drain=True)
        wv = [("sC", ctx.cnt["sC"])]
        if i >= 2:
            wv.append((f"d{2 + b}", st1[i - 2]))
        vxn[i] = ctx.op("vector", lambda e, b=b: e.tensor_scalar(
            xn[b], xa[b], st[:, 2:3], st[:, 6:7], ALU.subtract, ALU.mult),
            waits=wv, inc="sV", drain=True)
        wc = [("sV", vxn[i])]
        if i >= 2:
            wc.append(("sT", ttr[i - 2]))
        ccast[i] = ctx.op("scalar", lambda e, b=b: e.copy(xnb[b], xn[b]),
                          waits=wc, inc="sC2")
        st1[i] = ctx.op("sync", lambda e, i=i, b=b: e.dma_start(
            out=r0[i * 128:(i + 1) * 128, :], in_=xn[b]), waits=[("sV", vxn[i])],
            inc=f"d{2 + b}", inc_n=16)

        pb = psum[4 + b][:, :].bitcast(BF16)
        for c in range(NCH):
            wt = [("sC2", ccast[i])]
            if i >= 2 and c == 0:
                wt.append(("sV", vev[i - 2]))
            ctx.op("tensor", lambda e, b=b, c=c, pb=pb: e.transpose(
                pb[:, c * 128:(c + 1) * 128], xnb[b][:, c * 128:(c + 1) * 128], bufs["ident_sb"]),
                waits=wt, inc=("sT" if c == NCH - 1 else None))
        ttr[i] = ctx.cnt["sT"]
        we = [("sT", ttr[i])]
        if i >= 2:
            we.append((f"d{15 + b}", st2[i - 2]))
        vev[i] = ctx.op("vector", lambda e, b=b, pb=pb: e.tensor_copy(xtv[b], pb[:, 0:1024]),
                        waits=we, inc="sV", drain=True)
        st2[i] = ctx.op("sync", lambda e, i=i, b=b: e.dma_start(
            out=xnT_r[:, :, i * 128:(i + 1) * 128],
            in_=xtv[b].rearrange("p (c n) -> p c n", c=8)),
            waits=[("sV", vev[i])], inc=f"d{15 + b}", inc_n=16)


def alloc_stageB(nc, ar, bufs):
    bufs["B_xnT"] = ar.alloc(8 * 4096, BF16)
    bufs["B_vdw"] = ar.alloc(8 * 256, BF16)
    bufs["B_vdo"] = ar.alloc(32 * 272, BF16)
    bufs["B_bvdb"] = ar.alloc(256, F32)
    bufs["B_bvub"] = ar.alloc(1024, F32)
    bufs["B_bqki"] = ar.alloc(128, F32)
    bufs["B_bqk"] = ar.alloc(16, F32)
    bufs["B_qkw"] = ar.allocN(2, 2 * 8 * 128, BF16)
    bufs["B_qT"] = ar.allocN(2, 4096, BF16)
    bufs["B_kT"] = ar.allocN(2, 4096, BF16)
    bufs["B_pt"] = ar.allocN(4, 512, BF16)
    bufs["B_am"] = [ar.alloc(512, BF16) for _ in range(4)]
    bufs["B_avs"] = ar.allocN(4, 512, BF16)
    bufs["B_wvu"] = ar.alloc(1040, BF16)
    bufs["B_sst"] = ar.allocN(2, 512, F32)
    bufs["B_rcp"] = ar.alloc(8, F32)


def emit_stageB(ctx, bufs, io):
    xnT = bufs["B_xnT"]
    psum = bufs["psum"]
    cc = bufs["cconst"]
    identb = bufs["ident_sb"]

    xnT_r = io["xnT"].rearrange("(c p) n -> p c n", p=128)
    ctx.op("gpsimd", lambda e: e.dma_start(
        out=xnT.rearrange("p (c n) -> p c n", c=8), in_=xnT_r), inc="gld", inc_n=16)
    for h in range(16):
        ctx.op("gpsimd", lambda e, h=h: e.dma_start(
            out=bufs["B_vdw"].rearrange("p (c x) -> p c x", c=8)[:, :, 16 * h:16 * h + 16],
            in_=io["Wvd"][h].rearrange("(c p) r -> p c r", p=128)), inc="gld", inc_n=16)
    for k in range(4):
        ctx.op("gpsimd", lambda e, k=k: e.dma_start(
            out=bufs["B_am"][k], in_=io["amask" + str(k)]), inc="gld", inc_n=16)
    wvu_z = ctx.op("vector", lambda e: e.memset(bufs["B_wvu"][0:32, :], 0.0), inc="sV", drain=True)
    for h in range(16):
        ctx.op("gpsimd", lambda e, h=h: e.dma_start(
            out=bufs["B_wvu"][0:16, 65 * h:65 * h + 64], in_=io["Wvu"][h]),
            waits=([("sV", wvu_z)] if h == 0 else []), inc="gld", inc_n=16)
    ctx.op("gpsimd", lambda e: e.dma_start(
        out=bufs["B_wvu"][16:17, :], in_=io["wvu17row"]), inc="gld", inc_n=16)
    ld_pre = ctx.cnt["gld"]
    wvu_ones = 0
    ctx.op("sync", lambda e: e.dma_start(
        out=bufs["B_bvdb"], in_=io["bvd"].rearrange("h r -> (h r)").unsqueeze(0).partition_broadcast(128).squeeze(1)),
        inc="qld", inc_n=16)
    ctx.op("sync", lambda e: e.dma_start(
        out=bufs["B_bvub"], in_=io["bvu"].rearrange("h d -> (h d)").unsqueeze(0).partition_broadcast(128).squeeze(1)),
        inc="qld", inc_n=16)
    ctx.op("sync", lambda e: e.dma_start(out=bufs["B_bqki"][0:8, :],
        in_=io["bq"].rearrange("(j g) d -> j (g d)", g=2)), inc="qld", inc_n=16)
    ld_bqk = ctx.op("sync", lambda e: e.dma_start(out=bufs["B_bqki"][8:16, :],
        in_=io["bk"].rearrange("(j g) d -> j (g d)", g=2)), inc="qld", inc_n=16)
    ld_sync_all = ctx.cnt["qld"]

    ctx.op("tensor", lambda e: e.transpose(
        psum[2][:, 0:16], bufs["B_bqki"][0:16, :], bufs["ident_f"][0:16, 0:16]),
        waits=[("qld", ld_bqk)], inc="sT")
    _t = ctx.cnt["sT"]
    ctx.op("vector", lambda e: e.tensor_copy(bufs["B_bqk"], psum[2][:, 0:16]),
           waits=[("sT", _t)], inc="sV", drain=True)
    bqk_ready = ctx.cnt["sV"]

    vd_ev = [0] * NT
    for t in range(NT):
        bk = t % 2
        for c in range(NCH):
            w = [("gld", ld_pre)]
            if t >= 2 and c == 0:
                w.append(("sV", vd_ev[t - 2]))
            ctx.op("tensor", lambda e, t=t, c=c, bk=bk: e.matmul(
                psum[bk][:, 0:256], xnT[:, 4096 * c + 128 * t: 4096 * c + 128 * t + 128],
                bufs["B_vdw"][:, 256 * c:256 * c + 256],
                start=(c == 0), stop=(c == NCH - 1)), waits=w,
                inc=("sT" if c == NCH - 1 else None))
        mm_tok = ctx.cnt["sT"]
        ctx.op("vector", lambda e, t=t: e.memset(
            bufs["B_vdo"][:, 272 * t:272 * t + 272].rearrange("p (h x) -> p h x", h=16)[:, :, 16:17], 1.0),
            waits=[("sT", mm_tok), ("qld", ld_sync_all)], drain=True)
        vd_ev[t] = ctx.op("vector", lambda e, t=t, bk=bk: e.scalar_tensor_tensor(
            bufs["B_vdo"][:, 272 * t:272 * t + 272].rearrange("p (h x) -> p h x", h=16)[:, :, 0:16],
            psum[bk][:, 0:256].rearrange("p (h r) -> p h r", h=16),
            1.0, bufs["B_bvdb"].rearrange("p (h r) -> p h r", h=16),
            ALU.mult, ALU.add), inc="sV", drain=True)
    vd_done = ctx.cnt["sV"]

    qkw_tok = [0, 0]
    attn_done = [0, 0]
    proj_mm_done = [0, 0]

    def load_qkw(j):
        b = j % 2
        w = [("sT", proj_mm_done[b])] if j >= 2 else []
        for hh in range(2):
            h = 2 * j + hh
            ctx.op("gpsimd", lambda e, h=h, hh=hh, b=b: e.dma_start(
                out=bufs["B_qkw"][b].rearrange("p (g c x) -> p g c x", g=2, c=8)[:, 0, :, 64 * hh:64 * hh + 64],
                in_=io["Wq"][h].rearrange("(c p) d -> p c d", p=128)),
                waits=(w if hh == 0 else []), inc=f"d{10 + b}", inc_n=16)
            qkw_tok[b] = ctx.op("gpsimd", lambda e, h=h, hh=hh, b=b: e.dma_start(
                out=bufs["B_qkw"][b].rearrange("p (g c x) -> p g c x", g=2, c=8)[:, 1, :, 64 * hh:64 * hh + 64],
                in_=io["Wk"][h].rearrange("(c p) d -> p c d", p=128)), inc=f"d{10 + b}", inc_n=16)

    load_qkw(0)
    exp_tok = {}
    pv_tok = {}
    norm_tok = {}
    proj_ev = {}
    sst_st = [0, 0]
    out_ev = [0, 0]
    attvd_wr_guard = 0
    avt_ev_done = 0
    bank_exp = {0: 0, 1: 0, 2: 0, 3: 0}
    pt_guard = {0: 0, 1: 0, 2: 0, 3: 0}
    avs_ev = {}
    out_norm = {}

    for j in range(8):
        b = j % 2
        if j < 7:
            load_qkw(j + 1)
        for qk in range(2):
            for s in range(8):
                pbank = 6 + (s % 2)
                for c in range(NCH):
                    w = []
                    if c == 0:
                        w.append((f"d{10 + b}", qkw_tok[b]))
                        if s >= 2:
                            w.append(("sV", proj_ev[(j, qk, s - 2)]))
                        elif qk == 1:
                            w.append(("sV", proj_ev[(j, 0, 6 + s)]))
                        elif j > 0:
                            w.append(("sV", proj_ev[(j - 1, 1, 6 + s)]))
                        if j > 0 and qk == 0 and s < 2 and out_norm.get((s, 7), 0) >= 0 and out_norm.get((s % 2, 7), 0) > 0:
                            w.append(("sV", out_norm[(s % 2, 7)]))
                        if s == 0 and qk == 0 and j >= 2:
                            w.append(("sT", attn_done[b]))
                    ctx.op("tensor", lambda e, qk=qk, s=s, c=c, b=b, pbank=pbank: e.matmul(
                        psum[pbank][:, :],
                        bufs["B_qkw"][b][:, 1024 * qk + 128 * c:1024 * qk + 128 * c + 128],
                        xnT[:, 4096 * c + 512 * s:4096 * c + 512 * s + 512],
                        start=(c == 0), stop=(c == NCH - 1)), waits=w,
                        inc=("sT" if c == NCH - 1 else None))
                mm_tok = ctx.cnt["sT"]
                if qk == 1 and s == 7:
                    proj_mm_done[b] = mm_tok
                wv = [("sT", mm_tok), ("sV", bqk_ready)]
                if j >= 2 and qk == 0 and s == 0:
                    wv.append(("sT", attn_done[b]))
                if qk == 0:
                    proj_ev[(j, 0, s)] = ctx.op("vector", lambda e, s=s, b=b, j=j, pbank=pbank: e.tensor_scalar(
                        bufs["B_qT"][b][:, 512 * s:512 * s + 512],
                        psum[pbank][:, :], bufs["B_bqk"][:, j:j + 1], 1.0 / 32,
                        ALU.add, ALU.mult), waits=wv, inc="sV", drain=True)
                else:
                    proj_ev[(j, 1, s)] = ctx.op("vector", lambda e, s=s, b=b, j=j, pbank=pbank: e.tensor_scalar(
                        bufs["B_kT"][b][:, 512 * s:512 * s + 512],
                        psum[pbank][:, :], bufs["B_bqk"][:, 8 + j:8 + j + 1], None,
                        ALU.add), waits=wv, inc="sV", drain=True)
        proj_all_ev = ctx.cnt["sV"]

        for s in range(8):
            c0 = 4 * s
            nchunks = 32 - c0
            for ci in range(nchunks):
                c = c0 + ci
                for hh in range(2):
                    base = 64 * hh
                    h = 2 * j + hh
                    obank = 4 + hh
                    sbank = (2 * ci + hh) % 4
                    ptb = sbank
                    w = [("sV", proj_all_ev)]
                    if bank_exp[sbank] > 0:
                        w.append(("sC", bank_exp[sbank]))
                    if ci == 0:
                        if s >= 1:
                            w.append(("sV", avs_ev[(hh, s - 1)]))
                        elif j > 0:
                            w.append(("sV", avs_ev[(hh, 7)]))
                        if j == 0 and s == 0:
                            w.append(("sV", vd_done))
                    ctx.op("tensor", lambda e, s=s, c=c, b=b, base=base, sbank=sbank: e.matmul(
                        psum[sbank][:, :],
                        bufs["B_kT"][b][base:base + 64, 128 * c:128 * c + 128],
                        bufs["B_qT"][b][base:base + 64, 512 * s:512 * s + 512],
                        start=True, stop=True), waits=w, inc="sT")
                    st_tok_mm = ctx.cnt["sT"]
                    we = [("sT", st_tok_mm)]
                    if pt_guard[ptb]:
                        we.append(("sT", pt_guard[ptb]))
                    exp_tok[(hh, s, ci)] = ctx.op("scalar", lambda e, ptb=ptb, sbank=sbank: e.activation(
                        bufs["B_pt"][ptb], psum[sbank][:, :], AF.Exp,
                        bias=cc[:, C_ZERO:C_ZERO + 1], scale=cc[:, C_ONE:C_ONE + 1]),
                        waits=we, inc="sC")
                    bank_exp[sbank] = exp_tok[(hh, s, ci)]
                    pvw, pvt = "sC", exp_tok[(hh, s, ci)]
                    if ci < 4:
                        ctx.op("vector", lambda e, ptb=ptb, ci=ci: e.tensor_tensor(
                            bufs["B_pt"][ptb], bufs["B_pt"][ptb], bufs["B_am"][ci], ALU.mult),
                            waits=[("sC", exp_tok[(hh, s, ci)])], inc="sV", drain=True)
                        pvw, pvt = "sV", ctx.cnt["sV"]
                    ctx.op("tensor", lambda e, c=c, h=h, ptb=ptb, obank=obank, ci=ci, nch=nchunks: e.matmul(
                        psum[obank][0:17, :],
                        bufs["B_vdo"][:, 272 * c + 17 * h:272 * c + 17 * h + 17],
                        bufs["B_pt"][ptb],
                        start=(ci == 0), stop=(ci == nch - 1)), waits=[(pvw, pvt)], inc="sT")
                    pt_guard[ptb] = ctx.cnt["sT"]
            for hh in range(2):
                base = 64 * hh
                h = 2 * j + hh
                obank = 4 + hh
                avb = 2 * (s % 2) + hh
                wv = [("sT", pt_guard[(2 * (nchunks - 1) + hh) % 4])]
                avs_ev[(hh, s)] = ctx.op("vector", lambda e, avb=avb, obank=obank: e.tensor_copy(
                    bufs["B_avs"][avb][0:17, :], psum[obank][0:17, :]), waits=wv, inc="sV", drain=True)
                ob = 6 + hh
                for t in range(4):
                    w2 = []
                    if t == 0:
                        w2.append(("sV", avs_ev[(hh, s)]))
                        key = (hh, s - 1) if s >= 1 else (hh, 7)
                        if key in out_norm and out_norm[key] > 0:
                            w2.append(("sV", out_norm[key]))
                    ctx.op("tensor", lambda e, t=t, h=h, avb=avb, ob=ob: e.matmul(
                        psum[ob][:, 128 * t:128 * t + 65],
                        bufs["B_avs"][avb][0:17, 128 * t:128 * t + 128],
                        bufs["B_wvu"][0:17, 65 * h:65 * h + 65],
                        start=(t == 0), stop=(t == 3)), waits=w2,
                        inc=("sT" if t == 3 else None))
                omm = ctx.cnt["sT"]
                wv3 = [("sT", omm)]
                if sst_st[s % 2] > 0 and hh == 0:
                    wv3.append((f"d{2 + s % 2}", sst_st[s % 2]))
                ctx.op("vector", lambda e, ob=ob: e.reciprocal(
                    bufs["B_rcp"][:, 0:4].unsqueeze(2),
                    psum[ob][:, :].rearrange("p (t x) -> p t x", t=4)[:, :, 64:65]),
                    waits=wv3, drain=True)
                ctx.op("vector", lambda e, s=s, hh=hh, ob=ob: e.scalar_tensor_tensor(
                    bufs["B_sst"][s % 2].rearrange("p (t x) -> p t x", t=4)[:, :, 64 * hh:64 * hh + 64],
                    psum[ob][:, :].rearrange("p (t x) -> p t x", t=4)[:, :, 0:64],
                    1.0,
                    bufs["B_rcp"][:, 0:4].unsqueeze(2).broadcast_to((128, 4, 64)),
                    ALU.mult, ALU.mult), drain=True)
                out_norm[(hh, s)] = ctx.op("vector", lambda e, hh=hh, s=s, j=j: e.scalar_tensor_tensor(
                    bufs["B_sst"][s % 2].rearrange("p (t x) -> p t x", t=4)[:, :, 64 * hh:64 * hh + 64],
                    bufs["B_sst"][s % 2].rearrange("p (t x) -> p t x", t=4)[:, :, 64 * hh:64 * hh + 64],
                    1.0,
                    bufs["B_bvub"][:, 128 * j + 64 * hh:128 * j + 64 * hh + 64].unsqueeze(1).broadcast_to((128, 4, 64)),
                    ALU.mult, ALU.add), inc="sV", drain=True)
            sst_st[s % 2] = ctx.op("sync", lambda e, s=s, j=j: e.dma_start(
                out=io["s0"][512 * s:512 * s + 512, 128 * j:128 * j + 128].rearrange(
                    "(t p) c -> p t c", p=128),
                in_=bufs["B_sst"][s % 2].rearrange("p (t x) -> p t x", t=4)),
                waits=[("sV", out_norm[(1, s)])], inc=f"d{2 + s % 2}", inc_n=16)
        attn_done[b] = ctx.cnt["sT"]



def alloc_stageC(nc, ar, bufs):
    bufs["C_tbl"] = ar.allocN(2, 256, F32)
    ar.push()
    bufs["C_wqkv"] = ar.alloc(8 * 3072, BF16)
    bufs["C_wo"] = ar.alloc(8 * 1024, BF16)
    bufs["C_qsb"] = ar.alloc(4 * 64, F32)
    bufs["C_ra"] = ar.allocN(2, 1024, F32)
    bufs["C_rb"] = ar.alloc(1024, BF16)
    bufs["C_rT"] = ar.allocN(2, 8 * 129, BF16)
    bufs["C_qkv"] = ar.allocN(2, 3072, BF16)
    bufs["C_scr"] = ar.alloc(2048, BF16)
    bufs["C_nrm"] = ar.alloc(48, F32)
    bufs["C_rn"] = ar.alloc(32, F32)
    bufs["C_qh"] = ar.alloc(2048, BF16)
    bufs["C_qsw"] = ar.alloc(2048, BF16)
    bufs["C_qrot"] = ar.alloc(2048, BF16)
    bufs["C_AB"] = ar.alloc(256, F32)
    bufs["C_qT"] = ar.allocN(2, 1024, BF16)
    bufs["C_kTr"] = ar.alloc(4 * 1024, BF16)
    bufs["C_vr"] = ar.alloc(4 * 1040, BF16)
    bufs["C_pt"] = ar.allocN(2, 512, BF16)
    bufs["C_ocb"] = ar.alloc(1024, BF16)
    bufs["C_ocT"] = ar.alloc(1024, BF16)
    bufs["C_rcp"] = ar.alloc(4, F32)
    bufs["C_rout"] = ar.allocN(2, 1024, F32)
    attn_hi = ar.off
    ar.pop()
    ar.push()
    bufs["C_hT"] = ar.alloc(22 * 2048, BF16)
    bufs["C_U"] = ar.alloc(22 * 1024, BF16)
    bufs["C_w1a"] = ar.allocN(2, 8 * 128, BF16)
    bufs["C_w1g"] = ar.allocN(2, 8 * 128, BF16)
    bufs["C_b1row"] = ar.alloc(5504, BF16)
    bufs["C_b2row"] = ar.alloc(1024, BF16)
    bufs["C_gl"] = ar.allocN(2, 512, F32)
    bufs["C_xs"] = ar.allocN(2, 1024, F32)
    bufs["C_xn"] = ar.alloc(1024, F32)
    bufs["C_xnb"] = ar.alloc(1024, BF16)
    bufs["C_rr"] = ar.allocN(2, 1024, F32)
    bufs["C_ro2"] = ar.allocN(2, 1024, F32)
    bufs["C_st"] = ar.alloc(16, F32)
    bufs["C_gb"] = ar.alloc(1024, F32)
    bufs["C_bb"] = ar.alloc(1024, F32)
    bufs["C_bnd"] = ar.alloc(512, F32, parts=1)
    ffn_hi = ar.off
    ar.pop()
    ar.off = max(attn_hi, ffn_hi)


C_LAYERS = 6
C_ATTN = True
C_FFN = True
C_CUT = 99
C_SUB = 99

def emit_stageC(ctx, bufs, io):
    psum = bufs["psum"]
    cc = bufs["cconst"]
    identb = bufs["ident_sb"]
    r0 = io["r0"]
    rot = io["rot_all"].rearrange("(t p) d -> p t d", p=128)

    tbl_tok = [0, 0]
    tbl_user = [0, 0]

    for l in range(C_LAYERS):
        ctx.op("gpsimd", lambda e, l=l: e.dma_start(
            out=bufs["C_wqkv"].rearrange("p (c f) -> p c f", c=8),
            in_=io["Wqkv"][l].rearrange("(c p) f -> p c f", p=128)), inc="d6", inc_n=16)
        ctx.op("gpsimd", lambda e, l=l: e.dma_start(
            out=bufs["C_wo"].rearrange("p (c f) -> p c f", c=8),
            in_=io["Wo"][l].rearrange("(c p) f -> p c f", p=128)), inc="d6", inc_n=16)
        qs_ap = io["q_scale"][l]
        ks_ap = io["k_scale"][l]
        ctx.op("sync", lambda e, qs_ap=qs_ap: e.dma_start(
            out=bufs["C_qsb"][:, 0:64], in_=qs_ap.unsqueeze(0).partition_broadcast(128).squeeze(1)), inc="d12", inc_n=16)
        ctx.op("sync", lambda e, qs_ap=qs_ap: e.dma_start(
            out=bufs["C_qsb"][:, 64:96], in_=qs_ap[32:64].unsqueeze(0).partition_broadcast(128).squeeze(1)), inc="d12", inc_n=16)
        ctx.op("sync", lambda e, qs_ap=qs_ap: e.dma_start(
            out=bufs["C_qsb"][:, 96:128], in_=qs_ap[0:32].unsqueeze(0).partition_broadcast(128).squeeze(1)), inc="d12", inc_n=16)
        ctx.op("sync", lambda e, ks_ap=ks_ap: e.dma_start(
            out=bufs["C_qsb"][:, 128:192], in_=ks_ap.unsqueeze(0).partition_broadcast(128).squeeze(1)), inc="d12", inc_n=16)
        ctx.op("sync", lambda e, ks_ap=ks_ap: e.dma_start(
            out=bufs["C_qsb"][:, 192:224], in_=ks_ap[32:64].unsqueeze(0).partition_broadcast(128).squeeze(1)), inc="d12", inc_n=16)
        ctx.op("sync", lambda e, ks_ap=ks_ap: e.dma_start(
            out=bufs["C_qsb"][:, 224:256], in_=ks_ap[0:32].unsqueeze(0).partition_broadcast(128).squeeze(1)), inc="d12", inc_n=16)
        wq_tok = ctx.cnt["d6"]
        wq_tok2 = ctx.cnt["d12"]

        ctx.op("vector", lambda e: e.memset(
            bufs["C_vr"].rearrange("p (s h x) -> p s h x", s=4, h=16)[:, :, :, 64:65], 1.0), drain=True)

        ld_tok = [0] * NT
        cast_tok = [0] * NT
        rT_ev = [0] * NT
        rtr_done = [0] * NT
        qkv_ev = {}
        qkv_done = [0] * NT
        prep_done = [0] * NT
        ring_wr = [0] * NT
        pv_last = [0] * NT
        attn_sT_end = [0] * NT
        res_ev = [0] * NT
        st_tok = [0, 0]
        oc_guard = [0]
        qT_ev = [0] * NT
        bank_exp = {4: 0, 5: 0}
        pt_guard = {0: 0, 1: 0}
        norm_toks = {g: 0 for g in range(4)}

        qh_toks = [0] * NT
        rv_toks = [0] * NT
        ocT_evs = [0] * NT
        qrot_toks = [0] * NT
        prep_dones = [0] * NT

        def front(i):
            b = i % 2
            w = [("sV", res_ev[i - 2])] if i >= 2 else []
            ld_tok[i] = ctx.op("sync", lambda e, i=i, b=b: e.dma_start(
                out=bufs["C_ra"][b], in_=r0[128 * i:128 * i + 128, :]), waits=w,
                inc=f"d{b}", inc_n=16)
            wtb = [("sV", tbl_user[b])] if tbl_tok[b] > 0 else []
            tbl_tok[b] = ctx.op("sync", lambda e, i=i, b=b: e.dma_start(
                out=bufs["C_tbl"][b], in_=rot[:, i, :]), waits=wtb, inc=f"d{8 + b}", inc_n=16)
            wc = [(f"d{b}", ld_tok[i])]
            if i >= 1:
                wc.append(("sT", rtr_done[i - 1]))
            cast_tok[i] = ctx.op("gpsimd", lambda e, b=b: e.tensor_copy(bufs["C_rb"], bufs["C_ra"][b]),
                                 waits=wc, inc="sG", drain=True)
            pb = psum[2 + b][:, :].bitcast(BF16)
            for c in range(NCH):
                wt = [("sG", cast_tok[i])]
                if c == 0:
                    if i >= 2:
                        wt.append(("sV", ocT_evs[i - 2]))
                        wt.append(("sT", qkv_done[i - 2]))
                ctx.op("tensor", lambda e, c=c, pb=pb: e.transpose(
                    pb[:, 128 * c:128 * c + 128], bufs["C_rb"][:, 128 * c:128 * c + 128], identb),
                    waits=wt, inc=("sT" if c == NCH - 1 else None))
            rtr_done[i] = ctx.cnt["sT"]
            rT_ev[i] = ctx.op("vector", lambda e, b=b, pb=pb: e.tensor_copy(
                bufs["C_rT"][b].rearrange("p (c x) -> p c x", c=8)[:, :, 1:129],
                pb[:, 0:1024].rearrange("p (c x) -> p c x", c=8)),
                waits=[("sT", rtr_done[i])], inc="sV", drain=True)
            if i == 0:
                gv = ctx.op("vector", lambda e, b=b: e.memset(
                    bufs["C_rT"][b].rearrange("p (c x) -> p c x", c=8)[:, 4:8, 0:1], 0.0),
                    waits=[("sV", rT_ev[i])], inc="sV", drain=True)
            else:
                gv = ctx.op("vector", lambda e, b=b: e.tensor_copy(
                    bufs["C_rT"][b].rearrange("p (c x) -> p c x", c=8)[:, 4:8, 0:1],
                    bufs["C_rT"][1 - b].rearrange("p (c x) -> p c x", c=8)[:, 4:8, 128:129]),
                    waits=[("sV", rT_ev[i])], inc="sV", drain=True)
            rT_ready = gv
            for g in range(6):
                gb = g % 2
                for c in range(NCH):
                    wt = []
                    if c == 0:
                        wt.append(("sV", rT_ready))
                        if g == 0 and i == 0:
                            wt.append(("d6", wq_tok))
                        if g >= 2:
                            wt.append(("sC2", qkv_ev[(i, g - 2)]))
                        elif i >= 1:
                            wt.append(("sC2", qkv_ev[(i - 1, 4 + g)]))
                    off = 1 if c < 4 else 0
                    ctx.op("tensor", lambda e, c=c, g=g, b=b, gb=gb, off=off: e.matmul(
                        psum[gb][:, :],
                        bufs["C_rT"][b][:, 129 * c + off:129 * c + off + 128],
                        bufs["C_wqkv"][:, 3072 * c + 512 * g:3072 * c + 512 * g + 512],
                        start=(c == 0), stop=(c == NCH - 1)), waits=wt,
                        inc=("sT" if c == NCH - 1 else None))
                mmt = ctx.cnt["sT"]
                if g == 5:
                    qkv_done[i] = mmt
                wsc = [("sT", mmt)]
                if g == 0 and i >= 2:
                    wsc.append(("sV", qh_toks[i - 2]))
                    wsc.append(("sG", rv_toks[i - 2]))
                qkv_ev[(i, g)] = ctx.op("scalar", lambda e, g=g, gb=gb, b=b: e.copy(
                    bufs["C_qkv"][b][:, 512 * g:512 * g + 512], psum[gb][:, :]),
                    waits=wsc, inc="sC2", drain=True)

        def mid(i):
            b = i % 2
            qkv = bufs["C_qkv"][b]
            ctx.op("vector", lambda e, qkv=qkv: e.tensor_tensor(
                bufs["C_scr"], qkv[:, 0:2048], qkv[:, 0:2048], ALU.mult),
                waits=[("sC2", qkv_ev[(i, 3)])], drain=True)
            ctx.op("vector", lambda e: e.reduce_sum(
                bufs["C_nrm"][:, 0:32], bufs["C_scr"].rearrange("p (h x) -> p h x", h=32), axis=AX.X),
                drain=True)
            ctx.op("vector", lambda e: e.tensor_scalar(
                bufs["C_nrm"][:, 0:32], bufs["C_nrm"][:, 0:32], 1e-24, None, ALU.max),
                inc="sV", drain=True)
            nrm_tok = ctx.cnt["sV"]
            ctx.op("scalar", lambda e: e.activation(bufs["C_rn"][:, 0:32], bufs["C_nrm"][:, 0:32],
                   AF.Ln, bias=cc[:, C_ZERO:C_ZERO + 1], scale=cc[:, C_ONE:C_ONE + 1]),
                   waits=[("sV", nrm_tok)], drain=True)
            ctx.op("scalar", lambda e: e.activation(bufs["C_rn"][:, 0:32], bufs["C_rn"][:, 0:32],
                   AF.Exp, bias=cc[:, C_ZERO:C_ZERO + 1], scale=cc[:, C_NEGHALF:C_NEGHALF + 1]),
                   inc="sC", drain=True)
            rn_tok = ctx.cnt["sC"]
            wv = [(f"d{8 + b}", tbl_tok[b]), ("d12", wq_tok2)]
            for z in range(4):
                ctx.op("vector", lambda e, z=z, b=b: e.tensor_tensor(
                    bufs["C_AB"][:, 64 * z:64 * z + 64], bufs["C_tbl"][b][:, 64 * z:64 * z + 64],
                    bufs["C_qsb"][:, 64 * z:64 * z + 64], ALU.mult),
                    waits=(wv if z == 0 else []), inc=("sV" if z == 3 else None), drain=True)
            tbl_user[b] = ctx.cnt["sV"]
            qh_toks[i] = ctx.op("vector", lambda e, qkv=qkv: e.tensor_tensor(
                bufs["C_qh"].rearrange("p (h x) -> p h x", h=32),
                qkv[:, 0:2048].rearrange("p (h x) -> p h x", h=32),
                bufs["C_rn"][:, 0:32].unsqueeze(2).broadcast_to((128, 32, 64)), ALU.mult),
                waits=[("sC", rn_tok)], inc="sV", drain=True)
            ctx.op("gpsimd", lambda e: e.tensor_copy(
                bufs["C_qsw"].rearrange("p (h g x) -> p h g x", h=32, g=2)[:, :, 0, :],
                bufs["C_qh"].rearrange("p (h g x) -> p h g x", h=32, g=2)[:, :, 1, :]),
                waits=[("sV", qh_toks[i])], drain=True)
            ctx.op("gpsimd", lambda e: e.tensor_copy(
                bufs["C_qsw"].rearrange("p (h g x) -> p h g x", h=32, g=2)[:, :, 1, :],
                bufs["C_qh"].rearrange("p (h g x) -> p h g x", h=32, g=2)[:, :, 0, :]),
                inc="sG", drain=True)
            qsw_tok = ctx.cnt["sG"]
            ctx.op("vector", lambda e: e.tensor_tensor(
                bufs["C_scr"].rearrange("p (g h x) -> p g h x", g=2, h=16),
                bufs["C_qh"].rearrange("p (g h x) -> p g h x", g=2, h=16),
                bufs["C_AB"].rearrange("p (g two x) -> p g two x", g=2, two=2)[:, :, 0:1, :].broadcast_to((128, 2, 16, 64)),
                ALU.mult), drain=True)
            ctx.op("vector", lambda e: e.tensor_tensor(
                bufs["C_qsw"].rearrange("p (g h x) -> p g h x", g=2, h=16),
                bufs["C_qsw"].rearrange("p (g h x) -> p g h x", g=2, h=16),
                bufs["C_AB"].rearrange("p (g two x) -> p g two x", g=2, two=2)[:, :, 1:2, :].broadcast_to((128, 2, 16, 64)),
                ALU.mult), waits=[("sG", qsw_tok)], drain=True)
            ctx.op("vector", lambda e: e.tensor_tensor(
                bufs["C_qrot"], bufs["C_scr"], bufs["C_qsw"], ALU.add), inc="sV", drain=True)
            qrot_toks[i] = ctx.cnt["sV"]
            sl = i % 4
            wv2 = [("sC2", qkv_ev[(i, 5)])]
            if i >= 2:
                wv2.append(("sT", pv_last[i - 2]))
            ctx.op("gpsimd", lambda e, sl=sl, qkv=qkv: e.tensor_copy(
                bufs["C_vr"][:, 1040 * sl:1040 * sl + 1040].rearrange("p (h x) -> p h x", h=16)[:, :, 0:64],
                qkv[:, 2048:3072].rearrange("p (h x) -> p h x", h=16)),
                waits=wv2, inc="sG", drain=True)
            rv_toks[i] = ctx.cnt["sG"]

        def back(i):
            b = i % 2
            sl = i % 4
            pb = psum[2 + b][:, :].bitcast(BF16)
            for c in range(8):
                wt = [("sV", qrot_toks[i])]
                if c == 0:
                    wt.append(("sV", rT_ev[i]))
                ctx.op("tensor", lambda e, c=c, pb=pb: e.transpose(
                    pb[:, 128 * c:128 * c + 128], bufs["C_qrot"][:, 0:1024][:, 128 * c:128 * c + 128], identb),
                    waits=wt, inc=("sT" if c == 7 else None))
            qtr = ctx.cnt["sT"]
            wv3 = [("sT", qtr)]
            if i >= 2:
                wv3.append(("sT", attn_sT_end[i - 2]))
            qT_ev[i] = ctx.op("vector", lambda e, b=b, pb=pb: e.tensor_copy(
                bufs["C_qT"][b], pb[:, 0:1024]), waits=wv3, inc="sV", drain=True)
            for c in range(8):
                wt = [("sV", qT_ev[i])]
                ctx.op("tensor", lambda e, c=c, pb=pb: e.transpose(
                    pb[:, 128 * c:128 * c + 128], bufs["C_qrot"][:, 1024:2048][:, 128 * c:128 * c + 128], identb),
                    waits=(wt if c == 0 else []), inc=("sT" if c == 7 else None))
            ktr = ctx.cnt["sT"]
            wv4 = [("sT", ktr)]
            if i >= 2:
                wv4.append(("sT", pv_last[i - 2]))
            ring_wr[i] = ctx.op("vector", lambda e, sl=sl, pb=pb: e.tensor_copy(
                bufs["C_kTr"][:, 1024 * sl:1024 * sl + 1024], pb[:, 0:1024]),
                waits=wv4, inc="sV", drain=True)
            prep_dones[i] = ctx.cnt["sV"]
            cs = [c for c in (i - 2, i - 1, i) if c >= 0]
            exp_toks = {}
            for g in range(4):
                obank = 6 + (g % 2)
                for ci, c in enumerate(cs):
                    idx = g * len(cs) + ci
                    sbank = 4 + (idx % 2)
                    ptb = idx % 2
                    for m in range(4):
                        h = g + 4 * m
                        hp, par = h // 2, h % 2
                        base = 64 * par
                        wt = []
                        if m == 0:
                            wt.append(("sV", prep_dones[i]))
                            wt.append(("sG", rv_toks[i]))
                            if bank_exp[sbank] > 0:
                                wt.append(("sC", bank_exp[sbank]))
                        ctx.op("tensor", lambda e, c=c, hp=hp, base=base, b=b, m=m, sbank=sbank: e.matmul(
                            psum[sbank][:, 128 * m:128 * m + 128],
                            bufs["C_kTr"][base:base + 64, 1024 * (c % 4) + 128 * hp:1024 * (c % 4) + 128 * hp + 128],
                            bufs["C_qT"][b][base:base + 64, 128 * hp:128 * hp + 128],
                            start=True, stop=True), waits=wt,
                            inc=("sT" if m == 3 else None))
                    smm = ctx.cnt["sT"]
                    we = [("sT", smm)]
                    if pt_guard[ptb]:
                        we.append(("sT", pt_guard[ptb]))
                    exp_toks[(g, ci)] = ctx.op("scalar", lambda e, ptb=ptb, sbank=sbank: e.activation(
                        bufs["C_pt"][ptb], psum[sbank][:, :], AF.Exp,
                        bias=cc[:, C_ZERO:C_ZERO + 1], scale=cc[:, C_ONE:C_ONE + 1]),
                        waits=we, inc="sC")
                    bank_exp[sbank] = exp_toks[(g, ci)]
                    pvw, pvt = "sC", exp_toks[(g, ci)]
                    if c == i - 2:
                        ctx.op("vector", lambda e, ptb=ptb: e.tensor_tensor(
                            bufs["C_pt"][ptb], bufs["C_pt"][ptb], bufs["tri_lo4_sb"], ALU.mult),
                            waits=[("sC", exp_toks[(g, ci)])], inc="sV", drain=True)
                        pvw, pvt = "sV", ctx.cnt["sV"]
                    elif c == i:
                        ctx.op("vector", lambda e, ptb=ptb: e.tensor_tensor(
                            bufs["C_pt"][ptb], bufs["C_pt"][ptb], bufs["tri_up4_sb"], ALU.mult),
                            waits=[("sC", exp_toks[(g, ci)])], inc="sV", drain=True)
                        pvw, pvt = "sV", ctx.cnt["sV"]
                    for m in range(4):
                        h = g + 4 * m
                        wpv = [(pvw, pvt)] if m == 0 else []
                        if m == 0 and ci == 0 and norm_toks[(g - 2) % 4] > 0:
                            wpv.append(("sV", norm_toks[(g - 2) % 4]))
                        if m == 0 and ci == 0 and obank == 7 and i >= 1:
                            wpv.append(("sV", res_ev[i - 1]))
                        ctx.op("tensor", lambda e, m=m, c=c, h=h, ptb=ptb, obank=obank, ci=ci, ncs=len(cs): e.matmul(
                            psum[obank][:, 128 * m:128 * m + 65],
                            bufs["C_pt"][ptb][:, 128 * m:128 * m + 128],
                            bufs["C_vr"][:, 1040 * (c % 4) + 65 * h:1040 * (c % 4) + 65 * h + 65],
                            start=(ci == 0 and m == 0), stop=(ci == ncs - 1 and m == 3)), waits=wpv,
                            inc=("sT" if m == 3 else None))
                    pt_guard[ptb] = ctx.cnt["sT"]
                pv_last[i] = ctx.cnt["sT"]
                wn = [("sT", pv_last[i])]
                if oc_guard[0]:
                    wn.append(("sT", oc_guard[0]))
                ctx.op("vector", lambda e, obank=obank: e.reciprocal(
                    bufs["C_rcp"][:, 0:4].unsqueeze(2),
                    psum[obank][:, :].rearrange("p (m x) -> p m x", m=4)[:, :, 64:65]),
                    waits=wn, drain=True)
                norm_toks[g] = ctx.op("vector", lambda e, g=g, obank=obank: e.scalar_tensor_tensor(
                    bufs["C_ocb"].rearrange("p (m x) -> p m x", m=4)[:, :, 64 * g:64 * g + 64],
                    psum[obank][:, :].rearrange("p (m x) -> p m x", m=4)[:, :, 0:64],
                    1.0,
                    bufs["C_rcp"][:, 0:4].unsqueeze(2).broadcast_to((128, 4, 64)),
                    ALU.mult, ALU.mult), inc="sV", drain=True)
            attn_sT_end[i] = ctx.cnt["sT"]
            all_norm = max(norm_toks.values())
            for c in range(8):
                wt = [("sV", all_norm), ("sV", ring_wr[i])]
                ctx.op("tensor", lambda e, c=c, pb=pb: e.transpose(
                    pb[:, 128 * c:128 * c + 128], bufs["C_ocb"][:, 128 * c:128 * c + 128], identb),
                    waits=(wt if c == 0 else []), inc=("sT" if c == 7 else None))
            octr = ctx.cnt["sT"]
            ocT_evs[i] = ctx.op("vector", lambda e, pb=pb: e.tensor_copy(
                bufs["C_ocT"], pb[:, 0:1024]), waits=[("sT", octr)], inc="sV", drain=True)
            oc_guard[0] = octr
            for half in range(2):
                for c in range(NCH):
                    wt = []
                    if c == 0:
                        wt.append(("sV", ocT_evs[i]))
                        if half == 0 and i >= 1:
                            wt.append(("sV", res_ev[i - 1]))
                        if half == 1:
                            wt.append(("sV", ctx.cnt["sV"]))
                    ctx.op("tensor", lambda e, c=c, half=half: e.matmul(
                        psum[7][:, :],
                        bufs["C_ocT"][:, 128 * c:128 * c + 128],
                        bufs["C_wo"][:, 1024 * c + 512 * half:1024 * c + 512 * half + 512],
                        start=(c == 0), stop=(c == NCH - 1)), waits=wt,
                        inc=("sT" if c == NCH - 1 else None))
                womm = ctx.cnt["sT"]
                wv6 = [("sT", womm)]
                if half == 0 and st_tok[b] > 0:
                    wv6.append((f"d{2 + b}", st_tok[b]))
                res_ev[i] = ctx.op("vector", lambda e, half=half, b=b: e.tensor_tensor(
                    bufs["C_rout"][b][:, 512 * half:512 * half + 512], psum[7][:, :],
                    bufs["C_ra"][b][:, 512 * half:512 * half + 512], ALU.add),
                    waits=wv6, inc="sV", drain=True)
            st_tok[b] = ctx.op("sync", lambda e, i=i, b=b: e.dma_start(
                out=r0[128 * i:128 * i + 128, :], in_=bufs["C_rout"][b]),
                waits=[("sV", res_ev[i])], inc=f"d{2 + b}", inc_n=16)

        if C_ATTN:
            front(0)
            for i in range(NT):
                if i + 1 < NT:
                    front(i + 1)
                mid(i)
                back(i)
        ctx.barrier()

        ctx.op("gpsimd", lambda e, l=l: e.dma_start(
            out=bufs["C_b1row"][0:1, 0:5460], in_=io["b1"][l].unsqueeze(0)), inc="d6", inc_n=16)
        ctx.op("gpsimd", lambda e, l=l: e.dma_start(
            out=bufs["C_b2row"][0:1, :], in_=io["b2"][l].unsqueeze(0)), inc="d6", inc_n=16)
        ctx.op("sync", lambda e: e.dma_start(
            out=bufs["C_bnd"], in_=r0[2047:2048, 512:1024]), inc="d12", inc_n=16)
        ctx.op("sync", lambda e, l=l: e.dma_start(
            out=bufs["C_gb"], in_=io["ln_g"][l].unsqueeze(0).partition_broadcast(128).squeeze(1)), inc="d12", inc_n=16)
        ctx.op("sync", lambda e, l=l: e.dma_start(
            out=bufs["C_bb"], in_=io["ln_b"][l].unsqueeze(0).partition_broadcast(128).squeeze(1)), inc="d12", inc_n=16)
        brow_tok = ctx.cnt["d6"]
        brow_tok2 = ctx.cnt["d12"]

        for half in range(2 if C_FFN else 0):
            n0 = 2048 * half
            xs_tok = [0] * 16
            xn_ev = [0] * 16
            xnT_ev = [0] * 16
            for t in range(16):
                i = 16 * half + t
                bt = t % 2
                w = [("sV", xn_ev[t - 2])] if t >= 2 else []
                ctx.op("sync", lambda e, i=i, bt=bt: e.dma_start(
                    out=bufs["C_xs"][bt][:, 0:512], in_=r0[128 * i:128 * i + 128, 0:512]),
                    waits=w, inc=f"d{bt}", inc_n=16)
                if i == 0:
                    ctx.op("vector", lambda e, bt=bt: e.memset(bufs["C_xs"][bt][0:1, 512:1024], 0.0), drain=True)
                    xs_tok[t] = ctx.op("sync", lambda e, i=i, bt=bt: e.dma_start(
                        out=bufs["C_xs"][bt][1:128, 512:1024], in_=r0[0:127, 512:1024]),
                        inc=f"d{bt}", inc_n=16)
                elif t == 0:
                    ctx.op("sync", lambda e, bt=bt: e.dma_start(
                        out=bufs["C_xs"][bt][0:1, 512:1024], in_=bufs["C_bnd"]),
                        waits=[("d12", brow_tok2)], inc=f"d{bt}", inc_n=16)
                    xs_tok[t] = ctx.op("sync", lambda e, i=i, bt=bt: e.dma_start(
                        out=bufs["C_xs"][bt][1:128, 512:1024], in_=r0[128 * i:128 * i + 127, 512:1024]),
                        inc=f"d{bt}", inc_n=16)
                else:
                    xs_tok[t] = ctx.op("sync", lambda e, i=i, bt=bt: e.dma_start(
                        out=bufs["C_xs"][bt][:, 512:1024], in_=r0[128 * i - 1:128 * i + 127, 512:1024]),
                        inc=f"d{bt}", inc_n=16)
                ctx.op("vector", lambda e, bt=bt: e.tensor_scalar(
                    bufs["C_xn"], bufs["C_xs"][bt], 1.0, 0.0, ALU.mult, ALU.add, accum_out=bufs["C_st"][:, 0:1]),
                    waits=[(f"d{bt}", xs_tok[t]), ("d6", brow_tok), ("d12", brow_tok2)], drain=True)
                ctx.op("vector", lambda e, bt=bt: e.scalar_tensor_tensor(
                    bufs["C_xn"], bufs["C_xs"][bt], 1.0, bufs["C_xs"][bt], ALU.mult, ALU.mult,
                    accum_out=bufs["C_st"][:, 1:2]), drain=True)
                ctx.op("vector", lambda e: e.tensor_scalar(bufs["C_st"][:, 2:3], bufs["C_st"][:, 0:1], 1.0 / D, None, ALU.mult), drain=True)
                ctx.op("vector", lambda e: e.tensor_tensor(bufs["C_st"][:, 3:4], bufs["C_st"][:, 2:3], bufs["C_st"][:, 2:3], ALU.mult), drain=True)
                ctx.op("vector", lambda e: e.tensor_scalar(bufs["C_st"][:, 4:5], bufs["C_st"][:, 1:2], 1.0 / D, None, ALU.mult), drain=True)
                ctx.op("vector", lambda e: e.tensor_tensor(bufs["C_st"][:, 4:5], bufs["C_st"][:, 4:5], bufs["C_st"][:, 3:4], ALU.subtract), drain=True)
                ctx.op("vector", lambda e: e.tensor_scalar(bufs["C_st"][:, 4:5], bufs["C_st"][:, 4:5], 1e-5, None, ALU.add), inc="sV", drain=True)
                stv = ctx.cnt["sV"]
                ctx.op("scalar", lambda e: e.activation(bufs["C_st"][:, 5:6], bufs["C_st"][:, 4:5], AF.Ln,
                       bias=cc[:, C_ZERO:C_ZERO + 1], scale=cc[:, C_ONE:C_ONE + 1]),
                       waits=[("sV", stv)], drain=True)
                ctx.op("scalar", lambda e: e.activation(bufs["C_st"][:, 6:7], bufs["C_st"][:, 5:6], AF.Exp,
                       bias=cc[:, C_ZERO:C_ZERO + 1], scale=cc[:, C_NEGHALF:C_NEGHALF + 1]),
                       inc="sC", drain=True)
                rstdt = ctx.cnt["sC"]
                ctx.op("vector", lambda e, bt=bt: e.scalar_tensor_tensor(
                    bufs["C_xn"], bufs["C_xs"][bt], bufs["C_st"][:, 2:3], bufs["C_gb"],
                    ALU.subtract, ALU.mult), waits=[("sC", rstdt)], drain=True)
                xn_ev[t] = ctx.op("vector", lambda e: e.scalar_tensor_tensor(
                    bufs["C_xn"], bufs["C_xn"], bufs["C_st"][:, 6:7], bufs["C_bb"],
                    ALU.mult, ALU.add), inc="sV", drain=True)
                ctx.op("gpsimd", lambda e: e.tensor_copy(bufs["C_xnb"], bufs["C_xn"]),
                       waits=[("sV", xn_ev[t])], inc="sG", drain=True)
                csg = ctx.cnt["sG"]
                pbx = psum[bt][:, :].bitcast(BF16)
                for c in range(NCH):
                    wt = [("sG", csg)]
                    if t >= 2 and c == 0:
                        wt.append(("sV", xnT_ev[t - 2]))
                    ctx.op("tensor", lambda e, c=c, pbx=pbx: e.transpose(
                        pbx[:, 128 * c:128 * c + 128], bufs["C_xnb"][:, 128 * c:128 * c + 128], identb),
                        waits=wt, inc=("sT" if c == NCH - 1 else None))
                xtr = ctx.cnt["sT"]
                xnT_ev[t] = ctx.op("vector", lambda e, t=t, pbx=pbx: e.tensor_copy(
                    bufs["C_U"][:, 0:16384].rearrange("p (c x) -> p c x", c=8)[:, :, 128 * t:128 * t + 128],
                    pbx[:, 0:1024].rearrange("p (c x) -> p c x", c=8)),
                    waits=[("sT", xtr)], inc="sV", drain=True)
            xnT_done = ctx.cnt["sV"]

            w1_tok = [0, 0]
            w1_guard = [0, 0]
            jws = [128] * 21 + [42]
            h_ev = {}
            for j in range(22):
                js = j % 2
                jw = jws[j]
                wld = [("sT", w1_guard[js])] if w1_guard[js] else []
                ctx.op("gpsimd", lambda e, l=l, j=j, js=js, jw=jw: e.dma_start(
                    out=bufs["C_w1a"][js].rearrange("p (c x) -> p c x", c=8)[:, :, 0:jw],
                    in_=io["W1"][l].rearrange("(c p) f -> p c f", p=128)[:, :, 128 * j:128 * j + jw]),
                    waits=wld, inc=f"d{4 + js}", inc_n=16)
                w1_tok[js] = ctx.op("gpsimd", lambda e, l=l, j=j, js=js, jw=jw: e.dma_start(
                    out=bufs["C_w1g"][js].rearrange("p (c x) -> p c x", c=8)[:, :, 0:jw],
                    in_=io["W1"][l].rearrange("(c p) f -> p c f", p=128)[:, :, 2730 + 128 * j:2730 + 128 * j + jw]),
                    inc=f"d{4 + js}", inc_n=16)
                for ng in range(4):
                    ab = 2 + (ng % 2)
                    gb_ = 4 + (ng % 2)
                    for c in range(NCH):
                        wt = []
                        if c == 0:
                            wt.append((f"d{4 + js}", w1_tok[js]))
                            if j == 0:
                                wt.append(("sV", xnT_done))
                            if (j, ng - 2) in h_ev:
                                wt.append(("sV", h_ev[(j, ng - 2)]))
                            elif (j - 1, ng + 2) in h_ev:
                                wt.append(("sV", h_ev[(j - 1, ng + 2)]))
                        ctx.op("tensor", lambda e, c=c, ng=ng, js=js, jw=jw, ab=ab: e.matmul(
                            psum[ab][0:jw, :],
                            bufs["C_w1a"][js][:, 128 * c:128 * c + jw],
                            bufs["C_U"][:, 0:16384][:, 2048 * c + 512 * ng:2048 * c + 512 * ng + 512],
                            start=(c == 0), stop=False), waits=wt)
                        ctx.op("tensor", lambda e, c=c, ng=ng, js=js, jw=jw, gb_=gb_: e.matmul(
                            psum[gb_][0:jw, :],
                            bufs["C_w1g"][js][:, 128 * c:128 * c + jw],
                            bufs["C_U"][:, 0:16384][:, 2048 * c + 512 * ng:2048 * c + 512 * ng + 512],
                            start=(c == 0), stop=False))
                    ctx.op("tensor", lambda e, j=j, jw=jw, ab=ab: e.matmul(
                        psum[ab][0:jw, :], bufs["C_b1row"][0:1, 128 * j:128 * j + jw],
                        bufs["ones_sb"][0:1, 0:512], start=False, stop=True))
                    ctx.op("tensor", lambda e, j=j, jw=jw, gb_=gb_: e.matmul(
                        psum[gb_][0:jw, :], bufs["C_b1row"][0:1, 2730 + 128 * j:2730 + 128 * j + jw],
                        bufs["ones_sb"][0:1, 0:512], start=False, stop=True), inc="sT")
                    mmt = ctx.cnt["sT"]
                    glb = ng % 2
                    wgl = [("sT", mmt)]
                    if (j, ng - 2) in h_ev:
                        wgl.append(("sV", h_ev[(j, ng - 2)]))
                    elif (j - 1, ng + 2) in h_ev:
                        wgl.append(("sV", h_ev[(j - 1, ng + 2)]))
                    ctx.op("scalar", lambda e, jw=jw, gb_=gb_, glb=glb: e.activation(
                        bufs["C_gl"][glb][0:jw, :], psum[gb_][0:jw, :], AF.Gelu,
                        bias=cc[0:jw, C_ZERO:C_ZERO + 1], scale=cc[0:jw, C_ONE:C_ONE + 1]),
                        waits=wgl, inc="sC", drain=True)
                    glt = ctx.cnt["sC"]
                    h_ev[(j, ng)] = ctx.op("vector", lambda e, j=j, ng=ng, jw=jw, ab=ab, glb=glb: e.tensor_tensor(
                        bufs["C_hT"][0:jw, 2048 * j + 512 * ng:2048 * j + 512 * ng + 512],
                        psum[ab][0:jw, :], bufs["C_gl"][glb][0:jw, :], ALU.mult),
                        waits=[("sC", glt)], inc="sV", drain=True)
                w1_guard[js] = ctx.cnt["sT"]
            h_done = ctx.cnt["sV"]

            ctx.op("gpsimd", lambda e, l=l: e.dma_start(
                out=bufs["C_U"].rearrange("p (c x) -> p c x", c=22)[:, 0:21, :],
                in_=io["W2"][l][0:2688].rearrange("(c p) d -> p c d", p=128)),
                waits=[("sV", h_done)], inc="d7", inc_n=16)
            ctx.op("gpsimd", lambda e, l=l: e.dma_start(
                out=bufs["C_U"][0:42, :].rearrange("p (c x) -> p c x", c=22)[:, 21, :],
                in_=io["W2"][l][2688:2730]), inc="d7", inc_n=16)
            w2_tok = ctx.cnt["d7"]
            rr_tok = [0, 0]
            y_ev = {}
            st2_tok = [0, 0]
            for t in range(16):
                i = 16 * half + t
                bt = t % 2
                wr = []
                if t >= 2:
                    wr.append(("sV", y_ev[(t - 2, 1)]))
                rr_tok[bt] = ctx.op("sync", lambda e, i=i, bt=bt: e.dma_start(
                    out=bufs["C_rr"][bt], in_=r0[128 * i:128 * i + 128, :]),
                    waits=wr, inc=f"d{13 + bt}", inc_n=16)
                pbank = [6, 7] if t % 2 == 0 else [4, 5]
                for dh in range(2):
                    pk = pbank[dh]
                    for j in range(22):
                        jw = jws[j]
                        wt = []
                        if j == 0:
                            wt.append(("d7", w2_tok))
                            wt.append(("sV", h_done))
                            if t >= 2:
                                wt.append(("sV", y_ev[(t - 2, dh)]))
                        ctx.op("tensor", lambda e, t=t, j=j, jw=jw, dh=dh, pk=pk: e.matmul(
                            psum[pk][:, :],
                            bufs["C_hT"][0:jw, 2048 * j + 128 * t:2048 * j + 128 * t + 128],
                            bufs["C_U"][0:jw, 1024 * j + 512 * dh:1024 * j + 512 * dh + 512],
                            start=(j == 0), stop=False), waits=wt)
                    ctx.op("tensor", lambda e, dh=dh, pk=pk: e.matmul(
                        psum[pk][:, :], bufs["ones_sb"][0:1, 0:128],
                        bufs["C_b2row"][0:1, 512 * dh:512 * dh + 512],
                        start=False, stop=True), inc="sT")
                    ymm = ctx.cnt["sT"]
                    wv7 = [("sT", ymm), (f"d{13 + bt}", rr_tok[bt])]
                    if st2_tok[bt] > 0 and dh == 0:
                        wv7.append((f"d{2 + bt}", st2_tok[bt]))
                    y_ev[(t, dh)] = ctx.op("vector", lambda e, dh=dh, bt=bt, pk=pk: e.tensor_tensor(
                        bufs["C_ro2"][bt][:, 512 * dh:512 * dh + 512], psum[pk][:, :],
                        bufs["C_rr"][bt][:, 512 * dh:512 * dh + 512], ALU.add),
                        waits=wv7, inc="sV", drain=True)
                st2_tok[bt] = ctx.op("sync", lambda e, i=i, bt=bt: e.dma_start(
                    out=r0[128 * i:128 * i + 128, :], in_=bufs["C_ro2"][bt]),
                    waits=[("sV", y_ev[(t, 1)])], inc=f"d{2 + bt}", inc_n=16)
            ctx.barrier()


def alloc_stageD(nc, ar, bufs):
    bufs["D_x"] = ar.allocN(2, 1024, F32)
    bufs["D_s"] = ar.allocN(2, 1024, F32)
    bufs["D_f"] = ar.allocN(2, 1024, F32)
    bufs["D_g"] = ar.alloc(1024, F32)
    bufs["D_x1"] = ar.allocN(2, 1024, F32)
    bufs["D_xn"] = ar.allocN(2, 1024, F32)
    bufs["D_xnb"] = ar.alloc(1024, BF16)
    bufs["D_xtv"] = ar.allocN(2, 1024, BF16)
    bufs["D_st"] = ar.alloc(8, F32)
    bufs["D_scr"] = ar.alloc(1024, F32)


def emit_stageD(ctx, bufs, io):
    psum = bufs["psum"]
    cc = bufs["cconst"]
    identb = bufs["ident_sb"]
    x, s0, r0 = io["x"], io["s0"], io["r0"]
    x1, xn1T = io["x1"], io["xn1T"]
    xn1T_r = xn1T.rearrange("(c p) n -> p c n", p=128)
    st = bufs["D_st"]

    ldx = [0] * NT; lds = [0] * NT; ldf = [0] * NT
    gdone = [0] * NT; x1done = [0] * NT; xndone = [0] * NT
    stx1 = [0, 0]; stxn = [0, 0]; ttr = [0] * NT; vev = [0] * NT
    cast_t = [0] * NT

    for i in range(NT):
        b = i % 2
        wx = [("sV", x1done[i - 2])] if i >= 2 else []
        ldx[i] = ctx.op("sync", lambda e, i=i, b=b: e.dma_start(
            out=bufs["D_x"][b], in_=x[128 * i:128 * i + 128, :]), waits=wx, inc=f"d{b}", inc_n=16)
        lds[i] = ctx.op("sync", lambda e, i=i, b=b: e.dma_start(
            out=bufs["D_s"][b], in_=s0[128 * i:128 * i + 128, :]), waits=wx, inc=f"d{8 + b}", inc_n=16)
        wf = [("sC", gdone[i - 2])] if i >= 2 else []
        ldf[i] = ctx.op("sync", lambda e, i=i, b=b: e.dma_start(
            out=bufs["D_f"][b], in_=r0[128 * i:128 * i + 128, :]), waits=wf, inc=f"d{13 + b}", inc_n=16)
        wg = [(f"d{13 + b}", ldf[i])]
        if i >= 1:
            wg.append(("sV", x1done[i - 1]))
        gdone[i] = ctx.op("scalar", lambda e, b=b: e.activation(
            bufs["D_g"], bufs["D_f"][b], AF.Silu,
            bias=cc[:, C_ZERO:C_ZERO + 1], scale=cc[:, C_ONE:C_ONE + 1]),
            waits=wg, inc="sC", drain=True)
        wv = [("sC", gdone[i]), (f"d{b}", ldx[i]), (f"d{8 + b}", lds[i])]
        if i >= 2:
            wv.append((f"d{2 + b}", stx1[b]))
        ctx.op("vector", lambda e, b=b: e.tensor_tensor(
            bufs["D_scr"], bufs["D_g"], bufs["D_s"][b], ALU.mult), waits=wv, drain=True)
        x1done[i] = ctx.op("vector", lambda e, b=b: e.tensor_tensor(
            bufs["D_x1"][b], bufs["D_scr"], bufs["D_x"][b], ALU.add), inc="sV", drain=True)
        stx1[b] = ctx.op("sync", lambda e, i=i, b=b: e.dma_start(
            out=x1[128 * i:128 * i + 128, :], in_=bufs["D_x1"][b]),
            waits=[("sV", x1done[i])], inc=f"d{2 + b}", inc_n=16)
        ctx.op("vector", lambda e, b=b: e.tensor_scalar(
            bufs["D_scr"], bufs["D_x1"][b], 1.0, 0.0, ALU.mult, ALU.add, accum_out=st[:, 0:1]), drain=True)
        ctx.op("vector", lambda e, b=b: e.scalar_tensor_tensor(
            bufs["D_scr"], bufs["D_x1"][b], 1.0, bufs["D_x1"][b], ALU.mult, ALU.mult,
            accum_out=st[:, 1:2]), drain=True)
        ctx.op("vector", lambda e: e.tensor_scalar(st[:, 2:3], st[:, 0:1], 1.0 / D, None, ALU.mult), drain=True)
        ctx.op("vector", lambda e: e.tensor_tensor(st[:, 3:4], st[:, 2:3], st[:, 2:3], ALU.mult), drain=True)
        ctx.op("vector", lambda e: e.tensor_scalar(st[:, 4:5], st[:, 1:2], 1.0 / D, None, ALU.mult), drain=True)
        ctx.op("vector", lambda e: e.tensor_tensor(st[:, 4:5], st[:, 4:5], st[:, 3:4], ALU.subtract), drain=True)
        ctx.op("vector", lambda e: e.tensor_scalar(st[:, 4:5], st[:, 4:5], 1e-5, None, ALU.add), inc="sV", drain=True)
        stv = ctx.cnt["sV"]
        ctx.op("scalar", lambda e: e.activation(st[:, 5:6], st[:, 4:5], AF.Ln,
               bias=cc[:, C_ZERO:C_ZERO + 1], scale=cc[:, C_ONE:C_ONE + 1]),
               waits=[("sV", stv)], drain=True)
        ctx.op("scalar", lambda e: e.activation(st[:, 6:7], st[:, 5:6], AF.Exp,
               bias=cc[:, C_ZERO:C_ZERO + 1], scale=cc[:, C_NEGHALF:C_NEGHALF + 1]),
               inc="sC", drain=True)
        rst = ctx.cnt["sC"]
        wv2 = [("sC", rst)]
        if i >= 2:
            wv2.append(("sG", cast_t[i - 2]))
        xndone[i] = ctx.op("vector", lambda e, b=b: e.tensor_scalar(
            bufs["D_xn"][b], bufs["D_x1"][b], st[:, 2:3], st[:, 6:7], ALU.subtract, ALU.mult),
            waits=wv2, inc="sV", drain=True)
        wcast = [("sV", xndone[i])]
        if i >= 1:
            wcast.append(("sT", ttr[i - 1]))
        cast_t[i] = ctx.op("gpsimd", lambda e, b=b: e.tensor_copy(bufs["D_xnb"], bufs["D_xn"][b]),
                           waits=wcast, inc="sG", drain=True)
        pb = psum[4 + b][:, :].bitcast(BF16)
        for c in range(NCH):
            wt = [("sG", cast_t[i])]
            if i >= 2 and c == 0:
                wt.append(("sV", vev[i - 2]))
            ctx.op("tensor", lambda e, c=c, pb=pb: e.transpose(
                pb[:, 128 * c:128 * c + 128], bufs["D_xnb"][:, 128 * c:128 * c + 128], identb),
                waits=wt, inc=("sT" if c == NCH - 1 else None))
        ttr[i] = ctx.cnt["sT"]
        we = [("sT", ttr[i])]
        if i >= 2:
            we.append((f"d{15 + b}", stxn[b]))
        vev[i] = ctx.op("vector", lambda e, b=b, pb=pb: e.tensor_copy(bufs["D_xtv"][b], pb[:, 0:1024]),
                        waits=we, inc="sV", drain=True)
        stxn[b] = ctx.op("sync", lambda e, i=i, b=b: e.dma_start(
            out=xn1T_r[:, :, 128 * i:128 * i + 128],
            in_=bufs["D_xtv"][b].rearrange("p (c n) -> p c n", c=8)),
            waits=[("sV", vev[i])], inc=f"d{15 + b}", inc_n=16)


def alloc_stageE(nc, ar, bufs):
    bufs["E_w"] = ar.allocN(2, 4096, F32)
    bufs["E_acc"] = ar.alloc(4, F32)
    bufs["E_scr"] = ar.alloc(128, F32)
    bufs["E_thr"] = ar.alloc(4, F32)
    bufs["E_ge"] = ar.alloc(4096, F32)
    bufs["E_le"] = ar.alloc(4096, F32)
    bufs["E_wt"] = ar.alloc(4096, BF16)
    bufs["E_tv"] = ar.allocN(2, 4096, BF16)


def _stageE_one(ctx, bufs, io, wname, oname, rows, cols, scr_name):
    psum = bufs["psum"]
    cc = bufs["cconst"]
    identb = bufs["ident_sb"]
    W = io[wname]
    OT = io[oname].rearrange("(c p) n -> p c n", p=128)
    ntile = rows // 128
    nch = cols // 128
    numel = rows * cols
    ld = [0] * ntile
    used = [0] * ntile
    for t in range(ntile):
        b = t % 2
        w = [("sV", used[t - 2])] if t >= 2 else []
        ld[t] = ctx.op("sync", lambda e, t=t, b=b, W=W, cols=cols: e.dma_start(
            out=bufs["E_w"][b][:, 0:cols], in_=W[128 * t:128 * t + 128, :]),
            waits=w, inc=f"d{b}", inc_n=16)
        ctx.op("vector", lambda e, b=b, cols=cols: e.tensor_reduce(
            bufs["E_scr"][:, 0:1], bufs["E_w"][b][:, 0:cols], AX.X, ALU.add,
            apply_absolute_value=True), waits=[(f"d{b}", ld[t])], drain=True)
        if t == 0:
            used[t] = ctx.op("vector", lambda e: e.tensor_copy(
                bufs["E_acc"][:, 0:1], bufs["E_scr"][:, 0:1]), inc="sV", drain=True)
        else:
            used[t] = ctx.op("vector", lambda e: e.tensor_tensor(
                bufs["E_acc"][:, 0:1], bufs["E_acc"][:, 0:1], bufs["E_scr"][:, 0:1], ALU.add),
                inc="sV", drain=True)
    ctx.op("tensor", lambda e: e.transpose(
        psum[0][0:1, 0:128], bufs["E_acc"][:, 0:1], bufs["ident_f"]),
        waits=[("sV", used[ntile - 1])], inc="sT")
    tt = ctx.cnt["sT"]
    ctx.op("vector", lambda e: e.reduce_sum(
        bufs["E_thr"][0:1, 0:1], psum[0][0:1, 0:128], axis=AX.X), waits=[("sT", tt)], inc="sV", drain=True)
    ctx.op("vector", lambda e, numel=numel: e.tensor_scalar(
        bufs["E_thr"][0:1, 1:2], bufs["E_thr"][0:1, 0:1], 1.0 / numel, 1e-8, ALU.mult, ALU.max), drain=True)
    ctx.op("vector", lambda e: e.tensor_scalar(
        bufs["E_thr"][0:1, 2:3], bufs["E_thr"][0:1, 1:2], 0.5, None, ALU.mult), inc="sV", drain=True)
    thr_tok = ctx.cnt["sV"]
    ctx.op("sync", lambda e, scr_name=scr_name: e.dma_start(
        out=io[scr_name], in_=bufs["E_thr"][0:1, 2:3]), waits=[("sV", thr_tok)], inc="d8", inc_n=16)
    st8 = ctx.cnt["d8"]
    ctx.op("sync", lambda e, scr_name=scr_name: e.dma_start(
        out=bufs["E_thr"][:, 3:4], in_=io[scr_name].partition_broadcast(128).squeeze(1)),
        waits=[("d8", st8)], inc="d8", inc_n=16)
    bcast_tok = ctx.cnt["d8"]
    st_tok = [0, 0]
    tern_done = [0] * ntile
    for t in range(ntile):
        b = t % 2
        w = [("d8", bcast_tok)]
        if t >= 2:
            w.append(("sV", tern_done[t - 2]))
        ld2 = ctx.op("sync", lambda e, t=t, b=b, W=W, cols=cols: e.dma_start(
            out=bufs["E_w"][b][:, 0:cols], in_=W[128 * t:128 * t + 128, :]),
            waits=w, inc=f"d{b}", inc_n=16)
        wv = [(f"d{b}", ld2)]
        ctx.op("vector", lambda e, b=b, cols=cols: e.tensor_scalar(
            bufs["E_ge"][:, 0:cols], bufs["E_w"][b][:, 0:cols], bufs["E_thr"][:, 3:4], None, ALU.is_ge),
            waits=wv, drain=True)
        ctx.op("vector", lambda e, b=b, cols=cols: e.tensor_scalar(
            bufs["E_le"][:, 0:cols], bufs["E_w"][b][:, 0:cols],
            -1.0, bufs["E_thr"][:, 3:4], ALU.mult, ALU.is_ge), drain=True)
        tern_done[t] = ctx.op("vector", lambda e, cols=cols: e.tensor_sub(
            bufs["E_wt"][:, 0:cols], bufs["E_ge"][:, 0:cols], bufs["E_le"][:, 0:cols]),
            inc="sV", drain=True)
        for rnd in range((nch + 7) // 8):
            pb = psum[2 + (t + rnd) % 2][:, :].bitcast(BF16)
            ncthis = min(8, nch - 8 * rnd)
            for c in range(ncthis):
                wt = [("sV", tern_done[t])]
                if t * 8 + rnd >= 2 and c == 0:
                    wt.append(("sV", ev_guard[(t + rnd) % 2]))
                ctx.op("tensor", lambda e, c=c, rnd=rnd, pb=pb: e.transpose(
                    pb[:, 128 * c:128 * c + 128],
                    bufs["E_wt"][:, 1024 * rnd + 128 * c:1024 * rnd + 128 * c + 128], identb),
                    waits=wt, inc=("sT" if c == ncthis - 1 else None))
            trt = ctx.cnt["sT"]
            we = [("sT", trt)]
            if st_tok[(t + rnd) % 2] > 0:
                we.append((f"d{2 + (t + rnd) % 2}", st_tok[(t + rnd) % 2]))
            b2 = (t + rnd) % 2
            ev = ctx.op("vector", lambda e, b2=b2, ncthis=ncthis, pb=pb: e.tensor_copy(
                bufs["E_tv"][b2][:, 0:128 * ncthis], pb[:, 0:128 * ncthis]),
                waits=we, inc="sV", drain=True)
            ev_guard[b2] = ev
            st_tok[b2] = ctx.op("sync", lambda e, t=t, rnd=rnd, ncthis=ncthis,
                                b2=b2, OT=OT: e.dma_start(
                out=OT[:, 8 * rnd:8 * rnd + ncthis, 128 * t:128 * t + 128],
                in_=bufs["E_tv"][b2][:, 0:128 * ncthis].rearrange("p (c n) -> p c n", n=128)),
                waits=[("sV", ev)], inc=f"d{2 + b2}", inc_n=16)


def emit_stageE(ctx, bufs, io):
    global ev_guard
    ev_guard = {0: 0, 1: 0}
    _stageE_one(ctx, bufs, io, "W1q", "w1tT", 4096, 1024, "escr1")
    ctx.barrier()
    ev_guard = {0: 0, 1: 0}
    _stageE_one(ctx, bufs, io, "W2q", "w2tT", 1024, 4096, "escr2")


def alloc_stageF(nc, ar, bufs):
    bufs["F_xnT"] = ar.alloc(8 * 4096, BF16)
    bufs["F_wst"] = ar.alloc(8 * 1024, BF16)
    bufs["F_wqT"] = ar.alloc(8 * 1024, BF16)
    bufs["F_wkT"] = ar.alloc(8 * 1024, BF16)
    bufs["F_wvT"] = ar.alloc(8 * 1024, BF16)
    bufs["F_woT"] = ar.alloc(8 * 1024, BF16)
    bufs["F_om"] = ar.alloc(64, BF16)
    bufs["F_omst"] = ar.alloc(64, BF16)
    bufs["F_bo"] = ar.alloc(1024, BF16)
    bufs["F_q"] = ar.alloc(1024, F32)
    bufs["F_k"] = ar.alloc(1024, F32)
    bufs["F_qb"] = ar.alloc(1024, BF16)
    bufs["F_kb"] = ar.alloc(1024, BF16)
    bufs["F_v"] = ar.alloc(16 * 65, BF16)
    bufs["F_scr"] = ar.alloc(1024, F32)
    bufs["F_nu"] = ar.alloc(32, F32)
    bufs["F_phq"] = ar.alloc(1024, BF16)
    bufs["F_phk"] = ar.alloc(1024, BF16)
    bufs["F_pqT"] = ar.alloc(1024, BF16)
    bufs["F_pkT"] = ar.alloc(1024, BF16)
    bufs["F_AT"] = ar.allocN(2, 128, BF16)
    bufs["F_Sf"] = ar.alloc(8 * 65, F32)
    bufs["F_Sb"] = ar.alloc(8 * 65, BF16)
    bufs["F_oc"] = ar.alloc(1024, BF16)
    bufs["F_ocT"] = ar.alloc(1024, BF16)
    bufs["F_s1"] = ar.allocN(2, 1024, F32)
    bufs["F_z"] = ar.alloc(4, F32)


def emit_stageF(ctx, bufs, io):
    psum = bufs["psum"]
    cc = bufs["cconst"]
    identb = bufs["ident_sb"]
    LN8 = math.log(8.0)
    wev = {}
    at_guard = {0: 0, 1: 0}
    s1_ev_half = {}

    ctx.op("gpsimd", lambda e: e.dma_start(
        out=bufs["F_xnT"].rearrange("p (c n) -> p c n", c=8),
        in_=io["xn1T"].rearrange("(c p) n -> p c n", p=128)), inc="gld", inc_n=16)
    ctx.op("gpsimd", lambda e: e.dma_start(out=bufs["F_omst"][0:64, :], in_=io["omega"]),
           inc="gld", inc_n=16)
    ctx.op("gpsimd", lambda e: e.dma_start(out=bufs["F_bo"][0:1, :], in_=io["bo1"].unsqueeze(0)),
           inc="gld", inc_n=16)
    gld_pre = ctx.cnt["gld"]
    wT_ready = {}
    for wi, (wn, dst) in enumerate([("wq1", "F_wqT"), ("wk1", "F_wkT"),
                                    ("wv1", "F_wvT"), ("wo1", "F_woT")]):
        ldt = ctx.op("gpsimd", lambda e, wn=wn: e.dma_start(
            out=bufs["F_wst"].rearrange("p (c n) -> p c n", c=8),
            in_=io[wn].rearrange("(c p) n -> p c n", p=128)),
            waits=([("sT", wT_ready[wi - 1])] if wi > 0 else []), inc="d10", inc_n=16)
        for cf in range(8):
            pbw = psum[cf % 2][:, :].bitcast(BF16)
            for c in range(NCH):
                wt = []
                if c == 0:
                    wt.append(("d10", ldt))
                    if wi > 0 or cf >= 2:
                        wt.append(("sV", wev.get((wi, cf - 2), wev.get((wi - 1, cf + 6), 0))))
                ctx.op("tensor", lambda e, c=c, cf=cf, pbw=pbw: e.transpose(
                    pbw[:, 128 * c:128 * c + 128],
                    bufs["F_wst"][:, 1024 * cf + 128 * c:1024 * cf + 128 * c + 128], identb),
                    waits=wt, inc=("sT" if c == NCH - 1 else None))
            trt = ctx.cnt["sT"]
            wev[(wi, cf)] = ctx.op("vector", lambda e, cf=cf, dst=dst, pbw=pbw: e.tensor_copy(
                bufs[dst].rearrange("p (c n) -> p c n", c=8)[:, :, 128 * cf:128 * cf + 128],
                pbw[:, 0:1024].rearrange("p (c n) -> p c n", c=8)),
                waits=[("sT", trt)], inc="sV", drain=True)
        wT_ready[wi] = ctx.cnt["sT"]
    wT_all = ctx.cnt["sV"]
    ctx.op("tensor", lambda e: e.transpose(
        psum[2][:, :].bitcast(BF16)[0:64, 0:64], bufs["F_omst"][0:64, :], identb[0:64, 0:64]),
        waits=[("gld", gld_pre), ("sV", wT_all)], inc="sT")
    omt = ctx.cnt["sT"]
    omev = ctx.op("vector", lambda e: e.tensor_copy(
        bufs["F_om"][0:64, :], psum[2][:, :].bitcast(BF16)[0:64, 0:64]),
        waits=[("sT", omt)], inc="sV", drain=True)
    omrep = ctx.op("gpsimd", lambda e: e.dma_start(out=bufs["F_om"][64:128, :], in_=bufs["F_om"][0:64, :]),
                   waits=[("sV", omev)], inc="gld", inc_n=16)
    pre_all = ctx.cnt["gld"]
    ctx.op("vector", lambda e: e.memset(bufs["F_Sf"], 0.0), drain=True)
    ctx.op("vector", lambda e: e.memset(bufs["F_Sb"], 0.0), drain=True)
    zs = ctx.op("vector", lambda e: e.memset(
        bufs["F_v"].rearrange("p (h x) -> p h x", h=16)[:, :, 64:65], 1.0), inc="sV", drain=True)

    qkv_ev = {}
    qkv_mm_done = [0] * NT
    exp_done = {}
    bank67_guard = {6: 0, 7: 0}
    bank45_state = {4: 0, 5: 0}
    st_tok = [0, 0]
    s1_ev = [0, 0]
    scan_done = [0] * NT
    proj_done = [0] * NT
    cast_done = [0] * NT
    phT_ev = [0] * NT
    oct_ev = [0] * NT
    norm_tok = {h: 0 for h in range(16)}
    mask_tok = {h: 0 for h in range(16)}
    state_tok = {h: 0 for h in range(16)}

    for i in range(NT):
        for gi, (dstb, bks) in enumerate([("F_q", (0, 1)), ("F_k", (2, 3)), (None, (4, 5))]):
            for hf in range(2):
                bk = bks[hf]
                wsel = [bufs["F_wqT"], bufs["F_wkT"], bufs["F_wvT"]][gi]
                for c in range(NCH):
                    wt = []
                    if c == 0:
                        wt.append(("gld", pre_all))
                        wt.append(("sV", zs))
                        if i > 0:
                            wt.append(("sV", qkv_ev[(i - 1, gi, hf)]))
                        if gi == 0 and i > 0:
                            wt.append(("sV", oct_ev[i - 1]))
                        if gi == 1 and i > 0:
                            wt.append(("sV", s1_ev[(i - 1) % 2]))
                    ctx.op("tensor", lambda e, c=c, i=i, hf=hf, bk=bk, wsel=wsel: e.matmul(
                        psum[bk][:, :],
                        bufs["F_xnT"][:, 4096 * c + 128 * i:4096 * c + 128 * i + 128],
                        wsel[:, 1024 * c + 512 * hf:1024 * c + 512 * hf + 512],
                        start=(c == 0), stop=(c == NCH - 1)), waits=wt,
                        inc=("sT" if c == NCH - 1 else None))
                mmt = ctx.cnt["sT"]
                if gi < 2:
                    qkv_ev[(i, gi, hf)] = ctx.op("vector", lambda e, gi=gi, hf=hf, bk=bk, dstb=dstb: e.tensor_copy(
                        bufs[dstb][:, 512 * hf:512 * hf + 512], psum[bk][:, :]),
                        waits=[("sT", mmt)], inc="sV", drain=True)
                else:
                    wv = [("sT", mmt)]
                    if i > 0:
                        wv.append(("sT", scan_done[i - 1]))
                    qkv_ev[(i, gi, hf)] = ctx.op("vector", lambda e, hf=hf, bk=bk: e.tensor_copy(
                        bufs["F_v"].rearrange("p (h x) -> p h x", h=16)[:, 8 * hf:8 * hf + 8, 0:64],
                        psum[bk][:, :].rearrange("p (h x) -> p h x", h=8)),
                        waits=wv, inc="sV", drain=True)
        qkv_mm_done[i] = ctx.cnt["sT"]
        ctx.op("vector", lambda e: e.tensor_tensor(
            bufs["F_scr"], bufs["F_q"], bufs["F_q"], ALU.mult),
            waits=[("sV", qkv_ev[(i, 0, 1)])], drain=True)
        ctx.op("vector", lambda e: e.reduce_sum(
            bufs["F_nu"][:, 0:16], bufs["F_scr"].rearrange("p (h x) -> p h x", h=16), axis=AX.X), drain=True)
        ctx.op("vector", lambda e: e.tensor_scalar(
            bufs["F_nu"][:, 0:16], bufs["F_nu"][:, 0:16], -0.5, -LN8, ALU.mult, ALU.add), inc="sV", drain=True)
        ctx.op("vector", lambda e: e.tensor_tensor(
            bufs["F_scr"], bufs["F_k"], bufs["F_k"], ALU.mult),
            waits=[("sV", qkv_ev[(i, 1, 1)])], drain=True)
        ctx.op("vector", lambda e: e.reduce_sum(
            bufs["F_nu"][:, 16:32], bufs["F_scr"].rearrange("p (h x) -> p h x", h=16), axis=AX.X), drain=True)
        ctx.op("vector", lambda e: e.tensor_scalar(
            bufs["F_nu"][:, 16:32], bufs["F_nu"][:, 16:32], -0.5, -LN8, ALU.mult, ALU.add), inc="sV", drain=True)
        nu_tok = ctx.cnt["sV"]
        wcast = [("sV", qkv_ev[(i, 0, 1)])]
        if i > 0:
            wcast.append(("sT", proj_done[i - 1]))
        ctx.op("gpsimd", lambda e: e.tensor_copy(bufs["F_qb"], bufs["F_q"]), waits=wcast, drain=True)
        cast_done[i] = ctx.op("gpsimd", lambda e: e.tensor_copy(bufs["F_kb"], bufs["F_k"]),
                              waits=[("sV", qkv_ev[(i, 1, 1)])], inc="sG", drain=True)
        pb0 = psum[0][:, :].bitcast(BF16)
        pb1 = psum[1][:, :].bitcast(BF16)
        for c in range(8):
            wt = [("sG", cast_done[i])]
            if c == 0:
                wt.append(("sV", qkv_ev[(i, 0, 1)]))
            ctx.op("tensor", lambda e, c=c, pb0=pb0: e.transpose(
                pb0[:, 128 * c:128 * c + 128], bufs["F_qb"][:, 128 * c:128 * c + 128], identb),
                waits=wt, inc=("sT" if c == 7 else None))
        qtr = ctx.cnt["sT"]
        qT_ev = ctx.op("vector", lambda e, pb0=pb0: e.tensor_copy(bufs["F_pqT"], pb0[:, 0:1024]),
                       waits=[("sT", qtr)], inc="sV", drain=True)
        for c in range(8):
            ctx.op("tensor", lambda e, c=c, pb1=pb1: e.transpose(
                pb1[:, 128 * c:128 * c + 128], bufs["F_kb"][:, 128 * c:128 * c + 128], identb),
                waits=([("sG", cast_done[i])] if c == 0 else []), inc=("sT" if c == 7 else None))
        ktr = ctx.cnt["sT"]
        kT_ev = ctx.op("vector", lambda e, pb1=pb1: e.tensor_copy(bufs["F_pkT"], pb1[:, 0:1024]),
                       waits=[("sT", ktr)], inc="sV", drain=True)
        for qk in range(2):
            srcT = bufs["F_pqT"] if qk == 0 else bufs["F_pkT"]
            dstP = bufs["F_phq"] if qk == 0 else bufs["F_phk"]
            for h in range(16):
                hp, par = h // 2, h % 2
                base = 64 * par
                pk = 6 + ((qk * 16 + h) % 2)
                wt = [("sV", qT_ev if qk == 0 else kT_ev)]
                if bank67_guard[pk]:
                    wt.append(("sC", bank67_guard[pk]))
                if norm_tok[14 + (pk - 6)] > 0:
                    wt.append(("sV", norm_tok[14 + (pk - 6)]))
                ctx.op("tensor", lambda e, hp=hp, base=base, srcT=srcT, pk=pk: e.matmul(
                    psum[pk][:, 0:64],
                    srcT[base:base + 64, 128 * hp:128 * hp + 128],
                    bufs["F_om"][base:base + 64, :],
                    start=True, stop=True), waits=wt, inc="sT")
                pmm = ctx.cnt["sT"]
                bcol = h if qk == 0 else 16 + h
                bank67_guard[pk] = ctx.op("scalar", lambda e, h=h, bcol=bcol, dstP=dstP, pk=pk: e.activation(
                    dstP[:, 64 * h:64 * h + 64], psum[pk][:, 0:64], AF.Exp,
                    bias=bufs["F_nu"][:, bcol:bcol + 1], scale=cc[:, C_ONE:C_ONE + 1]),
                    waits=[("sT", pmm), ("sV", nu_tok)], inc="sC")
        proj_done[i] = ctx.cnt["sT"]
        exp_phi_done = ctx.cnt["sC"]
        for c in range(8):
            wt = [("sC", exp_phi_done)]
            if c == 0:
                wt.append(("sV", qT_ev))
                wt.append(("sT", proj_done[i]))
            ctx.op("tensor", lambda e, c=c, pb0=pb0: e.transpose(
                pb0[:, 128 * c:128 * c + 128], bufs["F_phq"][:, 128 * c:128 * c + 128], identb),
                waits=wt, inc=("sT" if c == 7 else None))
        pqtr = ctx.cnt["sT"]
        pqT_ev = ctx.op("vector", lambda e, pb0=pb0: e.tensor_copy(bufs["F_pqT"], pb0[:, 0:1024]),
                        waits=[("sT", pqtr)], inc="sV", drain=True)
        for c in range(8):
            wt = [("sC", exp_phi_done)]
            if c == 0:
                wt.append(("sV", kT_ev))
            ctx.op("tensor", lambda e, c=c, pb1=pb1: e.transpose(
                pb1[:, 128 * c:128 * c + 128], bufs["F_phk"][:, 128 * c:128 * c + 128], identb),
                waits=wt, inc=("sT" if c == 7 else None))
        pktr = ctx.cnt["sT"]
        pkT_ev = ctx.op("vector", lambda e, pb1=pb1: e.tensor_copy(bufs["F_pkT"], pb1[:, 0:1024]),
                        waits=[("sT", pktr)], inc="sV", drain=True)
        phT_ev[i] = ctx.cnt["sV"]

        for h in range(16):
            hp, par = h // 2, h % 2
            base = 64 * par
            ab = 4 + (h % 2)
            ob = 6 + (h % 2)
            atb = h % 2
            pba = psum[ab][:, :].bitcast(BF16)
            wt = [("sV", pkT_ev), ("sV", pqT_ev)]
            if state_tok[(h - 2) % 16] > 0:
                wt.append(("sV", state_tok[(h - 2) % 16]))
            if i > 0 and mask_tok[(h - 2) % 16] > 0 and h < 2:
                pass
            ctx.op("tensor", lambda e, hp=hp, base=base, ab=ab: e.matmul(
                psum[ab][:, 0:128],
                bufs["F_pkT"][base:base + 64, 128 * hp:128 * hp + 128],
                bufs["F_pqT"][base:base + 64, 128 * hp:128 * hp + 128],
                start=True, stop=True), waits=wt, inc="sT")
            atmm = ctx.cnt["sT"]
            wm = [("sT", atmm)]
            if at_guard[atb] > 0:
                wm.append(("sT", at_guard[atb]))
            mask_tok[h] = ctx.op("vector", lambda e, atb=atb, ab=ab: e.tensor_tensor(
                bufs["F_AT"][atb], psum[ab][:, 0:128], bufs["tri_up_sb"], ALU.mult),
                waits=wm, inc="sV", drain=True)
            wo1 = [("sV", mask_tok[h]), ("sV", qkv_ev[(i, 2, 1)])]
            if norm_tok[(h - 2) % 16] > 0:
                wo1.append(("sV", norm_tok[(h - 2) % 16]))
            ctx.op("tensor", lambda e, h=h, atb=atb, ob=ob: e.matmul(
                psum[ob][:, 0:65], bufs["F_AT"][atb],
                bufs["F_v"][:, 65 * h:65 * h + 65], start=True, stop=False), waits=wo1)
            wo2 = []
            if state_tok[h] > 0:
                wo2.append(("sV", state_tok[h]))
            ctx.op("tensor", lambda e, h=h, hp=hp, base=base, ob=ob: e.matmul(
                psum[ob][:, 0:65],
                bufs["F_pqT"][base:base + 64, 128 * hp:128 * hp + 128],
                bufs["F_Sb"][base:base + 64, 65 * hp:65 * hp + 65],
                start=False, stop=True), waits=wo2, inc="sT")
            at_guard[atb] = ctx.cnt["sT"]
            omm = ctx.cnt["sT"]
            ctx.op("tensor", lambda e, h=h, base=base, ab=ab: e.matmul(
                psum[ab][base:base + 64, 128:193],
                bufs["F_phk"][:, 64 * h:64 * h + 64],
                bufs["F_v"][:, 65 * h:65 * h + 65], start=True, stop=True), inc="sT")
            umm = ctx.cnt["sT"]
            ctx.op("vector", lambda e, ob=ob: e.tensor_scalar(
                bufs["F_z"][:, 0:1], psum[ob][:, 64:65], 1e-6, None, ALU.add),
                waits=[("sT", omm)], drain=True)
            ctx.op("vector", lambda e: e.reciprocal(bufs["F_z"][:, 1:2], bufs["F_z"][:, 0:1]), drain=True)
            norm_tok[h] = ctx.op("vector", lambda e, h=h, ob=ob: e.tensor_scalar(
                bufs["F_oc"][:, 64 * h:64 * h + 64], psum[ob][:, 0:64],
                bufs["F_z"][:, 1:2], None, ALU.mult), inc="sV", drain=True)
            ctx.op("vector", lambda e, h=h, hp=hp, base=base, ab=ab: e.tensor_tensor(
                bufs["F_Sf"][base:base + 64, 65 * hp:65 * hp + 65],
                bufs["F_Sf"][base:base + 64, 65 * hp:65 * hp + 65],
                psum[ab][base:base + 64, 128:193], ALU.add),
                waits=[("sT", umm)], drain=True)
            state_tok[h] = ctx.op("vector", lambda e, hp=hp, base=base: e.tensor_copy(
                bufs["F_Sb"][base:base + 64, 65 * hp:65 * hp + 65],
                bufs["F_Sf"][base:base + 64, 65 * hp:65 * hp + 65]), inc="sV", drain=True)
        scan_done[i] = ctx.cnt["sT"]
        for c in range(8):
            wt = [("sV", norm_tok[2 * c + 1]), ("sV", norm_tok[2 * c])]
            if c == 0:
                wt.append(("sV", pqT_ev))
            ctx.op("tensor", lambda e, c=c, pb0=pb0: e.transpose(
                pb0[:, 128 * c:128 * c + 128], bufs["F_oc"][:, 128 * c:128 * c + 128], identb),
                waits=wt, inc=("sT" if c == 7 else None))
        octr = ctx.cnt["sT"]
        ocT_evt = ctx.op("vector", lambda e, pb0=pb0: e.tensor_copy(bufs["F_ocT"], pb0[:, 0:1024]),
                         waits=[("sT", octr)], inc="sV", drain=True)
        oct_ev[i] = ocT_evt
        bs = i % 2
        for hf in range(2):
            bk = 2 + hf
            for c in range(NCH):
                wt = []
                if c == 0:
                    wt.append(("sV", ocT_evt))
                    if (i - 1, hf) in s1_ev_half:
                        wt.append(("sV", s1_ev_half[(i - 1, hf)]))
                ctx.op("tensor", lambda e, c=c, hf=hf, bk=bk: e.matmul(
                    psum[bk][:, :],
                    bufs["F_ocT"][:, 128 * c:128 * c + 128],
                    bufs["F_woT"][:, 1024 * c + 512 * hf:1024 * c + 512 * hf + 512],
                    start=(c == 0), stop=False), waits=wt)
            ctx.op("tensor", lambda e, hf=hf, bk=bk: e.matmul(
                psum[bk][:, :], bufs["ones_sb"][0:1, 0:128],
                bufs["F_bo"][0:1, 512 * hf:512 * hf + 512], start=False, stop=True), inc="sT")
            smm = ctx.cnt["sT"]
            wv = [("sT", smm)]
            if st_tok[bs] > 0 and hf == 0:
                wv.append((f"d{2 + bs}", st_tok[bs]))
            s1_ev_half[(i, hf)] = ctx.op("vector", lambda e, hf=hf, bs=bs, bk=bk: e.tensor_copy(
                bufs["F_s1"][bs][:, 512 * hf:512 * hf + 512], psum[bk][:, :]),
                waits=wv, inc="sV", drain=True)
        s1_ev[bs] = s1_ev_half[(i, 1)]
        st_tok[bs] = ctx.op("sync", lambda e, i=i, bs=bs: e.dma_start(
            out=io["s1"][128 * i:128 * i + 128, :], in_=bufs["F_s1"][bs]),
            waits=[("sV", s1_ev_half[(i, 1)])], inc=f"d{2 + bs}", inc_n=16)


def alloc_stageG(nc, ar, bufs):
    bufs["G_w1"] = ar.alloc(8 * 4096, BF16)
    bufs["G_w2"] = ar.alloc(32 * 1024, BF16)
    bufs["G_b1"] = ar.alloc(4096, BF16)
    bufs["G_b2"] = ar.alloc(1024, BF16)
    bufs["G_xnb"] = ar.allocN(2, 8 * 256, BF16)
    bufs["G_h"] = ar.alloc(32 * 256, BF16)
    bufs["G_g"] = ar.alloc(1024, F32)
    bufs["G_x1"] = ar.allocN(2, 1024, F32)
    bufs["G_s1"] = ar.allocN(2, 1024, F32)
    bufs["G_o"] = ar.allocN(2, 1024, F32)
    bufs["G_scr"] = ar.alloc(1024, F32)


def emit_stageG(ctx, bufs, io):
    psum = bufs["psum"]
    cc = bufs["cconst"]

    ctx.op("gpsimd", lambda e: e.dma_start(
        out=bufs["G_w1"].rearrange("p (c n) -> p c n", c=8),
        in_=io["w1tT"].rearrange("(c p) n -> p c n", p=128)), inc="gld", inc_n=16)
    ctx.op("gpsimd", lambda e: e.dma_start(
        out=bufs["G_w2"].rearrange("p (c n) -> p c n", c=32),
        in_=io["w2tT"].rearrange("(c p) n -> p c n", p=128)), inc="gld", inc_n=16)
    ctx.op("gpsimd", lambda e: e.dma_start(out=bufs["G_b1"][0:1, :], in_=io["b1q"].unsqueeze(0)), inc="gld", inc_n=16)
    ctx.op("gpsimd", lambda e: e.dma_start(out=bufs["G_b2"][0:1, :], in_=io["b2q"].unsqueeze(0)), inc="gld", inc_n=16)
    gpre = ctx.cnt["gld"]

    xn1T_r = io["xn1T"].rearrange("(c p) n -> p c n", p=128)
    h_ev = {}
    y_ev = {}
    st_tok = [0, 0]
    ld_tok = [0, 0]
    x1_tok = [0, 0]
    s1_tok = [0, 0]
    gate_done = [0, 0]
    phase1_done = [0] * 16

    for nb in range(16):
        b = nb % 2
        w = [("sT", phase1_done[nb - 2])] if nb >= 2 else []
        ld_tok[b] = ctx.op("sync", lambda e, nb=nb, b=b: e.dma_start(
            out=bufs["G_xnb"][b].rearrange("p (c n) -> p c n", c=8),
            in_=xn1T_r[:, :, 256 * nb:256 * nb + 256]), waits=w, inc=f"d{b}", inc_n=16)
        for j in range(32):
            bk = j % 2
            for c in range(NCH):
                wt = []
                if c == 0:
                    wt.append((f"d{b}", ld_tok[b]))
                    if nb == 0 and j < 2:
                        wt.append(("gld", gpre))
                    if (nb, j - 2) in h_ev:
                        wt.append(("sC", h_ev[(nb, j - 2)]))
                    elif (nb - 1, j + 30) in h_ev:
                        wt.append(("sC", h_ev[(nb - 1, j + 30)]))
                ctx.op("tensor", lambda e, c=c, j=j, b=b, bk=bk: e.matmul(
                    psum[bk][:, 0:256],
                    bufs["G_w1"][:, 4096 * c + 128 * j:4096 * c + 128 * j + 128],
                    bufs["G_xnb"][b][:, 256 * c:256 * c + 256],
                    start=(c == 0), stop=False), waits=wt)
            ctx.op("tensor", lambda e, j=j, bk=bk: e.matmul(
                psum[bk][:, 0:256], bufs["G_b1"][0:1, 128 * j:128 * j + 128],
                bufs["ones_sb"][0:1, 0:256], start=False, stop=True), inc="sT")
            mmt = ctx.cnt["sT"]
            wg = [("sT", mmt)]
            if (nb, j - 2) in h_ev or (nb - 1, j + 30) in h_ev:
                pass
            h_ev[(nb, j)] = ctx.op("scalar", lambda e, j=j, bk=bk: e.activation(
                bufs["G_h"][:, 256 * j:256 * j + 256], psum[bk][:, 0:256], AF.Gelu,
                bias=cc[:, C_ZERO:C_ZERO + 1], scale=cc[:, C_ONE:C_ONE + 1]),
                waits=wg, inc="sC", drain=True)
        phase1_done[nb] = ctx.cnt["sT"]
        h_all = ctx.cnt["sC"]
        for t in range(2):
            i = 2 * nb + t
            pb_pair = (2, 3) if t == 0 else (4, 5)
            wl = [("sV", gate_done[b])] if gate_done[b] else []
            x1_tok[b] = ctx.op("sync", lambda e, i=i, b=b: e.dma_start(
                out=bufs["G_x1"][b], in_=io["x1"][128 * i:128 * i + 128, :]),
                waits=wl, inc=f"d{8 + b}", inc_n=16)
            s1_tok[b] = ctx.op("sync", lambda e, i=i, b=b: e.dma_start(
                out=bufs["G_s1"][b], in_=io["s1"][128 * i:128 * i + 128, :]),
                inc=f"d{13 + b}", inc_n=16)
            for dh in range(2):
                pk = pb_pair[dh]
                for j in range(32):
                    wt = []
                    if j == 0:
                        wt.append(("sC", h_all))
                        if (i - 2, dh) in y_ev:
                            wt.append(("sC", y_ev[(i - 2, dh)]))
                    ctx.op("tensor", lambda e, j=j, t=t, dh=dh, pk=pk: e.matmul(
                        psum[pk][:, :],
                        bufs["G_h"][:, 256 * j + 128 * t:256 * j + 128 * t + 128],
                        bufs["G_w2"][:, 1024 * j + 512 * dh:1024 * j + 512 * dh + 512],
                        start=(j == 0), stop=False), waits=wt)
                ctx.op("tensor", lambda e, dh=dh, pk=pk: e.matmul(
                    psum[pk][:, :], bufs["ones_sb"][0:1, 0:128],
                    bufs["G_b2"][0:1, 512 * dh:512 * dh + 512], start=False, stop=True), inc="sT")
                ymm = ctx.cnt["sT"]
                wsl = [("sT", ymm)]
                if dh == 0 and (gate_done[0] or gate_done[1]):
                    wsl.append(("sV", max(gate_done)))
                y_ev[(i, dh)] = ctx.op("scalar", lambda e, dh=dh, pk=pk: e.activation(
                    bufs["G_g"][:, 512 * dh:512 * dh + 512], psum[pk][:, :], AF.Silu,
                    bias=cc[:, C_ZERO:C_ZERO + 1], scale=cc[:, C_ONE:C_ONE + 1]),
                    waits=wsl, inc="sC", drain=True)
            wv = [("sC", y_ev[(i, 1)]), (f"d{8 + b}", x1_tok[b]), (f"d{13 + b}", s1_tok[b])]
            if st_tok[b] > 0:
                wv.append((f"d{2 + b}", st_tok[b]))
            ctx.op("vector", lambda e, b=b: e.tensor_tensor(
                bufs["G_scr"], bufs["G_g"], bufs["G_s1"][b], ALU.mult), waits=wv, drain=True)
            gate_done[b] = ctx.op("vector", lambda e, b=b: e.tensor_tensor(
                bufs["G_o"][b], bufs["G_scr"], bufs["G_x1"][b], ALU.add), inc="sV", drain=True)
            st_tok[b] = ctx.op("sync", lambda e, i=i, b=b: e.dma_start(
                out=io["out"][128 * i:128 * i + 128, :], in_=bufs["G_o"][b]),
                waits=[("sV", gate_done[b])], inc=f"d{2 + b}", inc_n=16)

STAGES = {
    "A": (alloc_stageA, emit_stageA),
    "B": (alloc_stageB, emit_stageB),
    "C": (alloc_stageC, emit_stageC),
    "D": (alloc_stageD, emit_stageD),
    "E": (alloc_stageE, emit_stageE),
    "F": (alloc_stageF, emit_stageF),
    "G": (alloc_stageG, emit_stageG),
}


def build_group(stages, keep_outputs=None):
    table = dram_table()
    reads, writes, rws = set(), set(), set()
    for s in stages:
        r, w, rw = STAGE_IO[s]
        for name in r:
            if name not in writes and name not in rws:
                reads.add(name)
        for name in rw:
            if name not in writes and name not in rws:
                rws.add(name)
            writes.add(name)
        writes.update(w)
    if keep_outputs is None:
        keep_outputs = set(writes) | set(rws)
    nc = bass.Bass("TRN2", target_bir_lowering=False)
    io = {}
    in_names, out_names = [], []
    seed_pairs = []
    for name in sorted(rws):
        sh, dt = table[name]
        io[name + "__seed"] = nc.dram_tensor(name + "__seed", sh, dt, kind="ExternalInput").ap()
        in_names.append(name + "__seed")
        kind = "ExternalOutput" if name in keep_outputs else "Internal"
        io[name] = nc.dram_tensor(name, sh, dt, kind=kind).ap()
        if kind == "ExternalOutput":
            out_names.append(name)
        seed_pairs.append(name)
    for name in sorted(reads - rws):
        sh, dt = table[name]
        io[name] = nc.dram_tensor(name, sh, dt, kind="ExternalInput").ap()
        in_names.append(name)
    for name in sorted(writes - rws):
        sh, dt = table[name]
        kind = "ExternalOutput" if name in keep_outputs else "Internal"
        io[name] = nc.dram_tensor(name, sh, dt, kind=kind).ap()
        if kind == "ExternalOutput":
            out_names.append(name)

    ar = Arena(nc)
    bufs = {}
    bufs["cconst"] = ar.alloc(NCONST, F32)
    bufs["ident_sb"] = ar.alloc(128, BF16)
    bufs["ident_f"] = ar.alloc(128, F32)
    bufs["ones_sb"] = ar.alloc(512, BF16, parts=1)
    bufs["tri_lo_sb"] = ar.alloc(128, BF16)
    bufs["tri_up_sb"] = ar.alloc(128, BF16)
    bufs["tri_lo4_sb"] = ar.alloc(512, BF16)
    bufs["tri_up4_sb"] = ar.alloc(512, BF16)
    psum_cms = [nc.psum_tensor(f"pb{i}", [128, 512], F32) for i in range(8)]
    psums = [cm.__enter__() for cm in psum_cms]
    bufs["psum"] = [p[:, :] for p in psums]
    stage_base = ar.off
    hi = stage_base
    for s in stages:
        ar.off = stage_base
        STAGES[s][0](nc, ar, bufs)
        hi = max(hi, ar.off)
    ar.off = hi

    sems = {}
    sem_cms = {}
    for sn in SEM_NAMES:
        cm = nc.semaphore(sn)
        sem_cms[sn] = cm
        sems[sn] = cm.__enter__()

    with nc.Block() as blk:
        def body(eng_name):
            def run(eng):
                ctx = Ctx(sems, eng_name, eng)
                ctx.op("gpsimd", lambda e: e.dma_start(out=bufs["cconst"], in_=io["cconst"]), inc="gld", inc_n=16)
                if "ident" in io:
                    ctx.op("gpsimd", lambda e: e.dma_start(out=bufs["ident_sb"], in_=io["ident"]), inc="gld", inc_n=16)
                    ctx.op("sync", lambda e: e.dma_start(out=bufs["ident_f"], in_=io["ident"]), inc="qld", inc_n=16)
                if "ones_row" in io:
                    ctx.op("gpsimd", lambda e: e.dma_start(out=bufs["ones_sb"], in_=io["ones_row"]), inc="gld", inc_n=16)
                if "tri_lo" in io:
                    ctx.op("gpsimd", lambda e: e.dma_start(out=bufs["tri_lo_sb"], in_=io["tri_lo"]), inc="gld", inc_n=16)
                if "tri_up" in io:
                    ctx.op("gpsimd", lambda e: e.dma_start(out=bufs["tri_up_sb"], in_=io["tri_up"]), inc="gld", inc_n=16)
                if "tri_lo4" in io:
                    ctx.op("gpsimd", lambda e: e.dma_start(out=bufs["tri_lo4_sb"], in_=io["tri_lo4"]), inc="gld", inc_n=16)
                if "tri_up4" in io:
                    ctx.op("gpsimd", lambda e: e.dma_start(out=bufs["tri_up4_sb"], in_=io["tri_up4"]), inc="gld", inc_n=16)
                for name in seed_pairs:
                    ctx.op("sync", lambda e, name=name: e.dma_start(
                        out=io[name], in_=io[name + "__seed"]), inc="qld", inc_n=16)
                ctx.wait("tensor", ["gld", "qld"])
                ctx.wait("vector", ["gld", "qld"])
                ctx.wait("scalar", ["gld", "qld"])
                ctx.wait("sync", ["gld", "qld"])
                ctx.wait("gpsimd", ["gld", "qld"])
                for s in stages:
                    STAGES[s][1](ctx, bufs, io)
                    ctx.barrier()
                all_dma = ["qst", "gld", "qld"] + [f"d{i}" for i in range(20)]
                ctx.wait("sync", all_dma)
                ctx.barrier()
            return run
        blk.tensor(body("tensor"))
        blk.vector(body("vector"))
        blk.scalar(body("scalar"))
        blk.gpsimd(body("gpsimd"))
        blk.sync(body("sync"))

    return nc, in_names, out_names


GROUPS = [["A", "B", "C", "D", "E", "F", "G"]]


def kernel(**inputs):
    consts = build_consts()
    rename = {"wq1": "L1_attn_wq", "wk1": "L1_attn_wk", "wv1": "L1_attn_wv",
              "wo1": "L1_attn_wo", "bo1": "L1_attn_bo", "omega": "L1_attn_omega",
              "W1q": "L1_ffn_W1", "b1q": "L1_ffn_b1", "W2q": "L1_ffn_W2", "b2q": "L1_ffn_b2"}
    pool = {}
    for k, v in inputs.items():
        pool[k] = np.ascontiguousarray(v)
    for k in consts:
        pool[k] = consts[k]

    def fetch(name, core):
        base = name[:-6] if name.endswith("__seed") else name
        if base == "x":
            return np.ascontiguousarray(pool["L0_attn_x"][core])
        if base in consts:
            return consts[base]
        if base in rename:
            return pool[rename[base]]
        if base in ("Wq", "bq", "Wk", "bk", "Wvd", "bvd", "Wvu", "bvu"):
            return pool["L0_attn_" + base]
        if base in ("Wqkv", "q_scale", "k_scale", "Wo", "ln_g", "ln_b", "W1", "b1", "W2", "b2"):
            return pool["L0_ffn_" + base]
        return pool["__scratch__" + base][core]

    n_cores = 8
    for gi, stages in enumerate(GROUPS):
        needed = {"out"}
        for later in GROUPS[gi + 1:]:
            for s in later:
                r, w, rw = STAGE_IO[s]
                needed.update(r)
                needed.update(rw)
        nc, in_names, out_names = build_group(stages, needed)
        in_maps = []
        for core in range(n_cores):
            m = {name: fetch(name, core) for name in in_names}
            in_maps.append(m)
        res = run_bass_kernel_spmd(nc, in_maps, core_ids=list(range(n_cores)))
        for name in out_names:
            pool["__scratch__" + name] = np.stack([res.results[c][name] for c in range(n_cores)])

    return pool["__scratch__out"]


# version 10
# speedup vs baseline: 1.0180x; 1.0180x over previous
import math
import numpy as np
import ml_dtypes
import concourse.bass as bass
import concourse.mybir as mybir
from concourse.bass_utils import run_bass_kernel_spmd

F32 = mybir.dt.float32
BF16 = mybir.dt.bfloat16
AF = mybir.ActivationFunctionType
ALU = mybir.AluOpType
AX = mybir.AxisListType

N = 4096
D = 1024
NT = N // 128
NCH = D // 128

ENGS = ("tensor", "vector", "scalar", "gpsimd", "sync")

C_ZERO, C_ONE, C_NEGHALF = 0, 1, 2
NCONST = 4


def build_consts():
    c = {}
    c["ident"] = np.eye(128, dtype=np.float32)
    c["ones_row"] = np.ones((1, 512), np.float32)
    cc = np.zeros((128, NCONST), np.float32)
    cc[:, C_ONE] = 1.0
    cc[:, C_NEGHALF] = -0.5
    c["cconst"] = cc
    p = np.arange(128)[:, None]
    f = np.arange(128)[None, :]
    c["tri_lo"] = (p >= f).astype(np.float32)
    c["tri_up"] = (p <= f).astype(np.float32)
    c["tri_lo4"] = np.tile((p >= f), (1, 4)).astype(np.float32)
    c["tri_up4"] = np.tile((p <= f), (1, 4)).astype(np.float32)
    w17 = np.zeros((1, 1040), np.float32)
    w17[0, 64::65] = 1.0
    c["wvu17row"] = w17
    f2 = np.arange(512)[None, :]
    for k, off in enumerate((0, 128, 256, 384)):
        c[f"amask{k}"] = ((p + off) >= f2).astype(np.float32)
    DH, WIN = 64, 256
    half = np.arange(0, DH, 2, dtype=np.float64)
    inv_freq = 1.0 / 10000.0 ** (half / DH)
    t = np.arange(N, dtype=np.float64)
    freqs = t[:, None] * inv_freq[None, :]
    freqs = np.concatenate([freqs, freqs], -1)
    base_scale = (half + 0.4 * DH) / (1.4 * DH)
    power = (t - N // 2) / (WIN // 2)
    scale = base_scale[None, :] ** power[:, None]
    scale = np.concatenate([scale, scale], -1)
    cos, sin = np.cos(freqs), np.sin(freqs)
    sgn = np.concatenate([-np.ones(32), np.ones(32)])[None, :]
    c["rot_all"] = np.concatenate([
        cos * scale * 8.0, sin * scale * 8.0 * sgn,
        cos / scale, sin / scale * sgn], axis=1).astype(np.float32)
    return c


class Arena:

    def __init__(self, nc, nbytes=205 * 1024):
        self.nbytes = nbytes
        self._cm = nc.sbuf_tensor("arena", [128, nbytes], mybir.dt.uint8)
        self.t = self._cm.__enter__()
        self.off = 0
        self.marks = []

    def alloc(self, cols, dtype, parts=128):
        esz = 2 if dtype == BF16 else 4
        nb = cols * esz
        off = (self.off + 63) & ~63
        assert off + nb <= self.nbytes, f"arena overflow at {off}+{nb}"
        self.off = off + nb
        ap = self.t[:, off:off + nb].bitcast(dtype)
        if parts != 128:
            ap = ap[0:parts, :]
        return ap

    def allocN(self, n, cols, dtype, parts=128):
        return [self.alloc(cols, dtype, parts) for _ in range(n)]

    def push(self):
        self.marks.append(self.off)

    def pop(self):
        self.off = self.marks.pop()


class Ctx:
    def __init__(self, sems, eng_name, eng):
        self.sems = sems
        self.cnt = {k: 0 for k in sems}
        self.eng_name = eng_name
        self.eng = eng
        self._waited = {k: 0 for k in sems}

    def op(self, engine, fn, waits=(), inc=None, inc_n=1, drain=False):
        resolved = []
        for w in waits:
            if isinstance(w, str):
                resolved.append((w, self.cnt[w]))
            else:
                resolved.append(w)
        if self.eng_name == engine:
            for s, v in resolved:
                if v > 0 and self._waited[s] < v:
                    self.eng.wait_ge(self.sems[s], v)
                    self._waited[s] = v
            ins = fn(self.eng)
            if inc is not None:
                ins.then_inc(self.sems[inc], inc_n)
            if drain:
                self.eng.drain()
        if inc is not None:
            self.cnt[inc] += inc_n
            return self.cnt[inc]
        return None

    def wait(self, engine, waits):
        resolved = [(w, self.cnt[w]) if isinstance(w, str) else w for w in waits]
        if self.eng_name == engine:
            for s, v in resolved:
                if v > 0 and self._waited[s] < v:
                    self.eng.wait_ge(self.sems[s], v)
                    self._waited[s] = v

    def barrier(self):
        dma_sems = [k for k in self.sems if k in ("gld", "qld", "qst") or k.startswith("d")]
        if self.eng_name is not None:
            for s in dma_sems:
                v = self.cnt[s]
                if v > 0 and self._waited[s] < v:
                    self.eng.wait_ge(self.sems[s], v)
                    self._waited[s] = v
        self.cnt["bar"] += 5
        target = self.cnt["bar"]
        if self.eng_name is not None:
            self.eng.sem_inc(self.sems["bar"], 1)
            self.eng.wait_ge(self.sems["bar"], target)
            self._waited["bar"] = target


SEM_NAMES = ("gld", "qld", "qst", "sT", "sT2", "sV", "sV2", "sC", "sC2", "sG", "bar") + tuple(f"d{i}" for i in range(20))

def dram_table():
    return {
        "x": ([N, D], F32),
        "Wq": ([16, 1024, 64], F32), "bq": ([16, 64], F32),
        "Wk": ([16, 1024, 64], F32), "bk": ([16, 64], F32),
        "Wvd": ([16, 1024, 16], F32), "bvd": ([16, 16], F32),
        "Wvu": ([16, 16, 64], F32), "bvu": ([16, 64], F32),
        "Wqkv": ([6, 1024, 3072], F32), "q_scale": ([6, 64], F32), "k_scale": ([6, 64], F32),
        "Wo": ([6, 1024, 1024], F32), "ln_g": ([6, 1024], F32), "ln_b": ([6, 1024], F32),
        "W1": ([6, 1024, 5460], F32), "b1": ([6, 5460], F32),
        "W2": ([6, 2730, 1024], F32), "b2": ([6, 1024], F32),
        "omega": ([64, 64], F32),
        "wq1": ([1024, 1024], F32), "wk1": ([1024, 1024], F32),
        "wv1": ([1024, 1024], F32), "wo1": ([1024, 1024], F32), "bo1": ([1024], F32),
        "W1q": ([4096, 1024], F32), "b1q": ([4096], F32),
        "W2q": ([1024, 4096], F32), "b2q": ([1024], F32),
        "ident": ([128, 128], F32), "ones_row": ([1, 512], F32), "cconst": ([128, NCONST], F32),
        "tri_lo": ([128, 128], F32), "tri_up": ([128, 128], F32),
        "tri_lo4": ([128, 512], F32), "tri_up4": ([128, 512], F32),
        "wvu17row": ([1, 1040], F32),
        "amask0": ([128, 512], F32), "amask1": ([128, 512], F32),
        "amask2": ([128, 512], F32), "amask3": ([128, 512], F32),
        "rot_all": ([N, 256], F32),
        "r0": ([N, D], F32), "xnT": ([D, N], BF16),
        "s0": ([N, D], F32),
        "x1": ([N, D], F32), "xn1T": ([D, N], BF16),
        "s1": ([N, D], F32),
        "w1tT": ([D, 4096], BF16), "w2tT": ([4096, D], BF16),
        "escr1": ([1, 1], F32), "escr2": ([1, 1], F32),
        "out": ([N, D], F32),
    }


STAGE_IO = {
    "A": (["x", "cconst", "ident"], ["r0", "xnT"], []),
    "B": (["xnT", "Wq", "bq", "Wk", "bk", "Wvd", "bvd", "Wvu", "bvu", "ident", "wvu17row",
           "cconst", "ones_row", "amask0", "amask1", "amask2", "amask3"], ["s0"], []),
    "C": (["Wqkv", "q_scale", "k_scale", "Wo", "ln_g", "ln_b", "W1", "b1", "W2", "b2",
           "cconst", "ones_row", "ident", "tri_lo", "tri_up", "tri_lo4", "tri_up4",
           "rot_all"], [], ["r0"]),
    "D": (["x", "s0", "r0", "cconst", "ident"], ["x1", "xn1T"], []),
    "E": (["W1q", "W2q", "cconst", "ident", "ones_row"], ["w1tT", "w2tT", "escr1", "escr2"], []),
    "F": (["xn1T", "omega", "wq1", "wk1", "wv1", "wo1", "bo1",
           "cconst", "ones_row", "ident", "tri_up"], ["s1"], []),
    "G": (["xn1T", "w1tT", "w2tT", "b1q", "b2q", "x1", "s1", "cconst", "ones_row"], ["out"], []),
}

CONST_NAMES = {"ident", "ones_row", "cconst", "tri_lo", "tri_up", "tri_lo4", "tri_up4",
               "amask0", "amask1", "amask2", "amask3", "rot_all", "wvu17row"}


def alloc_stageA(nc, ar, bufs):
    bufs["A_xa"] = ar.allocN(2, 1024, F32)
    bufs["A_xn"] = ar.allocN(2, 1024, F32)
    bufs["A_xnb"] = ar.allocN(2, 1024, BF16)
    bufs["A_xtv"] = ar.allocN(2, 1024, BF16)
    bufs["A_scr"] = ar.allocN(1, 1024, F32)
    bufs["A_st"] = ar.alloc(8, F32)


def emit_stageA(ctx, bufs, io):
    x, r0, xnT = io["x"], io["r0"], io["xnT"]
    xa, xn, xnb, xtv = bufs["A_xa"], bufs["A_xn"], bufs["A_xnb"], bufs["A_xtv"]
    st = bufs["A_st"]
    scr = bufs["A_scr"][0]
    cconst, psum = bufs["cconst"], bufs["psum"]
    xnT_r = xnT.rearrange("(c p) n -> p c n", p=128)

    ld = [0] * NT; vxn = [0] * NT; ccast = [0] * NT; ttr = [0] * NT
    vev = [0] * NT; st1 = [0] * NT; st2 = [0] * NT

    for i in range(NT):
        b = i % 2
        w = [("sV", vxn[i - 2])] if i >= 2 else []
        ld[i] = ctx.op("sync", lambda e, i=i, b=b: e.dma_start(
            out=xa[b], in_=x[i * 128:(i + 1) * 128, :]), waits=w, inc=f"d{b}", inc_n=16)

        ctx.op("vector", lambda e, b=b: e.tensor_scalar(
            scr, xa[b], 1.0, 0.0, ALU.mult, ALU.add, accum_out=st[:, 0:1]),
            waits=[(f"d{b}", ld[i])], drain=True)
        ctx.op("vector", lambda e, b=b: e.scalar_tensor_tensor(
            scr, xa[b], 1.0, xa[b], ALU.mult, ALU.mult, accum_out=st[:, 1:2]), drain=True)
        ctx.op("vector", lambda e: e.tensor_scalar(st[:, 2:3], st[:, 0:1], 1.0 / D, None, ALU.mult), drain=True)
        ctx.op("vector", lambda e: e.tensor_tensor(st[:, 3:4], st[:, 2:3], st[:, 2:3], ALU.mult), drain=True)
        ctx.op("vector", lambda e: e.tensor_scalar(st[:, 4:5], st[:, 1:2], 1.0 / D, None, ALU.mult), drain=True)
        ctx.op("vector", lambda e: e.tensor_tensor(st[:, 4:5], st[:, 4:5], st[:, 3:4], ALU.subtract), drain=True)
        ctx.op("vector", lambda e: e.tensor_scalar(st[:, 4:5], st[:, 4:5], 1e-5, None, ALU.add),
               inc="sV2", drain=True)
        ctx.op("scalar", lambda e: e.activation(st[:, 5:6], st[:, 4:5], AF.Ln,
               bias=cconst[:, C_ZERO:C_ZERO + 1], scale=cconst[:, C_ONE:C_ONE + 1]),
               waits=["sV2"], drain=True)
        ctx.op("scalar", lambda e: e.activation(st[:, 6:7], st[:, 5:6], AF.Exp,
               bias=cconst[:, C_ZERO:C_ZERO + 1], scale=cconst[:, C_NEGHALF:C_NEGHALF + 1]),
               inc="sC", drain=True)
        wv = [("sC", ctx.cnt["sC"])]
        if i >= 2:
            wv.append((f"d{2 + b}", st1[i - 2]))
        vxn[i] = ctx.op("vector", lambda e, b=b: e.tensor_scalar(
            xn[b], xa[b], st[:, 2:3], st[:, 6:7], ALU.subtract, ALU.mult),
            waits=wv, inc="sV", drain=True)
        wc = [("sV", vxn[i])]
        if i >= 2:
            wc.append(("sT", ttr[i - 2]))
        ccast[i] = ctx.op("scalar", lambda e, b=b: e.copy(xnb[b], xn[b]),
                          waits=wc, inc="sC2")
        st1[i] = ctx.op("sync", lambda e, i=i, b=b: e.dma_start(
            out=r0[i * 128:(i + 1) * 128, :], in_=xn[b]), waits=[("sV", vxn[i])],
            inc=f"d{2 + b}", inc_n=16)

        pb = psum[4 + b][:, :].bitcast(BF16)
        for c in range(NCH):
            wt = [("sC2", ccast[i])]
            if i >= 2 and c == 0:
                wt.append(("sV", vev[i - 2]))
            ctx.op("tensor", lambda e, b=b, c=c, pb=pb: e.transpose(
                pb[:, c * 128:(c + 1) * 128], xnb[b][:, c * 128:(c + 1) * 128], bufs["ident_sb"]),
                waits=wt, inc=("sT" if c == NCH - 1 else None))
        ttr[i] = ctx.cnt["sT"]
        we = [("sT", ttr[i])]
        if i >= 2:
            we.append((f"d{15 + b}", st2[i - 2]))
        vev[i] = ctx.op("vector", lambda e, b=b, pb=pb: e.tensor_copy(xtv[b], pb[:, 0:1024]),
                        waits=we, inc="sV", drain=True)
        st2[i] = ctx.op("sync", lambda e, i=i, b=b: e.dma_start(
            out=xnT_r[:, :, i * 128:(i + 1) * 128],
            in_=xtv[b].rearrange("p (c n) -> p c n", c=8)),
            waits=[("sV", vev[i])], inc=f"d{15 + b}", inc_n=16)


def alloc_stageB(nc, ar, bufs):
    bufs["B_xnT"] = ar.alloc(8 * 4096, BF16)
    bufs["B_vdw"] = ar.alloc(8 * 256, BF16)
    bufs["B_vdo"] = ar.alloc(32 * 272, BF16)
    bufs["B_bvdb"] = ar.alloc(256, F32)
    bufs["B_bvub"] = ar.alloc(1024, F32)
    bufs["B_bqki"] = ar.alloc(128, F32)
    bufs["B_bqk"] = ar.alloc(16, F32)
    bufs["B_qkw"] = ar.allocN(2, 2 * 8 * 128, BF16)
    bufs["B_qT"] = ar.allocN(2, 4096, BF16)
    bufs["B_kT"] = ar.allocN(2, 4096, BF16)
    bufs["B_pt"] = ar.allocN(4, 512, BF16)
    bufs["B_am"] = [ar.alloc(512, BF16) for _ in range(4)]
    bufs["B_avs"] = ar.allocN(4, 512, BF16)
    bufs["B_wvu"] = ar.alloc(1040, BF16)
    bufs["B_sst"] = ar.allocN(2, 512, F32)
    bufs["B_rcp"] = ar.alloc(8, F32)


def emit_stageB(ctx, bufs, io):
    xnT = bufs["B_xnT"]
    psum = bufs["psum"]
    cc = bufs["cconst"]
    identb = bufs["ident_sb"]

    xnT_r = io["xnT"].rearrange("(c p) n -> p c n", p=128)
    ctx.op("gpsimd", lambda e: e.dma_start(
        out=xnT.rearrange("p (c n) -> p c n", c=8), in_=xnT_r), inc="gld", inc_n=16)
    for h in range(16):
        ctx.op("gpsimd", lambda e, h=h: e.dma_start(
            out=bufs["B_vdw"].rearrange("p (c x) -> p c x", c=8)[:, :, 16 * h:16 * h + 16],
            in_=io["Wvd"][h].rearrange("(c p) r -> p c r", p=128)), inc="gld", inc_n=16)
    for k in range(4):
        ctx.op("gpsimd", lambda e, k=k: e.dma_start(
            out=bufs["B_am"][k], in_=io["amask" + str(k)]), inc="gld", inc_n=16)
    wvu_z = ctx.op("vector", lambda e: e.memset(bufs["B_wvu"][0:32, :], 0.0), inc="sV", drain=True)
    for h in range(16):
        ctx.op("gpsimd", lambda e, h=h: e.dma_start(
            out=bufs["B_wvu"][0:16, 65 * h:65 * h + 64], in_=io["Wvu"][h]),
            waits=([("sV", wvu_z)] if h == 0 else []), inc="gld", inc_n=16)
    ctx.op("gpsimd", lambda e: e.dma_start(
        out=bufs["B_wvu"][16:17, :], in_=io["wvu17row"]), inc="gld", inc_n=16)
    ld_pre = ctx.cnt["gld"]
    wvu_ones = 0
    ctx.op("sync", lambda e: e.dma_start(
        out=bufs["B_bvdb"], in_=io["bvd"].rearrange("h r -> (h r)").unsqueeze(0).partition_broadcast(128).squeeze(1)),
        inc="qld", inc_n=16)
    ctx.op("sync", lambda e: e.dma_start(
        out=bufs["B_bvub"], in_=io["bvu"].rearrange("h d -> (h d)").unsqueeze(0).partition_broadcast(128).squeeze(1)),
        inc="qld", inc_n=16)
    ctx.op("sync", lambda e: e.dma_start(out=bufs["B_bqki"][0:8, :],
        in_=io["bq"].rearrange("(j g) d -> j (g d)", g=2)), inc="qld", inc_n=16)
    ld_bqk = ctx.op("sync", lambda e: e.dma_start(out=bufs["B_bqki"][8:16, :],
        in_=io["bk"].rearrange("(j g) d -> j (g d)", g=2)), inc="qld", inc_n=16)
    ld_sync_all = ctx.cnt["qld"]

    ctx.op("tensor", lambda e: e.transpose(
        psum[2][:, 0:16], bufs["B_bqki"][0:16, :], bufs["ident_f"][0:16, 0:16]),
        waits=[("qld", ld_bqk)], inc="sT")
    _t = ctx.cnt["sT"]
    ctx.op("vector", lambda e: e.tensor_copy(bufs["B_bqk"], psum[2][:, 0:16]),
           waits=[("sT", _t)], inc="sV", drain=True)
    bqk_ready = ctx.cnt["sV"]

    vd_ev = [0] * NT
    for t in range(NT):
        bk = t % 2
        for c in range(NCH):
            w = [("gld", ld_pre)]
            if t >= 2 and c == 0:
                w.append(("sV", vd_ev[t - 2]))
            ctx.op("tensor", lambda e, t=t, c=c, bk=bk: e.matmul(
                psum[bk][:, 0:256], xnT[:, 4096 * c + 128 * t: 4096 * c + 128 * t + 128],
                bufs["B_vdw"][:, 256 * c:256 * c + 256],
                start=(c == 0), stop=(c == NCH - 1)), waits=w,
                inc=("sT" if c == NCH - 1 else None))
        mm_tok = ctx.cnt["sT"]
        ctx.op("vector", lambda e, t=t: e.memset(
            bufs["B_vdo"][:, 272 * t:272 * t + 272].rearrange("p (h x) -> p h x", h=16)[:, :, 16:17], 1.0),
            waits=[("sT", mm_tok), ("qld", ld_sync_all)], drain=True)
        vd_ev[t] = ctx.op("vector", lambda e, t=t, bk=bk: e.scalar_tensor_tensor(
            bufs["B_vdo"][:, 272 * t:272 * t + 272].rearrange("p (h x) -> p h x", h=16)[:, :, 0:16],
            psum[bk][:, 0:256].rearrange("p (h r) -> p h r", h=16),
            1.0, bufs["B_bvdb"].rearrange("p (h r) -> p h r", h=16),
            ALU.mult, ALU.add), inc="sV", drain=True)
    vd_done = ctx.cnt["sV"]

    qkw_tok = [0, 0]
    attn_done = [0, 0]
    proj_mm_done = [0, 0]

    def load_qkw(j):
        b = j % 2
        w = [("sT", proj_mm_done[b])] if j >= 2 else []
        for hh in range(2):
            h = 2 * j + hh
            ctx.op("gpsimd", lambda e, h=h, hh=hh, b=b: e.dma_start(
                out=bufs["B_qkw"][b].rearrange("p (g c x) -> p g c x", g=2, c=8)[:, 0, :, 64 * hh:64 * hh + 64],
                in_=io["Wq"][h].rearrange("(c p) d -> p c d", p=128)),
                waits=(w if hh == 0 else []), inc=f"d{10 + b}", inc_n=16)
            qkw_tok[b] = ctx.op("gpsimd", lambda e, h=h, hh=hh, b=b: e.dma_start(
                out=bufs["B_qkw"][b].rearrange("p (g c x) -> p g c x", g=2, c=8)[:, 1, :, 64 * hh:64 * hh + 64],
                in_=io["Wk"][h].rearrange("(c p) d -> p c d", p=128)), inc=f"d{10 + b}", inc_n=16)

    load_qkw(0)
    exp_tok = {}
    pv_tok = {}
    norm_tok = {}
    proj_ev = {}
    sst_st = [0, 0]
    out_ev = [0, 0]
    attvd_wr_guard = 0
    avt_ev_done = 0
    bank_exp = {0: 0, 1: 0, 2: 0, 3: 0}
    pt_guard = {0: 0, 1: 0, 2: 0, 3: 0}
    avs_ev = {}
    out_norm = {}

    for j in range(8):
        b = j % 2
        if j < 7:
            load_qkw(j + 1)
        for qk in range(2):
            for s in range(8):
                pbank = 6 + (s % 2)
                for c in range(NCH):
                    w = []
                    if c == 0:
                        w.append((f"d{10 + b}", qkw_tok[b]))
                        if s >= 2:
                            w.append(("sV", proj_ev[(j, qk, s - 2)]))
                        elif qk == 1:
                            w.append(("sV", proj_ev[(j, 0, 6 + s)]))
                        elif j > 0:
                            w.append(("sV", proj_ev[(j - 1, 1, 6 + s)]))
                        if j > 0 and qk == 0 and s < 2 and out_norm.get((s, 7), 0) >= 0 and out_norm.get((s % 2, 7), 0) > 0:
                            w.append(("sV", out_norm[(s % 2, 7)]))
                        if s == 0 and qk == 0 and j >= 2:
                            w.append(("sT", attn_done[b]))
                    ctx.op("tensor", lambda e, qk=qk, s=s, c=c, b=b, pbank=pbank: e.matmul(
                        psum[pbank][:, :],
                        bufs["B_qkw"][b][:, 1024 * qk + 128 * c:1024 * qk + 128 * c + 128],
                        xnT[:, 4096 * c + 512 * s:4096 * c + 512 * s + 512],
                        start=(c == 0), stop=(c == NCH - 1)), waits=w,
                        inc=("sT" if c == NCH - 1 else None))
                mm_tok = ctx.cnt["sT"]
                if qk == 1 and s == 7:
                    proj_mm_done[b] = mm_tok
                wv = [("sT", mm_tok), ("sV", bqk_ready)]
                if j >= 2 and qk == 0 and s == 0:
                    wv.append(("sT", attn_done[b]))
                if qk == 0:
                    proj_ev[(j, 0, s)] = ctx.op("vector", lambda e, s=s, b=b, j=j, pbank=pbank: e.tensor_scalar(
                        bufs["B_qT"][b][:, 512 * s:512 * s + 512],
                        psum[pbank][:, :], bufs["B_bqk"][:, j:j + 1], 1.0 / 32,
                        ALU.add, ALU.mult), waits=wv, inc="sV", drain=True)
                else:
                    proj_ev[(j, 1, s)] = ctx.op("vector", lambda e, s=s, b=b, j=j, pbank=pbank: e.tensor_scalar(
                        bufs["B_kT"][b][:, 512 * s:512 * s + 512],
                        psum[pbank][:, :], bufs["B_bqk"][:, 8 + j:8 + j + 1], None,
                        ALU.add), waits=wv, inc="sV", drain=True)
        proj_all_ev = ctx.cnt["sV"]

        for s in range(8):
            c0 = 4 * s
            nchunks = 32 - c0
            for ci in range(nchunks):
                c = c0 + ci
                for hh in range(2):
                    base = 64 * hh
                    h = 2 * j + hh
                    obank = 4 + hh
                    sbank = (2 * ci + hh) % 4
                    ptb = sbank
                    w = [("sV", proj_all_ev)]
                    if bank_exp[sbank] > 0:
                        w.append(("sC", bank_exp[sbank]))
                    if ci == 0:
                        if s >= 1:
                            w.append(("sV", avs_ev[(hh, s - 1)]))
                        elif j > 0:
                            w.append(("sV", avs_ev[(hh, 7)]))
                        if j == 0 and s == 0:
                            w.append(("sV", vd_done))
                    ctx.op("tensor", lambda e, s=s, c=c, b=b, base=base, sbank=sbank: e.matmul(
                        psum[sbank][:, :],
                        bufs["B_kT"][b][base:base + 64, 128 * c:128 * c + 128],
                        bufs["B_qT"][b][base:base + 64, 512 * s:512 * s + 512],
                        start=True, stop=True), waits=w, inc="sT")
                    st_tok_mm = ctx.cnt["sT"]
                    we = [("sT", st_tok_mm)]
                    if pt_guard[ptb]:
                        we.append(("sT", pt_guard[ptb]))
                    exp_tok[(hh, s, ci)] = ctx.op("scalar", lambda e, ptb=ptb, sbank=sbank: e.activation(
                        bufs["B_pt"][ptb], psum[sbank][:, :], AF.Exp,
                        bias=cc[:, C_ZERO:C_ZERO + 1], scale=cc[:, C_ONE:C_ONE + 1]),
                        waits=we, inc="sC")
                    bank_exp[sbank] = exp_tok[(hh, s, ci)]
                    pvw, pvt = "sC", exp_tok[(hh, s, ci)]
                    if ci < 4:
                        ctx.op("vector", lambda e, ptb=ptb, ci=ci: e.tensor_tensor(
                            bufs["B_pt"][ptb], bufs["B_pt"][ptb], bufs["B_am"][ci], ALU.mult),
                            waits=[("sC", exp_tok[(hh, s, ci)])], inc="sV", drain=True)
                        pvw, pvt = "sV", ctx.cnt["sV"]
                    ctx.op("tensor", lambda e, c=c, h=h, ptb=ptb, obank=obank, ci=ci, nch=nchunks: e.matmul(
                        psum[obank][0:17, :],
                        bufs["B_vdo"][:, 272 * c + 17 * h:272 * c + 17 * h + 17],
                        bufs["B_pt"][ptb],
                        start=(ci == 0), stop=(ci == nch - 1)), waits=[(pvw, pvt)], inc="sT")
                    pt_guard[ptb] = ctx.cnt["sT"]
            for hh in range(2):
                base = 64 * hh
                h = 2 * j + hh
                obank = 4 + hh
                avb = 2 * (s % 2) + hh
                wv = [("sT", pt_guard[(2 * (nchunks - 1) + hh) % 4])]
                avs_ev[(hh, s)] = ctx.op("vector", lambda e, avb=avb, obank=obank: e.tensor_copy(
                    bufs["B_avs"][avb][0:17, :], psum[obank][0:17, :]), waits=wv, inc="sV", drain=True)
                ob = 6 + hh
                for t in range(4):
                    w2 = []
                    if t == 0:
                        w2.append(("sV", avs_ev[(hh, s)]))
                        key = (hh, s - 1) if s >= 1 else (hh, 7)
                        if key in out_norm and out_norm[key] > 0:
                            w2.append(("sV", out_norm[key]))
                    ctx.op("tensor", lambda e, t=t, h=h, avb=avb, ob=ob: e.matmul(
                        psum[ob][:, 128 * t:128 * t + 65],
                        bufs["B_avs"][avb][0:17, 128 * t:128 * t + 128],
                        bufs["B_wvu"][0:17, 65 * h:65 * h + 65],
                        start=(t == 0), stop=(t == 3)), waits=w2,
                        inc=("sT" if t == 3 else None))
                omm = ctx.cnt["sT"]
                wv3 = [("sT", omm)]
                if sst_st[s % 2] > 0 and hh == 0:
                    wv3.append((f"d{2 + s % 2}", sst_st[s % 2]))
                ctx.op("vector", lambda e, ob=ob: e.reciprocal(
                    bufs["B_rcp"][:, 0:4].unsqueeze(2),
                    psum[ob][:, :].rearrange("p (t x) -> p t x", t=4)[:, :, 64:65]),
                    waits=wv3, drain=True)
                ctx.op("vector", lambda e, s=s, hh=hh, ob=ob: e.scalar_tensor_tensor(
                    bufs["B_sst"][s % 2].rearrange("p (t x) -> p t x", t=4)[:, :, 64 * hh:64 * hh + 64],
                    psum[ob][:, :].rearrange("p (t x) -> p t x", t=4)[:, :, 0:64],
                    1.0,
                    bufs["B_rcp"][:, 0:4].unsqueeze(2).broadcast_to((128, 4, 64)),
                    ALU.mult, ALU.mult), drain=True)
                out_norm[(hh, s)] = ctx.op("vector", lambda e, hh=hh, s=s, j=j: e.scalar_tensor_tensor(
                    bufs["B_sst"][s % 2].rearrange("p (t x) -> p t x", t=4)[:, :, 64 * hh:64 * hh + 64],
                    bufs["B_sst"][s % 2].rearrange("p (t x) -> p t x", t=4)[:, :, 64 * hh:64 * hh + 64],
                    1.0,
                    bufs["B_bvub"][:, 128 * j + 64 * hh:128 * j + 64 * hh + 64].unsqueeze(1).broadcast_to((128, 4, 64)),
                    ALU.mult, ALU.add), inc="sV", drain=True)
            sst_st[s % 2] = ctx.op("sync", lambda e, s=s, j=j: e.dma_start(
                out=io["s0"][512 * s:512 * s + 512, 128 * j:128 * j + 128].rearrange(
                    "(t p) c -> p t c", p=128),
                in_=bufs["B_sst"][s % 2].rearrange("p (t x) -> p t x", t=4)),
                waits=[("sV", out_norm[(1, s)])], inc=f"d{2 + s % 2}", inc_n=16)
        attn_done[b] = ctx.cnt["sT"]



def alloc_stageC(nc, ar, bufs):
    bufs["C_tbl"] = ar.allocN(2, 256, F32)
    ar.push()
    bufs["C_wqkv"] = ar.alloc(8 * 3072, BF16)
    bufs["C_wo"] = ar.alloc(8 * 1024, BF16)
    bufs["C_qsb"] = ar.alloc(4 * 64, F32)
    bufs["C_ra"] = ar.allocN(2, 1024, F32)
    bufs["C_rb"] = ar.alloc(1024, BF16)
    bufs["C_rT"] = ar.allocN(2, 8 * 129, BF16)
    bufs["C_qkv"] = ar.allocN(2, 3072, BF16)
    bufs["C_scr"] = ar.alloc(2048, BF16)
    bufs["C_nrm"] = ar.alloc(48, F32)
    bufs["C_rn"] = ar.alloc(32, F32)
    bufs["C_qh"] = ar.alloc(2048, BF16)
    bufs["C_qsw"] = ar.alloc(2048, BF16)
    bufs["C_qrot"] = ar.alloc(2048, BF16)
    bufs["C_AB"] = ar.alloc(256, F32)
    bufs["C_qT"] = ar.allocN(2, 1024, BF16)
    bufs["C_kTr"] = ar.alloc(4 * 1024, BF16)
    bufs["C_vr"] = ar.alloc(4 * 1040, BF16)
    bufs["C_pt"] = ar.allocN(4, 512, BF16)
    bufs["C_ocb"] = ar.alloc(1024, BF16)
    bufs["C_ocT"] = ar.alloc(1024, BF16)
    bufs["C_rcp"] = ar.alloc(4, F32)
    bufs["C_rout"] = ar.allocN(2, 1024, F32)
    attn_hi = ar.off
    ar.pop()
    ar.push()
    bufs["C_hT"] = ar.alloc(22 * 2048, BF16)
    bufs["C_U"] = ar.alloc(22 * 1024, BF16)
    bufs["C_w1a"] = ar.allocN(2, 8 * 128, BF16)
    bufs["C_w1g"] = ar.allocN(2, 8 * 128, BF16)
    bufs["C_b1row"] = ar.alloc(5504, BF16)
    bufs["C_b2row"] = ar.alloc(1024, BF16)
    bufs["C_gl"] = ar.allocN(2, 512, F32)
    bufs["C_xs"] = ar.allocN(2, 1024, F32)
    bufs["C_xn"] = ar.alloc(1024, F32)
    bufs["C_xnb"] = ar.alloc(1024, BF16)
    bufs["C_rr"] = ar.allocN(2, 1024, F32)
    bufs["C_ro2"] = ar.allocN(2, 1024, F32)
    bufs["C_st"] = ar.alloc(16, F32)
    bufs["C_gb"] = ar.alloc(1024, F32)
    bufs["C_bb"] = ar.alloc(1024, F32)
    bufs["C_bnd"] = ar.alloc(512, F32, parts=1)
    ffn_hi = ar.off
    ar.pop()
    ar.off = max(attn_hi, ffn_hi)


C_LAYERS = 6
C_ATTN = True
C_FFN = True
C_CUT = 99
C_SUB = 99

def emit_stageC(ctx, bufs, io):
    psum = bufs["psum"]
    cc = bufs["cconst"]
    identb = bufs["ident_sb"]
    r0 = io["r0"]
    rot = io["rot_all"].rearrange("(t p) d -> p t d", p=128)

    tbl_tok = [0, 0]
    tbl_user = [0, 0]

    for l in range(C_LAYERS):
        ctx.op("gpsimd", lambda e, l=l: e.dma_start(
            out=bufs["C_wqkv"].rearrange("p (c f) -> p c f", c=8),
            in_=io["Wqkv"][l].rearrange("(c p) f -> p c f", p=128)), inc="d6", inc_n=16)
        ctx.op("gpsimd", lambda e, l=l: e.dma_start(
            out=bufs["C_wo"].rearrange("p (c f) -> p c f", c=8),
            in_=io["Wo"][l].rearrange("(c p) f -> p c f", p=128)), inc="d6", inc_n=16)
        qs_ap = io["q_scale"][l]
        ks_ap = io["k_scale"][l]
        ctx.op("sync", lambda e, qs_ap=qs_ap: e.dma_start(
            out=bufs["C_qsb"][:, 0:64], in_=qs_ap.unsqueeze(0).partition_broadcast(128).squeeze(1)), inc="d12", inc_n=16)
        ctx.op("sync", lambda e, qs_ap=qs_ap: e.dma_start(
            out=bufs["C_qsb"][:, 64:96], in_=qs_ap[32:64].unsqueeze(0).partition_broadcast(128).squeeze(1)), inc="d12", inc_n=16)
        ctx.op("sync", lambda e, qs_ap=qs_ap: e.dma_start(
            out=bufs["C_qsb"][:, 96:128], in_=qs_ap[0:32].unsqueeze(0).partition_broadcast(128).squeeze(1)), inc="d12", inc_n=16)
        ctx.op("sync", lambda e, ks_ap=ks_ap: e.dma_start(
            out=bufs["C_qsb"][:, 128:192], in_=ks_ap.unsqueeze(0).partition_broadcast(128).squeeze(1)), inc="d12", inc_n=16)
        ctx.op("sync", lambda e, ks_ap=ks_ap: e.dma_start(
            out=bufs["C_qsb"][:, 192:224], in_=ks_ap[32:64].unsqueeze(0).partition_broadcast(128).squeeze(1)), inc="d12", inc_n=16)
        ctx.op("sync", lambda e, ks_ap=ks_ap: e.dma_start(
            out=bufs["C_qsb"][:, 224:256], in_=ks_ap[0:32].unsqueeze(0).partition_broadcast(128).squeeze(1)), inc="d12", inc_n=16)
        wq_tok = ctx.cnt["d6"]
        wq_tok2 = ctx.cnt["d12"]

        ctx.op("vector", lambda e: e.memset(
            bufs["C_vr"].rearrange("p (s h x) -> p s h x", s=4, h=16)[:, :, :, 64:65], 1.0), drain=True)

        ld_tok = [0] * NT
        cast_tok = [0] * NT
        rT_ev = [0] * NT
        rtr_done = [0] * NT
        qkv_ev = {}
        qkv_done = [0] * NT
        prep_done = [0] * NT
        ring_wr = [0] * NT
        pv_last = [0] * NT
        attn_sT_end = [0] * NT
        res_ev = [0] * NT
        st_tok = [0, 0]
        oc_guard = [0]
        qT_ev = [0] * NT
        bank_exp = {4: 0, 5: 0}
        pt_guard = {0: 0, 1: 0, 2: 0, 3: 0}
        norm_toks = {g: 0 for g in range(4)}

        qh_toks = [0] * NT
        rv_toks = [0] * NT
        ocT_evs = [0] * NT
        qrot_toks = [0] * NT
        prep_dones = [0] * NT

        def front(i):
            b = i % 2
            w = [("sV", res_ev[i - 2])] if i >= 2 else []
            ld_tok[i] = ctx.op("sync", lambda e, i=i, b=b: e.dma_start(
                out=bufs["C_ra"][b], in_=r0[128 * i:128 * i + 128, :]), waits=w,
                inc=f"d{b}", inc_n=16)
            wtb = [("sV", tbl_user[b])] if tbl_tok[b] > 0 else []
            tbl_tok[b] = ctx.op("sync", lambda e, i=i, b=b: e.dma_start(
                out=bufs["C_tbl"][b], in_=rot[:, i, :]), waits=wtb, inc=f"d{8 + b}", inc_n=16)
            wc = [(f"d{b}", ld_tok[i])]
            if i >= 1:
                wc.append(("sT", rtr_done[i - 1]))
            cast_tok[i] = ctx.op("gpsimd", lambda e, b=b: e.tensor_copy(bufs["C_rb"], bufs["C_ra"][b]),
                                 waits=wc, inc="sG", drain=True)
            pb = psum[2 + b][:, :].bitcast(BF16)
            for c in range(NCH):
                wt = [("sG", cast_tok[i])]
                if c == 0:
                    if i >= 2:
                        wt.append(("sV", ocT_evs[i - 2]))
                        wt.append(("sT", qkv_done[i - 2]))
                ctx.op("tensor", lambda e, c=c, pb=pb: e.transpose(
                    pb[:, 128 * c:128 * c + 128], bufs["C_rb"][:, 128 * c:128 * c + 128], identb),
                    waits=wt, inc=("sT" if c == NCH - 1 else None))
            rtr_done[i] = ctx.cnt["sT"]
            rT_ev[i] = ctx.op("vector", lambda e, b=b, pb=pb: e.tensor_copy(
                bufs["C_rT"][b].rearrange("p (c x) -> p c x", c=8)[:, :, 1:129],
                pb[:, 0:1024].rearrange("p (c x) -> p c x", c=8)),
                waits=[("sT", rtr_done[i])], inc="sV", drain=True)
            if i == 0:
                gv = ctx.op("vector", lambda e, b=b: e.memset(
                    bufs["C_rT"][b].rearrange("p (c x) -> p c x", c=8)[:, 4:8, 0:1], 0.0),
                    waits=[("sV", rT_ev[i])], inc="sV", drain=True)
            else:
                gv = ctx.op("vector", lambda e, b=b: e.tensor_copy(
                    bufs["C_rT"][b].rearrange("p (c x) -> p c x", c=8)[:, 4:8, 0:1],
                    bufs["C_rT"][1 - b].rearrange("p (c x) -> p c x", c=8)[:, 4:8, 128:129]),
                    waits=[("sV", rT_ev[i])], inc="sV", drain=True)
            rT_ready = gv
            for g in range(6):
                gb = g % 2
                for c in range(NCH):
                    wt = []
                    if c == 0:
                        wt.append(("sV", rT_ready))
                        if g == 0 and i == 0:
                            wt.append(("d6", wq_tok))
                        if g >= 2:
                            wt.append(("sC2", qkv_ev[(i, g - 2)]))
                        elif i >= 1:
                            wt.append(("sC2", qkv_ev[(i - 1, 4 + g)]))
                    off = 1 if c < 4 else 0
                    ctx.op("tensor", lambda e, c=c, g=g, b=b, gb=gb, off=off: e.matmul(
                        psum[gb][:, :],
                        bufs["C_rT"][b][:, 129 * c + off:129 * c + off + 128],
                        bufs["C_wqkv"][:, 3072 * c + 512 * g:3072 * c + 512 * g + 512],
                        start=(c == 0), stop=(c == NCH - 1)), waits=wt,
                        inc=("sT" if c == NCH - 1 else None))
                mmt = ctx.cnt["sT"]
                if g == 5:
                    qkv_done[i] = mmt
                wsc = [("sT", mmt)]
                if g == 0 and i >= 2:
                    wsc.append(("sV", qh_toks[i - 2]))
                    wsc.append(("sG", rv_toks[i - 2]))
                qkv_ev[(i, g)] = ctx.op("scalar", lambda e, g=g, gb=gb, b=b: e.copy(
                    bufs["C_qkv"][b][:, 512 * g:512 * g + 512], psum[gb][:, :]),
                    waits=wsc, inc="sC2", drain=True)

        def mid(i):
            b = i % 2
            qkv = bufs["C_qkv"][b]
            ctx.op("vector", lambda e, qkv=qkv: e.tensor_tensor(
                bufs["C_scr"], qkv[:, 0:2048], qkv[:, 0:2048], ALU.mult),
                waits=[("sC2", qkv_ev[(i, 3)])], drain=True)
            ctx.op("vector", lambda e: e.reduce_sum(
                bufs["C_nrm"][:, 0:32], bufs["C_scr"].rearrange("p (h x) -> p h x", h=32), axis=AX.X),
                drain=True)
            ctx.op("vector", lambda e: e.tensor_scalar(
                bufs["C_nrm"][:, 0:32], bufs["C_nrm"][:, 0:32], 1e-24, None, ALU.max),
                inc="sV", drain=True)
            nrm_tok = ctx.cnt["sV"]
            ctx.op("scalar", lambda e: e.activation(bufs["C_rn"][:, 0:32], bufs["C_nrm"][:, 0:32],
                   AF.Ln, bias=cc[:, C_ZERO:C_ZERO + 1], scale=cc[:, C_ONE:C_ONE + 1]),
                   waits=[("sV", nrm_tok)], drain=True)
            ctx.op("scalar", lambda e: e.activation(bufs["C_rn"][:, 0:32], bufs["C_rn"][:, 0:32],
                   AF.Exp, bias=cc[:, C_ZERO:C_ZERO + 1], scale=cc[:, C_NEGHALF:C_NEGHALF + 1]),
                   inc="sC", drain=True)
            rn_tok = ctx.cnt["sC"]
            wv = [(f"d{8 + b}", tbl_tok[b]), ("d12", wq_tok2)]
            for z in range(4):
                ctx.op("vector", lambda e, z=z, b=b: e.tensor_tensor(
                    bufs["C_AB"][:, 64 * z:64 * z + 64], bufs["C_tbl"][b][:, 64 * z:64 * z + 64],
                    bufs["C_qsb"][:, 64 * z:64 * z + 64], ALU.mult),
                    waits=(wv if z == 0 else []), inc=("sV" if z == 3 else None), drain=True)
            tbl_user[b] = ctx.cnt["sV"]
            qh_toks[i] = ctx.op("vector", lambda e, qkv=qkv: e.tensor_tensor(
                bufs["C_qh"].rearrange("p (h x) -> p h x", h=32),
                qkv[:, 0:2048].rearrange("p (h x) -> p h x", h=32),
                bufs["C_rn"][:, 0:32].unsqueeze(2).broadcast_to((128, 32, 64)), ALU.mult),
                waits=[("sC", rn_tok)], inc="sV", drain=True)
            ctx.op("gpsimd", lambda e: e.tensor_copy(
                bufs["C_qsw"].rearrange("p (h g x) -> p h g x", h=32, g=2)[:, :, 0, :],
                bufs["C_qh"].rearrange("p (h g x) -> p h g x", h=32, g=2)[:, :, 1, :]),
                waits=[("sV", qh_toks[i])], drain=True)
            ctx.op("gpsimd", lambda e: e.tensor_copy(
                bufs["C_qsw"].rearrange("p (h g x) -> p h g x", h=32, g=2)[:, :, 1, :],
                bufs["C_qh"].rearrange("p (h g x) -> p h g x", h=32, g=2)[:, :, 0, :]),
                inc="sG", drain=True)
            qsw_tok = ctx.cnt["sG"]
            ctx.op("vector", lambda e: e.tensor_tensor(
                bufs["C_scr"].rearrange("p (g h x) -> p g h x", g=2, h=16),
                bufs["C_qh"].rearrange("p (g h x) -> p g h x", g=2, h=16),
                bufs["C_AB"].rearrange("p (g two x) -> p g two x", g=2, two=2)[:, :, 0:1, :].broadcast_to((128, 2, 16, 64)),
                ALU.mult), drain=True)
            ctx.op("vector", lambda e: e.tensor_tensor(
                bufs["C_qsw"].rearrange("p (g h x) -> p g h x", g=2, h=16),
                bufs["C_qsw"].rearrange("p (g h x) -> p g h x", g=2, h=16),
                bufs["C_AB"].rearrange("p (g two x) -> p g two x", g=2, two=2)[:, :, 1:2, :].broadcast_to((128, 2, 16, 64)),
                ALU.mult), waits=[("sG", qsw_tok)], drain=True)
            ctx.op("vector", lambda e: e.tensor_tensor(
                bufs["C_qrot"], bufs["C_scr"], bufs["C_qsw"], ALU.add), inc="sV", drain=True)
            qrot_toks[i] = ctx.cnt["sV"]
            sl = i % 4
            wv2 = [("sC2", qkv_ev[(i, 5)])]
            if i >= 2:
                wv2.append(("sT", pv_last[i - 2]))
            ctx.op("gpsimd", lambda e, sl=sl, qkv=qkv: e.tensor_copy(
                bufs["C_vr"][:, 1040 * sl:1040 * sl + 1040].rearrange("p (h x) -> p h x", h=16)[:, :, 0:64],
                qkv[:, 2048:3072].rearrange("p (h x) -> p h x", h=16)),
                waits=wv2, inc="sG", drain=True)
            rv_toks[i] = ctx.cnt["sG"]

        def back(i):
            b = i % 2
            sl = i % 4
            pb = psum[2 + b][:, :].bitcast(BF16)
            for c in range(8):
                wt = [("sV", qrot_toks[i])]
                if c == 0:
                    wt.append(("sV", rT_ev[i]))
                ctx.op("tensor", lambda e, c=c, pb=pb: e.transpose(
                    pb[:, 128 * c:128 * c + 128], bufs["C_qrot"][:, 0:1024][:, 128 * c:128 * c + 128], identb),
                    waits=wt, inc=("sT" if c == 7 else None))
            qtr = ctx.cnt["sT"]
            wv3 = [("sT", qtr)]
            if i >= 2:
                wv3.append(("sT", attn_sT_end[i - 2]))
            qT_ev[i] = ctx.op("vector", lambda e, b=b, pb=pb: e.tensor_copy(
                bufs["C_qT"][b], pb[:, 0:1024]), waits=wv3, inc="sV", drain=True)
            for c in range(8):
                wt = [("sV", qT_ev[i])]
                ctx.op("tensor", lambda e, c=c, pb=pb: e.transpose(
                    pb[:, 128 * c:128 * c + 128], bufs["C_qrot"][:, 1024:2048][:, 128 * c:128 * c + 128], identb),
                    waits=(wt if c == 0 else []), inc=("sT" if c == 7 else None))
            ktr = ctx.cnt["sT"]
            wv4 = [("sT", ktr)]
            if i >= 2:
                wv4.append(("sT", pv_last[i - 2]))
            ring_wr[i] = ctx.op("vector", lambda e, sl=sl, pb=pb: e.tensor_copy(
                bufs["C_kTr"][:, 1024 * sl:1024 * sl + 1024], pb[:, 0:1024]),
                waits=wv4, inc="sV", drain=True)
            prep_dones[i] = ctx.cnt["sV"]
            cs = [c for c in (i - 2, i - 1, i) if c >= 0]
            exp_toks = {}
            for g in range(4):
                obank = 6 + (g % 2)
                for ci, c in enumerate(cs):
                    idx = g * len(cs) + ci
                    sbank = 4 + (idx % 2)
                    ptb = idx % 4
                    for m in range(4):
                        h = g + 4 * m
                        hp, par = h // 2, h % 2
                        base = 64 * par
                        wt = []
                        if m == 0:
                            wt.append(("sV", prep_dones[i]))
                            wt.append(("sG", rv_toks[i]))
                            if bank_exp[sbank] > 0:
                                wt.append(("sC", bank_exp[sbank]))
                        ctx.op("tensor", lambda e, c=c, hp=hp, base=base, b=b, m=m, sbank=sbank: e.matmul(
                            psum[sbank][:, 128 * m:128 * m + 128],
                            bufs["C_kTr"][base:base + 64, 1024 * (c % 4) + 128 * hp:1024 * (c % 4) + 128 * hp + 128],
                            bufs["C_qT"][b][base:base + 64, 128 * hp:128 * hp + 128],
                            start=True, stop=True), waits=wt,
                            inc=("sT" if m == 3 else None))
                    smm = ctx.cnt["sT"]
                    we = [("sT", smm)]
                    if pt_guard[ptb]:
                        we.append(("sT", pt_guard[ptb]))
                    exp_toks[(g, ci)] = ctx.op("scalar", lambda e, ptb=ptb, sbank=sbank: e.activation(
                        bufs["C_pt"][ptb], psum[sbank][:, :], AF.Exp,
                        bias=cc[:, C_ZERO:C_ZERO + 1], scale=cc[:, C_ONE:C_ONE + 1]),
                        waits=we, inc="sC")
                    bank_exp[sbank] = exp_toks[(g, ci)]
                    pvw, pvt = "sC", exp_toks[(g, ci)]
                    if c == i - 2:
                        ctx.op("vector", lambda e, ptb=ptb: e.tensor_tensor(
                            bufs["C_pt"][ptb], bufs["C_pt"][ptb], bufs["tri_lo4_sb"], ALU.mult),
                            waits=[("sC", exp_toks[(g, ci)])], inc="sV", drain=True)
                        pvw, pvt = "sV", ctx.cnt["sV"]
                    elif c == i:
                        ctx.op("vector", lambda e, ptb=ptb: e.tensor_tensor(
                            bufs["C_pt"][ptb], bufs["C_pt"][ptb], bufs["tri_up4_sb"], ALU.mult),
                            waits=[("sC", exp_toks[(g, ci)])], inc="sV", drain=True)
                        pvw, pvt = "sV", ctx.cnt["sV"]
                    for m in range(4):
                        h = g + 4 * m
                        wpv = [(pvw, pvt)] if m == 0 else []
                        if m == 0 and ci == 0 and norm_toks[(g - 2) % 4] > 0:
                            wpv.append(("sV", norm_toks[(g - 2) % 4]))
                        if m == 0 and ci == 0 and obank == 7 and i >= 1:
                            wpv.append(("sV", res_ev[i - 1]))
                        ctx.op("tensor", lambda e, m=m, c=c, h=h, ptb=ptb, obank=obank, ci=ci, ncs=len(cs): e.matmul(
                            psum[obank][:, 128 * m:128 * m + 65],
                            bufs["C_pt"][ptb][:, 128 * m:128 * m + 128],
                            bufs["C_vr"][:, 1040 * (c % 4) + 65 * h:1040 * (c % 4) + 65 * h + 65],
                            start=(ci == 0 and m == 0), stop=(ci == ncs - 1 and m == 3)), waits=wpv,
                            inc=("sT" if m == 3 else None))
                    pt_guard[ptb] = ctx.cnt["sT"]
                pv_last[i] = ctx.cnt["sT"]
                wn = [("sT", pv_last[i])]
                if oc_guard[0]:
                    wn.append(("sT", oc_guard[0]))
                ctx.op("vector", lambda e, obank=obank: e.reciprocal(
                    bufs["C_rcp"][:, 0:4].unsqueeze(2),
                    psum[obank][:, :].rearrange("p (m x) -> p m x", m=4)[:, :, 64:65]),
                    waits=wn, drain=True)
                norm_toks[g] = ctx.op("vector", lambda e, g=g, obank=obank: e.scalar_tensor_tensor(
                    bufs["C_ocb"].rearrange("p (m x) -> p m x", m=4)[:, :, 64 * g:64 * g + 64],
                    psum[obank][:, :].rearrange("p (m x) -> p m x", m=4)[:, :, 0:64],
                    1.0,
                    bufs["C_rcp"][:, 0:4].unsqueeze(2).broadcast_to((128, 4, 64)),
                    ALU.mult, ALU.mult), inc="sV", drain=True)
            attn_sT_end[i] = ctx.cnt["sT"]
            all_norm = max(norm_toks.values())
            for c in range(8):
                wt = [("sV", all_norm), ("sV", ring_wr[i])]
                ctx.op("tensor", lambda e, c=c, pb=pb: e.transpose(
                    pb[:, 128 * c:128 * c + 128], bufs["C_ocb"][:, 128 * c:128 * c + 128], identb),
                    waits=(wt if c == 0 else []), inc=("sT" if c == 7 else None))
            octr = ctx.cnt["sT"]
            ocT_evs[i] = ctx.op("vector", lambda e, pb=pb: e.tensor_copy(
                bufs["C_ocT"], pb[:, 0:1024]), waits=[("sT", octr)], inc="sV", drain=True)
            oc_guard[0] = octr
            for half in range(2):
                for c in range(NCH):
                    wt = []
                    if c == 0:
                        wt.append(("sV", ocT_evs[i]))
                        if half == 0 and i >= 1:
                            wt.append(("sV", res_ev[i - 1]))
                        if half == 1:
                            wt.append(("sV", ctx.cnt["sV"]))
                    ctx.op("tensor", lambda e, c=c, half=half: e.matmul(
                        psum[7][:, :],
                        bufs["C_ocT"][:, 128 * c:128 * c + 128],
                        bufs["C_wo"][:, 1024 * c + 512 * half:1024 * c + 512 * half + 512],
                        start=(c == 0), stop=(c == NCH - 1)), waits=wt,
                        inc=("sT" if c == NCH - 1 else None))
                womm = ctx.cnt["sT"]
                wv6 = [("sT", womm)]
                if half == 0 and st_tok[b] > 0:
                    wv6.append((f"d{2 + b}", st_tok[b]))
                res_ev[i] = ctx.op("vector", lambda e, half=half, b=b: e.tensor_tensor(
                    bufs["C_rout"][b][:, 512 * half:512 * half + 512], psum[7][:, :],
                    bufs["C_ra"][b][:, 512 * half:512 * half + 512], ALU.add),
                    waits=wv6, inc="sV", drain=True)
            st_tok[b] = ctx.op("sync", lambda e, i=i, b=b: e.dma_start(
                out=r0[128 * i:128 * i + 128, :], in_=bufs["C_rout"][b]),
                waits=[("sV", res_ev[i])], inc=f"d{2 + b}", inc_n=16)

        if C_ATTN:
            front(0)
            for i in range(NT):
                if i + 1 < NT:
                    front(i + 1)
                mid(i)
                back(i)
        ctx.barrier()

        ctx.op("gpsimd", lambda e, l=l: e.dma_start(
            out=bufs["C_b1row"][0:1, 0:5460], in_=io["b1"][l].unsqueeze(0)), inc="d6", inc_n=16)
        ctx.op("gpsimd", lambda e, l=l: e.dma_start(
            out=bufs["C_b2row"][0:1, :], in_=io["b2"][l].unsqueeze(0)), inc="d6", inc_n=16)
        ctx.op("sync", lambda e: e.dma_start(
            out=bufs["C_bnd"], in_=r0[2047:2048, 512:1024]), inc="d12", inc_n=16)
        ctx.op("sync", lambda e, l=l: e.dma_start(
            out=bufs["C_gb"], in_=io["ln_g"][l].unsqueeze(0).partition_broadcast(128).squeeze(1)), inc="d12", inc_n=16)
        ctx.op("sync", lambda e, l=l: e.dma_start(
            out=bufs["C_bb"], in_=io["ln_b"][l].unsqueeze(0).partition_broadcast(128).squeeze(1)), inc="d12", inc_n=16)
        brow_tok = ctx.cnt["d6"]
        brow_tok2 = ctx.cnt["d12"]

        for half in range(2 if C_FFN else 0):
            n0 = 2048 * half
            xs_tok = [0] * 16
            xn_ev = [0] * 16
            xnT_ev = [0] * 16
            for t in range(16):
                i = 16 * half + t
                bt = t % 2
                w = [("sV", xn_ev[t - 2])] if t >= 2 else []
                ctx.op("sync", lambda e, i=i, bt=bt: e.dma_start(
                    out=bufs["C_xs"][bt][:, 0:512], in_=r0[128 * i:128 * i + 128, 0:512]),
                    waits=w, inc=f"d{bt}", inc_n=16)
                if i == 0:
                    ctx.op("vector", lambda e, bt=bt: e.memset(bufs["C_xs"][bt][0:1, 512:1024], 0.0), drain=True)
                    xs_tok[t] = ctx.op("sync", lambda e, i=i, bt=bt: e.dma_start(
                        out=bufs["C_xs"][bt][1:128, 512:1024], in_=r0[0:127, 512:1024]),
                        inc=f"d{bt}", inc_n=16)
                elif t == 0:
                    ctx.op("sync", lambda e, bt=bt: e.dma_start(
                        out=bufs["C_xs"][bt][0:1, 512:1024], in_=bufs["C_bnd"]),
                        waits=[("d12", brow_tok2)], inc=f"d{bt}", inc_n=16)
                    xs_tok[t] = ctx.op("sync", lambda e, i=i, bt=bt: e.dma_start(
                        out=bufs["C_xs"][bt][1:128, 512:1024], in_=r0[128 * i:128 * i + 127, 512:1024]),
                        inc=f"d{bt}", inc_n=16)
                else:
                    xs_tok[t] = ctx.op("sync", lambda e, i=i, bt=bt: e.dma_start(
                        out=bufs["C_xs"][bt][:, 512:1024], in_=r0[128 * i - 1:128 * i + 127, 512:1024]),
                        inc=f"d{bt}", inc_n=16)
                ctx.op("vector", lambda e, bt=bt: e.tensor_scalar(
                    bufs["C_xn"], bufs["C_xs"][bt], 1.0, 0.0, ALU.mult, ALU.add, accum_out=bufs["C_st"][:, 0:1]),
                    waits=[(f"d{bt}", xs_tok[t]), ("d6", brow_tok), ("d12", brow_tok2)], drain=True)
                ctx.op("vector", lambda e, bt=bt: e.scalar_tensor_tensor(
                    bufs["C_xn"], bufs["C_xs"][bt], 1.0, bufs["C_xs"][bt], ALU.mult, ALU.mult,
                    accum_out=bufs["C_st"][:, 1:2]), drain=True)
                ctx.op("vector", lambda e: e.tensor_scalar(bufs["C_st"][:, 2:3], bufs["C_st"][:, 0:1], 1.0 / D, None, ALU.mult), drain=True)
                ctx.op("vector", lambda e: e.tensor_tensor(bufs["C_st"][:, 3:4], bufs["C_st"][:, 2:3], bufs["C_st"][:, 2:3], ALU.mult), drain=True)
                ctx.op("vector", lambda e: e.tensor_scalar(bufs["C_st"][:, 4:5], bufs["C_st"][:, 1:2], 1.0 / D, None, ALU.mult), drain=True)
                ctx.op("vector", lambda e: e.tensor_tensor(bufs["C_st"][:, 4:5], bufs["C_st"][:, 4:5], bufs["C_st"][:, 3:4], ALU.subtract), drain=True)
                ctx.op("vector", lambda e: e.tensor_scalar(bufs["C_st"][:, 4:5], bufs["C_st"][:, 4:5], 1e-5, None, ALU.add), inc="sV", drain=True)
                stv = ctx.cnt["sV"]
                ctx.op("scalar", lambda e: e.activation(bufs["C_st"][:, 5:6], bufs["C_st"][:, 4:5], AF.Ln,
                       bias=cc[:, C_ZERO:C_ZERO + 1], scale=cc[:, C_ONE:C_ONE + 1]),
                       waits=[("sV", stv)], drain=True)
                ctx.op("scalar", lambda e: e.activation(bufs["C_st"][:, 6:7], bufs["C_st"][:, 5:6], AF.Exp,
                       bias=cc[:, C_ZERO:C_ZERO + 1], scale=cc[:, C_NEGHALF:C_NEGHALF + 1]),
                       inc="sC", drain=True)
                rstdt = ctx.cnt["sC"]
                ctx.op("vector", lambda e, bt=bt: e.scalar_tensor_tensor(
                    bufs["C_xn"], bufs["C_xs"][bt], bufs["C_st"][:, 2:3], bufs["C_gb"],
                    ALU.subtract, ALU.mult), waits=[("sC", rstdt)], drain=True)
                xn_ev[t] = ctx.op("vector", lambda e: e.scalar_tensor_tensor(
                    bufs["C_xn"], bufs["C_xn"], bufs["C_st"][:, 6:7], bufs["C_bb"],
                    ALU.mult, ALU.add), inc="sV", drain=True)
                ctx.op("gpsimd", lambda e: e.tensor_copy(bufs["C_xnb"], bufs["C_xn"]),
                       waits=[("sV", xn_ev[t])], inc="sG", drain=True)
                csg = ctx.cnt["sG"]
                pbx = psum[bt][:, :].bitcast(BF16)
                for c in range(NCH):
                    wt = [("sG", csg)]
                    if t >= 2 and c == 0:
                        wt.append(("sV", xnT_ev[t - 2]))
                    ctx.op("tensor", lambda e, c=c, pbx=pbx: e.transpose(
                        pbx[:, 128 * c:128 * c + 128], bufs["C_xnb"][:, 128 * c:128 * c + 128], identb),
                        waits=wt, inc=("sT" if c == NCH - 1 else None))
                xtr = ctx.cnt["sT"]
                xnT_ev[t] = ctx.op("vector", lambda e, t=t, pbx=pbx: e.tensor_copy(
                    bufs["C_U"][:, 0:16384].rearrange("p (c x) -> p c x", c=8)[:, :, 128 * t:128 * t + 128],
                    pbx[:, 0:1024].rearrange("p (c x) -> p c x", c=8)),
                    waits=[("sT", xtr)], inc="sV", drain=True)
            xnT_done = ctx.cnt["sV"]

            w1_tok = [0, 0]
            w1_guard = [0, 0]
            jws = [128] * 21 + [42]
            h_ev = {}
            for j in range(22):
                js = j % 2
                jw = jws[j]
                wld = [("sT", w1_guard[js])] if w1_guard[js] else []
                ctx.op("gpsimd", lambda e, l=l, j=j, js=js, jw=jw: e.dma_start(
                    out=bufs["C_w1a"][js].rearrange("p (c x) -> p c x", c=8)[:, :, 0:jw],
                    in_=io["W1"][l].rearrange("(c p) f -> p c f", p=128)[:, :, 128 * j:128 * j + jw]),
                    waits=wld, inc=f"d{4 + js}", inc_n=16)
                w1_tok[js] = ctx.op("gpsimd", lambda e, l=l, j=j, js=js, jw=jw: e.dma_start(
                    out=bufs["C_w1g"][js].rearrange("p (c x) -> p c x", c=8)[:, :, 0:jw],
                    in_=io["W1"][l].rearrange("(c p) f -> p c f", p=128)[:, :, 2730 + 128 * j:2730 + 128 * j + jw]),
                    inc=f"d{4 + js}", inc_n=16)
                for ng in range(4):
                    ab = 2 + (ng % 2)
                    gb_ = 4 + (ng % 2)
                    for c in range(NCH):
                        wt = []
                        if c == 0:
                            wt.append((f"d{4 + js}", w1_tok[js]))
                            if j == 0:
                                wt.append(("sV", xnT_done))
                            if (j, ng - 2) in h_ev:
                                wt.append(("sV", h_ev[(j, ng - 2)]))
                            elif (j - 1, ng + 2) in h_ev:
                                wt.append(("sV", h_ev[(j - 1, ng + 2)]))
                        ctx.op("tensor", lambda e, c=c, ng=ng, js=js, jw=jw, ab=ab: e.matmul(
                            psum[ab][0:jw, :],
                            bufs["C_w1a"][js][:, 128 * c:128 * c + jw],
                            bufs["C_U"][:, 0:16384][:, 2048 * c + 512 * ng:2048 * c + 512 * ng + 512],
                            start=(c == 0), stop=False), waits=wt)
                        ctx.op("tensor", lambda e, c=c, ng=ng, js=js, jw=jw, gb_=gb_: e.matmul(
                            psum[gb_][0:jw, :],
                            bufs["C_w1g"][js][:, 128 * c:128 * c + jw],
                            bufs["C_U"][:, 0:16384][:, 2048 * c + 512 * ng:2048 * c + 512 * ng + 512],
                            start=(c == 0), stop=False))
                    ctx.op("tensor", lambda e, j=j, jw=jw, ab=ab: e.matmul(
                        psum[ab][0:jw, :], bufs["C_b1row"][0:1, 128 * j:128 * j + jw],
                        bufs["ones_sb"][0:1, 0:512], start=False, stop=True))
                    ctx.op("tensor", lambda e, j=j, jw=jw, gb_=gb_: e.matmul(
                        psum[gb_][0:jw, :], bufs["C_b1row"][0:1, 2730 + 128 * j:2730 + 128 * j + jw],
                        bufs["ones_sb"][0:1, 0:512], start=False, stop=True), inc="sT")
                    mmt = ctx.cnt["sT"]
                    glb = ng % 2
                    wgl = [("sT", mmt)]
                    if (j, ng - 2) in h_ev:
                        wgl.append(("sV", h_ev[(j, ng - 2)]))
                    elif (j - 1, ng + 2) in h_ev:
                        wgl.append(("sV", h_ev[(j - 1, ng + 2)]))
                    ctx.op("scalar", lambda e, jw=jw, gb_=gb_, glb=glb: e.activation(
                        bufs["C_gl"][glb][0:jw, :], psum[gb_][0:jw, :], AF.Gelu,
                        bias=cc[0:jw, C_ZERO:C_ZERO + 1], scale=cc[0:jw, C_ONE:C_ONE + 1]),
                        waits=wgl, inc="sC", drain=True)
                    glt = ctx.cnt["sC"]
                    h_ev[(j, ng)] = ctx.op("vector", lambda e, j=j, ng=ng, jw=jw, ab=ab, glb=glb: e.tensor_tensor(
                        bufs["C_hT"][0:jw, 2048 * j + 512 * ng:2048 * j + 512 * ng + 512],
                        psum[ab][0:jw, :], bufs["C_gl"][glb][0:jw, :], ALU.mult),
                        waits=[("sC", glt)], inc="sV", drain=True)
                w1_guard[js] = ctx.cnt["sT"]
            h_done = ctx.cnt["sV"]

            ctx.op("gpsimd", lambda e, l=l: e.dma_start(
                out=bufs["C_U"].rearrange("p (c x) -> p c x", c=22)[:, 0:21, :],
                in_=io["W2"][l][0:2688].rearrange("(c p) d -> p c d", p=128)),
                waits=[("sV", h_done)], inc="d7", inc_n=16)
            ctx.op("gpsimd", lambda e, l=l: e.dma_start(
                out=bufs["C_U"][0:42, :].rearrange("p (c x) -> p c x", c=22)[:, 21, :],
                in_=io["W2"][l][2688:2730]), inc="d7", inc_n=16)
            w2_tok = ctx.cnt["d7"]
            rr_tok = [0, 0]
            y_ev = {}
            st2_tok = [0, 0]
            for t in range(16):
                i = 16 * half + t
                bt = t % 2
                wr = []
                if t >= 2:
                    wr.append(("sV", y_ev[(t - 2, 1)]))
                rr_tok[bt] = ctx.op("sync", lambda e, i=i, bt=bt: e.dma_start(
                    out=bufs["C_rr"][bt], in_=r0[128 * i:128 * i + 128, :]),
                    waits=wr, inc=f"d{13 + bt}", inc_n=16)
                pbank = [6, 7] if t % 2 == 0 else [4, 5]
                for dh in range(2):
                    pk = pbank[dh]
                    for j in range(22):
                        jw = jws[j]
                        wt = []
                        if j == 0:
                            wt.append(("d7", w2_tok))
                            wt.append(("sV", h_done))
                            if t >= 2:
                                wt.append(("sV", y_ev[(t - 2, dh)]))
                        ctx.op("tensor", lambda e, t=t, j=j, jw=jw, dh=dh, pk=pk: e.matmul(
                            psum[pk][:, :],
                            bufs["C_hT"][0:jw, 2048 * j + 128 * t:2048 * j + 128 * t + 128],
                            bufs["C_U"][0:jw, 1024 * j + 512 * dh:1024 * j + 512 * dh + 512],
                            start=(j == 0), stop=False), waits=wt)
                    ctx.op("tensor", lambda e, dh=dh, pk=pk: e.matmul(
                        psum[pk][:, :], bufs["ones_sb"][0:1, 0:128],
                        bufs["C_b2row"][0:1, 512 * dh:512 * dh + 512],
                        start=False, stop=True), inc="sT")
                    ymm = ctx.cnt["sT"]
                    wv7 = [("sT", ymm), (f"d{13 + bt}", rr_tok[bt])]
                    if st2_tok[bt] > 0 and dh == 0:
                        wv7.append((f"d{2 + bt}", st2_tok[bt]))
                    y_ev[(t, dh)] = ctx.op("vector", lambda e, dh=dh, bt=bt, pk=pk: e.tensor_tensor(
                        bufs["C_ro2"][bt][:, 512 * dh:512 * dh + 512], psum[pk][:, :],
                        bufs["C_rr"][bt][:, 512 * dh:512 * dh + 512], ALU.add),
                        waits=wv7, inc="sV", drain=True)
                st2_tok[bt] = ctx.op("sync", lambda e, i=i, bt=bt: e.dma_start(
                    out=r0[128 * i:128 * i + 128, :], in_=bufs["C_ro2"][bt]),
                    waits=[("sV", y_ev[(t, 1)])], inc=f"d{2 + bt}", inc_n=16)
            ctx.barrier()


def alloc_stageD(nc, ar, bufs):
    bufs["D_x"] = ar.allocN(2, 1024, F32)
    bufs["D_s"] = ar.allocN(2, 1024, F32)
    bufs["D_f"] = ar.allocN(2, 1024, F32)
    bufs["D_g"] = ar.alloc(1024, F32)
    bufs["D_x1"] = ar.allocN(2, 1024, F32)
    bufs["D_xn"] = ar.allocN(2, 1024, F32)
    bufs["D_xnb"] = ar.alloc(1024, BF16)
    bufs["D_xtv"] = ar.allocN(2, 1024, BF16)
    bufs["D_st"] = ar.alloc(8, F32)
    bufs["D_scr"] = ar.alloc(1024, F32)


def emit_stageD(ctx, bufs, io):
    psum = bufs["psum"]
    cc = bufs["cconst"]
    identb = bufs["ident_sb"]
    x, s0, r0 = io["x"], io["s0"], io["r0"]
    x1, xn1T = io["x1"], io["xn1T"]
    xn1T_r = xn1T.rearrange("(c p) n -> p c n", p=128)
    st = bufs["D_st"]

    ldx = [0] * NT; lds = [0] * NT; ldf = [0] * NT
    gdone = [0] * NT; x1done = [0] * NT; xndone = [0] * NT
    stx1 = [0, 0]; stxn = [0, 0]; ttr = [0] * NT; vev = [0] * NT
    cast_t = [0] * NT

    for i in range(NT):
        b = i % 2
        wx = [("sV", x1done[i - 2])] if i >= 2 else []
        ldx[i] = ctx.op("sync", lambda e, i=i, b=b: e.dma_start(
            out=bufs["D_x"][b], in_=x[128 * i:128 * i + 128, :]), waits=wx, inc=f"d{b}", inc_n=16)
        lds[i] = ctx.op("sync", lambda e, i=i, b=b: e.dma_start(
            out=bufs["D_s"][b], in_=s0[128 * i:128 * i + 128, :]), waits=wx, inc=f"d{8 + b}", inc_n=16)
        wf = [("sC", gdone[i - 2])] if i >= 2 else []
        ldf[i] = ctx.op("sync", lambda e, i=i, b=b: e.dma_start(
            out=bufs["D_f"][b], in_=r0[128 * i:128 * i + 128, :]), waits=wf, inc=f"d{13 + b}", inc_n=16)
        wg = [(f"d{13 + b}", ldf[i])]
        if i >= 1:
            wg.append(("sV", x1done[i - 1]))
        gdone[i] = ctx.op("scalar", lambda e, b=b: e.activation(
            bufs["D_g"], bufs["D_f"][b], AF.Silu,
            bias=cc[:, C_ZERO:C_ZERO + 1], scale=cc[:, C_ONE:C_ONE + 1]),
            waits=wg, inc="sC", drain=True)
        wv = [("sC", gdone[i]), (f"d{b}", ldx[i]), (f"d{8 + b}", lds[i])]
        if i >= 2:
            wv.append((f"d{2 + b}", stx1[b]))
        ctx.op("vector", lambda e, b=b: e.tensor_tensor(
            bufs["D_scr"], bufs["D_g"], bufs["D_s"][b], ALU.mult), waits=wv, drain=True)
        x1done[i] = ctx.op("vector", lambda e, b=b: e.tensor_tensor(
            bufs["D_x1"][b], bufs["D_scr"], bufs["D_x"][b], ALU.add), inc="sV", drain=True)
        stx1[b] = ctx.op("sync", lambda e, i=i, b=b: e.dma_start(
            out=x1[128 * i:128 * i + 128, :], in_=bufs["D_x1"][b]),
            waits=[("sV", x1done[i])], inc=f"d{2 + b}", inc_n=16)
        ctx.op("vector", lambda e, b=b: e.tensor_scalar(
            bufs["D_scr"], bufs["D_x1"][b], 1.0, 0.0, ALU.mult, ALU.add, accum_out=st[:, 0:1]), drain=True)
        ctx.op("vector", lambda e, b=b: e.scalar_tensor_tensor(
            bufs["D_scr"], bufs["D_x1"][b], 1.0, bufs["D_x1"][b], ALU.mult, ALU.mult,
            accum_out=st[:, 1:2]), drain=True)
        ctx.op("vector", lambda e: e.tensor_scalar(st[:, 2:3], st[:, 0:1], 1.0 / D, None, ALU.mult), drain=True)
        ctx.op("vector", lambda e: e.tensor_tensor(st[:, 3:4], st[:, 2:3], st[:, 2:3], ALU.mult), drain=True)
        ctx.op("vector", lambda e: e.tensor_scalar(st[:, 4:5], st[:, 1:2], 1.0 / D, None, ALU.mult), drain=True)
        ctx.op("vector", lambda e: e.tensor_tensor(st[:, 4:5], st[:, 4:5], st[:, 3:4], ALU.subtract), drain=True)
        ctx.op("vector", lambda e: e.tensor_scalar(st[:, 4:5], st[:, 4:5], 1e-5, None, ALU.add), inc="sV", drain=True)
        stv = ctx.cnt["sV"]
        ctx.op("scalar", lambda e: e.activation(st[:, 5:6], st[:, 4:5], AF.Ln,
               bias=cc[:, C_ZERO:C_ZERO + 1], scale=cc[:, C_ONE:C_ONE + 1]),
               waits=[("sV", stv)], drain=True)
        ctx.op("scalar", lambda e: e.activation(st[:, 6:7], st[:, 5:6], AF.Exp,
               bias=cc[:, C_ZERO:C_ZERO + 1], scale=cc[:, C_NEGHALF:C_NEGHALF + 1]),
               inc="sC", drain=True)
        rst = ctx.cnt["sC"]
        wv2 = [("sC", rst)]
        if i >= 2:
            wv2.append(("sG", cast_t[i - 2]))
        xndone[i] = ctx.op("vector", lambda e, b=b: e.tensor_scalar(
            bufs["D_xn"][b], bufs["D_x1"][b], st[:, 2:3], st[:, 6:7], ALU.subtract, ALU.mult),
            waits=wv2, inc="sV", drain=True)
        wcast = [("sV", xndone[i])]
        if i >= 1:
            wcast.append(("sT", ttr[i - 1]))
        cast_t[i] = ctx.op("gpsimd", lambda e, b=b: e.tensor_copy(bufs["D_xnb"], bufs["D_xn"][b]),
                           waits=wcast, inc="sG", drain=True)
        pb = psum[4 + b][:, :].bitcast(BF16)
        for c in range(NCH):
            wt = [("sG", cast_t[i])]
            if i >= 2 and c == 0:
                wt.append(("sV", vev[i - 2]))
            ctx.op("tensor", lambda e, c=c, pb=pb: e.transpose(
                pb[:, 128 * c:128 * c + 128], bufs["D_xnb"][:, 128 * c:128 * c + 128], identb),
                waits=wt, inc=("sT" if c == NCH - 1 else None))
        ttr[i] = ctx.cnt["sT"]
        we = [("sT", ttr[i])]
        if i >= 2:
            we.append((f"d{15 + b}", stxn[b]))
        vev[i] = ctx.op("vector", lambda e, b=b, pb=pb: e.tensor_copy(bufs["D_xtv"][b], pb[:, 0:1024]),
                        waits=we, inc="sV", drain=True)
        stxn[b] = ctx.op("sync", lambda e, i=i, b=b: e.dma_start(
            out=xn1T_r[:, :, 128 * i:128 * i + 128],
            in_=bufs["D_xtv"][b].rearrange("p (c n) -> p c n", c=8)),
            waits=[("sV", vev[i])], inc=f"d{15 + b}", inc_n=16)


def alloc_stageE(nc, ar, bufs):
    bufs["E_w"] = ar.allocN(2, 4096, F32)
    bufs["E_acc"] = ar.alloc(4, F32)
    bufs["E_scr"] = ar.alloc(128, F32)
    bufs["E_thr"] = ar.alloc(4, F32)
    bufs["E_ge"] = ar.alloc(4096, F32)
    bufs["E_le"] = ar.alloc(4096, F32)
    bufs["E_wt"] = ar.alloc(4096, BF16)
    bufs["E_tv"] = ar.allocN(2, 4096, BF16)


def _stageE_one(ctx, bufs, io, wname, oname, rows, cols, scr_name):
    psum = bufs["psum"]
    cc = bufs["cconst"]
    identb = bufs["ident_sb"]
    W = io[wname]
    OT = io[oname].rearrange("(c p) n -> p c n", p=128)
    ntile = rows // 128
    nch = cols // 128
    numel = rows * cols
    ld = [0] * ntile
    used = [0] * ntile
    for t in range(ntile):
        b = t % 2
        w = [("sV", used[t - 2])] if t >= 2 else []
        ld[t] = ctx.op("gpsimd", lambda e, t=t, b=b, W=W, cols=cols: e.dma_start(
            out=bufs["E_w"][b][:, 0:cols], in_=W[128 * t:128 * t + 128, :]),
            waits=w, inc=f"d{10 + b}", inc_n=16)
        ctx.op("vector", lambda e, b=b, cols=cols: e.tensor_reduce(
            bufs["E_scr"][:, 0:1], bufs["E_w"][b][:, 0:cols], AX.X, ALU.add,
            apply_absolute_value=True), waits=[(f"d{10 + b}", ld[t])], drain=True)
        if t == 0:
            used[t] = ctx.op("vector", lambda e: e.tensor_copy(
                bufs["E_acc"][:, 0:1], bufs["E_scr"][:, 0:1]), inc="sV", drain=True)
        else:
            used[t] = ctx.op("vector", lambda e: e.tensor_tensor(
                bufs["E_acc"][:, 0:1], bufs["E_acc"][:, 0:1], bufs["E_scr"][:, 0:1], ALU.add),
                inc="sV", drain=True)
    ctx.op("tensor", lambda e: e.transpose(
        psum[0][0:1, 0:128], bufs["E_acc"][:, 0:1], bufs["ident_f"]),
        waits=[("sV", used[ntile - 1])], inc="sT")
    tt = ctx.cnt["sT"]
    ctx.op("vector", lambda e: e.reduce_sum(
        bufs["E_thr"][0:1, 0:1], psum[0][0:1, 0:128], axis=AX.X), waits=[("sT", tt)], inc="sV", drain=True)
    ctx.op("vector", lambda e, numel=numel: e.tensor_scalar(
        bufs["E_thr"][0:1, 1:2], bufs["E_thr"][0:1, 0:1], 1.0 / numel, 1e-8, ALU.mult, ALU.max), drain=True)
    ctx.op("vector", lambda e: e.tensor_scalar(
        bufs["E_thr"][0:1, 2:3], bufs["E_thr"][0:1, 1:2], 0.5, None, ALU.mult), inc="sV", drain=True)
    thr_tok = ctx.cnt["sV"]
    ctx.op("sync", lambda e, scr_name=scr_name: e.dma_start(
        out=io[scr_name], in_=bufs["E_thr"][0:1, 2:3]), waits=[("sV", thr_tok)], inc="d19", inc_n=16)
    st8 = ctx.cnt["d19"]
    ctx.op("sync", lambda e, scr_name=scr_name: e.dma_start(
        out=bufs["E_thr"][:, 3:4], in_=io[scr_name].partition_broadcast(128).squeeze(1)),
        waits=[("d19", st8)], inc="d19", inc_n=16)
    bcast_tok = ctx.cnt["d19"]
    st_tok = [0, 0]
    tern_done = [0] * ntile
    for t in range(ntile):
        b = t % 2
        w = [("d19", bcast_tok)]
        if t >= 2:
            w.append(("sV", tern_done[t - 2]))
        ld2 = ctx.op("gpsimd", lambda e, t=t, b=b, W=W, cols=cols: e.dma_start(
            out=bufs["E_w"][b][:, 0:cols], in_=W[128 * t:128 * t + 128, :]),
            waits=w, inc=f"d{10 + b}", inc_n=16)
        wv = [(f"d{10 + b}", ld2)]
        ctx.op("vector", lambda e, b=b, cols=cols: e.tensor_scalar(
            bufs["E_ge"][:, 0:cols], bufs["E_w"][b][:, 0:cols], bufs["E_thr"][:, 3:4], None, ALU.is_ge),
            waits=wv, drain=True)
        ctx.op("vector", lambda e, b=b, cols=cols: e.tensor_scalar(
            bufs["E_le"][:, 0:cols], bufs["E_w"][b][:, 0:cols],
            -1.0, bufs["E_thr"][:, 3:4], ALU.mult, ALU.is_ge), drain=True)
        tern_done[t] = ctx.op("vector", lambda e, cols=cols: e.tensor_sub(
            bufs["E_wt"][:, 0:cols], bufs["E_ge"][:, 0:cols], bufs["E_le"][:, 0:cols]),
            inc="sV", drain=True)
        for rnd in range((nch + 7) // 8):
            pb = psum[2 + (t + rnd) % 2][:, :].bitcast(BF16)
            ncthis = min(8, nch - 8 * rnd)
            for c in range(ncthis):
                wt = [("sV", tern_done[t])]
                if t * 8 + rnd >= 2 and c == 0:
                    wt.append(("sV", ev_guard[(t + rnd) % 2]))
                ctx.op("tensor", lambda e, c=c, rnd=rnd, pb=pb: e.transpose(
                    pb[:, 128 * c:128 * c + 128],
                    bufs["E_wt"][:, 1024 * rnd + 128 * c:1024 * rnd + 128 * c + 128], identb),
                    waits=wt, inc=("sT" if c == ncthis - 1 else None))
            trt = ctx.cnt["sT"]
            we = [("sT", trt)]
            if st_tok[(t + rnd) % 2] > 0:
                we.append((f"d{17 + (t + rnd) % 2}", st_tok[(t + rnd) % 2]))
            b2 = (t + rnd) % 2
            ev = ctx.op("vector", lambda e, b2=b2, ncthis=ncthis, pb=pb: e.tensor_copy(
                bufs["E_tv"][b2][:, 0:128 * ncthis], pb[:, 0:128 * ncthis]),
                waits=we, inc="sV", drain=True)
            ev_guard[b2] = ev
            st_tok[b2] = ctx.op("sync", lambda e, t=t, rnd=rnd, ncthis=ncthis,
                                b2=b2, OT=OT: e.dma_start(
                out=OT[:, 8 * rnd:8 * rnd + ncthis, 128 * t:128 * t + 128],
                in_=bufs["E_tv"][b2][:, 0:128 * ncthis].rearrange("p (c n) -> p c n", n=128)),
                waits=[("sV", ev)], inc=f"d{17 + b2}", inc_n=16)


def emit_stageE(ctx, bufs, io):
    global ev_guard
    ev_guard = {0: 0, 1: 0}
    _stageE_one(ctx, bufs, io, "W1q", "w1tT", 4096, 1024, "escr1")
    ctx.barrier()
    ev_guard = {0: 0, 1: 0}
    _stageE_one(ctx, bufs, io, "W2q", "w2tT", 1024, 4096, "escr2")


def alloc_stageF(nc, ar, bufs):
    bufs["F_xnT"] = ar.alloc(8 * 4096, BF16)
    bufs["F_wst"] = ar.alloc(8 * 1024, BF16)
    bufs["F_wqT"] = ar.alloc(8 * 1024, BF16)
    bufs["F_wkT"] = ar.alloc(8 * 1024, BF16)
    bufs["F_wvT"] = ar.alloc(8 * 1024, BF16)
    bufs["F_woT"] = ar.alloc(8 * 1024, BF16)
    bufs["F_om"] = ar.alloc(64, BF16)
    bufs["F_omst"] = ar.alloc(64, BF16)
    bufs["F_bo"] = ar.alloc(1024, BF16)
    bufs["F_q"] = ar.alloc(1024, F32)
    bufs["F_k"] = ar.alloc(1024, F32)
    bufs["F_qb"] = ar.alloc(1024, BF16)
    bufs["F_kb"] = ar.alloc(1024, BF16)
    bufs["F_v"] = ar.alloc(16 * 65, BF16)
    bufs["F_scr"] = ar.alloc(1024, F32)
    bufs["F_nu"] = ar.alloc(32, F32)
    bufs["F_phq"] = ar.alloc(1024, BF16)
    bufs["F_phk"] = ar.alloc(1024, BF16)
    bufs["F_pqT"] = ar.alloc(1024, BF16)
    bufs["F_pkT"] = ar.alloc(1024, BF16)
    bufs["F_AT"] = ar.allocN(2, 128, BF16)
    bufs["F_Sf"] = ar.alloc(8 * 65, F32)
    bufs["F_Sb"] = ar.alloc(8 * 65, BF16)
    bufs["F_oc"] = ar.alloc(1024, BF16)
    bufs["F_ocT"] = ar.alloc(1024, BF16)
    bufs["F_s1"] = ar.allocN(2, 1024, F32)
    bufs["F_z"] = ar.alloc(4, F32)


def emit_stageF(ctx, bufs, io):
    psum = bufs["psum"]
    cc = bufs["cconst"]
    identb = bufs["ident_sb"]
    LN8 = math.log(8.0)
    wev = {}
    at_guard = {0: 0, 1: 0}
    s1_ev_half = {}

    ctx.op("gpsimd", lambda e: e.dma_start(
        out=bufs["F_xnT"].rearrange("p (c n) -> p c n", c=8),
        in_=io["xn1T"].rearrange("(c p) n -> p c n", p=128)), inc="gld", inc_n=16)
    ctx.op("gpsimd", lambda e: e.dma_start(out=bufs["F_omst"][0:64, :], in_=io["omega"]),
           inc="gld", inc_n=16)
    ctx.op("gpsimd", lambda e: e.dma_start(out=bufs["F_bo"][0:1, :], in_=io["bo1"].unsqueeze(0)),
           inc="gld", inc_n=16)
    gld_pre = ctx.cnt["gld"]
    wT_ready = {}
    for wi, (wn, dst) in enumerate([("wq1", "F_wqT"), ("wk1", "F_wkT"),
                                    ("wv1", "F_wvT"), ("wo1", "F_woT")]):
        ldt = ctx.op("gpsimd", lambda e, wn=wn: e.dma_start(
            out=bufs["F_wst"].rearrange("p (c n) -> p c n", c=8),
            in_=io[wn].rearrange("(c p) n -> p c n", p=128)),
            waits=([("sT", wT_ready[wi - 1])] if wi > 0 else []), inc="d10", inc_n=16)
        for cf in range(8):
            pbw = psum[cf % 2][:, :].bitcast(BF16)
            for c in range(NCH):
                wt = []
                if c == 0:
                    wt.append(("d10", ldt))
                    if wi > 0 or cf >= 2:
                        wt.append(("sV", wev.get((wi, cf - 2), wev.get((wi - 1, cf + 6), 0))))
                ctx.op("tensor", lambda e, c=c, cf=cf, pbw=pbw: e.transpose(
                    pbw[:, 128 * c:128 * c + 128],
                    bufs["F_wst"][:, 1024 * cf + 128 * c:1024 * cf + 128 * c + 128], identb),
                    waits=wt, inc=("sT" if c == NCH - 1 else None))
            trt = ctx.cnt["sT"]
            wev[(wi, cf)] = ctx.op("vector", lambda e, cf=cf, dst=dst, pbw=pbw: e.tensor_copy(
                bufs[dst].rearrange("p (c n) -> p c n", c=8)[:, :, 128 * cf:128 * cf + 128],
                pbw[:, 0:1024].rearrange("p (c n) -> p c n", c=8)),
                waits=[("sT", trt)], inc="sV", drain=True)
        wT_ready[wi] = ctx.cnt["sT"]
    wT_all = ctx.cnt["sV"]
    ctx.op("tensor", lambda e: e.transpose(
        psum[2][:, :].bitcast(BF16)[0:64, 0:64], bufs["F_omst"][0:64, :], identb[0:64, 0:64]),
        waits=[("gld", gld_pre), ("sV", wT_all)], inc="sT")
    omt = ctx.cnt["sT"]
    omev = ctx.op("vector", lambda e: e.tensor_copy(
        bufs["F_om"][0:64, :], psum[2][:, :].bitcast(BF16)[0:64, 0:64]),
        waits=[("sT", omt)], inc="sV", drain=True)
    omrep = ctx.op("gpsimd", lambda e: e.dma_start(out=bufs["F_om"][64:128, :], in_=bufs["F_om"][0:64, :]),
                   waits=[("sV", omev)], inc="gld", inc_n=16)
    pre_all = ctx.cnt["gld"]
    ctx.op("vector", lambda e: e.memset(bufs["F_Sf"], 0.0), drain=True)
    ctx.op("vector", lambda e: e.memset(bufs["F_Sb"], 0.0), drain=True)
    zs = ctx.op("vector", lambda e: e.memset(
        bufs["F_v"].rearrange("p (h x) -> p h x", h=16)[:, :, 64:65], 1.0), inc="sV", drain=True)

    qkv_ev = {}
    qkv_mm_done = [0] * NT
    exp_done = {}
    bank67_guard = {6: 0, 7: 0}
    bank45_state = {4: 0, 5: 0}
    st_tok = [0, 0]
    s1_ev = [0, 0]
    scan_done = [0] * NT
    proj_done = [0] * NT
    cast_done = [0] * NT
    phT_ev = [0] * NT
    oct_ev = [0] * NT
    norm_tok = {h: 0 for h in range(16)}
    mask_tok = {h: 0 for h in range(16)}
    state_tok = {h: 0 for h in range(16)}

    for i in range(NT):
        for gi, (dstb, bks) in enumerate([("F_q", (0, 1)), ("F_k", (2, 3)), (None, (4, 5))]):
            for hf in range(2):
                bk = bks[hf]
                wsel = [bufs["F_wqT"], bufs["F_wkT"], bufs["F_wvT"]][gi]
                for c in range(NCH):
                    wt = []
                    if c == 0:
                        wt.append(("gld", pre_all))
                        wt.append(("sV", zs))
                        if i > 0:
                            wt.append(("sV", qkv_ev[(i - 1, gi, hf)]))
                        if gi == 0 and i > 0:
                            wt.append(("sV", oct_ev[i - 1]))
                        if gi == 1 and i > 0:
                            wt.append(("sV", s1_ev[(i - 1) % 2]))
                    ctx.op("tensor", lambda e, c=c, i=i, hf=hf, bk=bk, wsel=wsel: e.matmul(
                        psum[bk][:, :],
                        bufs["F_xnT"][:, 4096 * c + 128 * i:4096 * c + 128 * i + 128],
                        wsel[:, 1024 * c + 512 * hf:1024 * c + 512 * hf + 512],
                        start=(c == 0), stop=(c == NCH - 1)), waits=wt,
                        inc=("sT" if c == NCH - 1 else None))
                mmt = ctx.cnt["sT"]
                if gi < 2:
                    qkv_ev[(i, gi, hf)] = ctx.op("vector", lambda e, gi=gi, hf=hf, bk=bk, dstb=dstb: e.tensor_copy(
                        bufs[dstb][:, 512 * hf:512 * hf + 512], psum[bk][:, :]),
                        waits=[("sT", mmt)], inc="sV", drain=True)
                else:
                    wv = [("sT", mmt)]
                    if i > 0:
                        wv.append(("sT", scan_done[i - 1]))
                    qkv_ev[(i, gi, hf)] = ctx.op("vector", lambda e, hf=hf, bk=bk: e.tensor_copy(
                        bufs["F_v"].rearrange("p (h x) -> p h x", h=16)[:, 8 * hf:8 * hf + 8, 0:64],
                        psum[bk][:, :].rearrange("p (h x) -> p h x", h=8)),
                        waits=wv, inc="sV", drain=True)
        qkv_mm_done[i] = ctx.cnt["sT"]
        ctx.op("vector", lambda e: e.tensor_tensor(
            bufs["F_scr"], bufs["F_q"], bufs["F_q"], ALU.mult),
            waits=[("sV", qkv_ev[(i, 0, 1)])], drain=True)
        ctx.op("vector", lambda e: e.reduce_sum(
            bufs["F_nu"][:, 0:16], bufs["F_scr"].rearrange("p (h x) -> p h x", h=16), axis=AX.X), drain=True)
        ctx.op("vector", lambda e: e.tensor_scalar(
            bufs["F_nu"][:, 0:16], bufs["F_nu"][:, 0:16], -0.5, -LN8, ALU.mult, ALU.add), inc="sV", drain=True)
        ctx.op("vector", lambda e: e.tensor_tensor(
            bufs["F_scr"], bufs["F_k"], bufs["F_k"], ALU.mult),
            waits=[("sV", qkv_ev[(i, 1, 1)])], drain=True)
        ctx.op("vector", lambda e: e.reduce_sum(
            bufs["F_nu"][:, 16:32], bufs["F_scr"].rearrange("p (h x) -> p h x", h=16), axis=AX.X), drain=True)
        ctx.op("vector", lambda e: e.tensor_scalar(
            bufs["F_nu"][:, 16:32], bufs["F_nu"][:, 16:32], -0.5, -LN8, ALU.mult, ALU.add), inc="sV", drain=True)
        nu_tok = ctx.cnt["sV"]
        wcast = [("sV", qkv_ev[(i, 0, 1)])]
        if i > 0:
            wcast.append(("sT", proj_done[i - 1]))
        ctx.op("gpsimd", lambda e: e.tensor_copy(bufs["F_qb"], bufs["F_q"]), waits=wcast, drain=True)
        cast_done[i] = ctx.op("gpsimd", lambda e: e.tensor_copy(bufs["F_kb"], bufs["F_k"]),
                              waits=[("sV", qkv_ev[(i, 1, 1)])], inc="sG", drain=True)
        pb0 = psum[0][:, :].bitcast(BF16)
        pb1 = psum[1][:, :].bitcast(BF16)
        for c in range(8):
            wt = [("sG", cast_done[i])]
            if c == 0:
                wt.append(("sV", qkv_ev[(i, 0, 1)]))
            ctx.op("tensor", lambda e, c=c, pb0=pb0: e.transpose(
                pb0[:, 128 * c:128 * c + 128], bufs["F_qb"][:, 128 * c:128 * c + 128], identb),
                waits=wt, inc=("sT" if c == 7 else None))
        qtr = ctx.cnt["sT"]
        qT_ev = ctx.op("vector", lambda e, pb0=pb0: e.tensor_copy(bufs["F_pqT"], pb0[:, 0:1024]),
                       waits=[("sT", qtr)], inc="sV", drain=True)
        for c in range(8):
            ctx.op("tensor", lambda e, c=c, pb1=pb1: e.transpose(
                pb1[:, 128 * c:128 * c + 128], bufs["F_kb"][:, 128 * c:128 * c + 128], identb),
                waits=([("sG", cast_done[i])] if c == 0 else []), inc=("sT" if c == 7 else None))
        ktr = ctx.cnt["sT"]
        kT_ev = ctx.op("vector", lambda e, pb1=pb1: e.tensor_copy(bufs["F_pkT"], pb1[:, 0:1024]),
                       waits=[("sT", ktr)], inc="sV", drain=True)
        for qk in range(2):
            srcT = bufs["F_pqT"] if qk == 0 else bufs["F_pkT"]
            dstP = bufs["F_phq"] if qk == 0 else bufs["F_phk"]
            for h in range(16):
                hp, par = h // 2, h % 2
                base = 64 * par
                pk = 6 + ((qk * 16 + h) % 2)
                wt = [("sV", qT_ev if qk == 0 else kT_ev)]
                if bank67_guard[pk]:
                    wt.append(("sC", bank67_guard[pk]))
                if norm_tok[14 + (pk - 6)] > 0:
                    wt.append(("sV", norm_tok[14 + (pk - 6)]))
                ctx.op("tensor", lambda e, hp=hp, base=base, srcT=srcT, pk=pk: e.matmul(
                    psum[pk][:, 0:64],
                    srcT[base:base + 64, 128 * hp:128 * hp + 128],
                    bufs["F_om"][base:base + 64, :],
                    start=True, stop=True), waits=wt, inc="sT")
                pmm = ctx.cnt["sT"]
                bcol = h if qk == 0 else 16 + h
                bank67_guard[pk] = ctx.op("scalar", lambda e, h=h, bcol=bcol, dstP=dstP, pk=pk: e.activation(
                    dstP[:, 64 * h:64 * h + 64], psum[pk][:, 0:64], AF.Exp,
                    bias=bufs["F_nu"][:, bcol:bcol + 1], scale=cc[:, C_ONE:C_ONE + 1]),
                    waits=[("sT", pmm), ("sV", nu_tok)], inc="sC")
        proj_done[i] = ctx.cnt["sT"]
        exp_phi_done = ctx.cnt["sC"]
        for c in range(8):
            wt = [("sC", exp_phi_done)]
            if c == 0:
                wt.append(("sV", qT_ev))
                wt.append(("sT", proj_done[i]))
            ctx.op("tensor", lambda e, c=c, pb0=pb0: e.transpose(
                pb0[:, 128 * c:128 * c + 128], bufs["F_phq"][:, 128 * c:128 * c + 128], identb),
                waits=wt, inc=("sT" if c == 7 else None))
        pqtr = ctx.cnt["sT"]
        pqT_ev = ctx.op("vector", lambda e, pb0=pb0: e.tensor_copy(bufs["F_pqT"], pb0[:, 0:1024]),
                        waits=[("sT", pqtr)], inc="sV", drain=True)
        for c in range(8):
            wt = [("sC", exp_phi_done)]
            if c == 0:
                wt.append(("sV", kT_ev))
            ctx.op("tensor", lambda e, c=c, pb1=pb1: e.transpose(
                pb1[:, 128 * c:128 * c + 128], bufs["F_phk"][:, 128 * c:128 * c + 128], identb),
                waits=wt, inc=("sT" if c == 7 else None))
        pktr = ctx.cnt["sT"]
        pkT_ev = ctx.op("vector", lambda e, pb1=pb1: e.tensor_copy(bufs["F_pkT"], pb1[:, 0:1024]),
                        waits=[("sT", pktr)], inc="sV", drain=True)
        phT_ev[i] = ctx.cnt["sV"]

        for h in range(16):
            hp, par = h // 2, h % 2
            base = 64 * par
            ab = 4 + (h % 2)
            ob = 6 + (h % 2)
            atb = h % 2
            pba = psum[ab][:, :].bitcast(BF16)
            wt = [("sV", pkT_ev), ("sV", pqT_ev)]
            if state_tok[(h - 2) % 16] > 0:
                wt.append(("sV", state_tok[(h - 2) % 16]))
            if i > 0 and mask_tok[(h - 2) % 16] > 0 and h < 2:
                pass
            ctx.op("tensor", lambda e, hp=hp, base=base, ab=ab: e.matmul(
                psum[ab][:, 0:128],
                bufs["F_pkT"][base:base + 64, 128 * hp:128 * hp + 128],
                bufs["F_pqT"][base:base + 64, 128 * hp:128 * hp + 128],
                start=True, stop=True), waits=wt, inc="sT")
            atmm = ctx.cnt["sT"]
            wm = [("sT", atmm)]
            if at_guard[atb] > 0:
                wm.append(("sT", at_guard[atb]))
            mask_tok[h] = ctx.op("vector", lambda e, atb=atb, ab=ab: e.tensor_tensor(
                bufs["F_AT"][atb], psum[ab][:, 0:128], bufs["tri_up_sb"], ALU.mult),
                waits=wm, inc="sV", drain=True)
            wo1 = [("sV", mask_tok[h]), ("sV", qkv_ev[(i, 2, 1)])]
            if norm_tok[(h - 2) % 16] > 0:
                wo1.append(("sV", norm_tok[(h - 2) % 16]))
            ctx.op("tensor", lambda e, h=h, atb=atb, ob=ob: e.matmul(
                psum[ob][:, 0:65], bufs["F_AT"][atb],
                bufs["F_v"][:, 65 * h:65 * h + 65], start=True, stop=False), waits=wo1)
            wo2 = []
            if state_tok[h] > 0:
                wo2.append(("sV", state_tok[h]))
            ctx.op("tensor", lambda e, h=h, hp=hp, base=base, ob=ob: e.matmul(
                psum[ob][:, 0:65],
                bufs["F_pqT"][base:base + 64, 128 * hp:128 * hp + 128],
                bufs["F_Sb"][base:base + 64, 65 * hp:65 * hp + 65],
                start=False, stop=True), waits=wo2, inc="sT")
            at_guard[atb] = ctx.cnt["sT"]
            omm = ctx.cnt["sT"]
            ctx.op("tensor", lambda e, h=h, base=base, ab=ab: e.matmul(
                psum[ab][base:base + 64, 128:193],
                bufs["F_phk"][:, 64 * h:64 * h + 64],
                bufs["F_v"][:, 65 * h:65 * h + 65], start=True, stop=True), inc="sT")
            umm = ctx.cnt["sT"]
            ctx.op("vector", lambda e, ob=ob: e.tensor_scalar(
                bufs["F_z"][:, 0:1], psum[ob][:, 64:65], 1e-6, None, ALU.add),
                waits=[("sT", omm)], drain=True)
            ctx.op("vector", lambda e: e.reciprocal(bufs["F_z"][:, 1:2], bufs["F_z"][:, 0:1]), drain=True)
            norm_tok[h] = ctx.op("vector", lambda e, h=h, ob=ob: e.tensor_scalar(
                bufs["F_oc"][:, 64 * h:64 * h + 64], psum[ob][:, 0:64],
                bufs["F_z"][:, 1:2], None, ALU.mult), inc="sV", drain=True)
            ctx.op("vector", lambda e, h=h, hp=hp, base=base, ab=ab: e.tensor_tensor(
                bufs["F_Sf"][base:base + 64, 65 * hp:65 * hp + 65],
                bufs["F_Sf"][base:base + 64, 65 * hp:65 * hp + 65],
                psum[ab][base:base + 64, 128:193], ALU.add),
                waits=[("sT", umm)], drain=True)
            state_tok[h] = ctx.op("vector", lambda e, hp=hp, base=base: e.tensor_copy(
                bufs["F_Sb"][base:base + 64, 65 * hp:65 * hp + 65],
                bufs["F_Sf"][base:base + 64, 65 * hp:65 * hp + 65]), inc="sV", drain=True)
        scan_done[i] = ctx.cnt["sT"]
        for c in range(8):
            wt = [("sV", norm_tok[2 * c + 1]), ("sV", norm_tok[2 * c])]
            if c == 0:
                wt.append(("sV", pqT_ev))
            ctx.op("tensor", lambda e, c=c, pb0=pb0: e.transpose(
                pb0[:, 128 * c:128 * c + 128], bufs["F_oc"][:, 128 * c:128 * c + 128], identb),
                waits=wt, inc=("sT" if c == 7 else None))
        octr = ctx.cnt["sT"]
        ocT_evt = ctx.op("vector", lambda e, pb0=pb0: e.tensor_copy(bufs["F_ocT"], pb0[:, 0:1024]),
                         waits=[("sT", octr)], inc="sV", drain=True)
        oct_ev[i] = ocT_evt
        bs = i % 2
        for hf in range(2):
            bk = 2 + hf
            for c in range(NCH):
                wt = []
                if c == 0:
                    wt.append(("sV", ocT_evt))
                    if (i - 1, hf) in s1_ev_half:
                        wt.append(("sV", s1_ev_half[(i - 1, hf)]))
                ctx.op("tensor", lambda e, c=c, hf=hf, bk=bk: e.matmul(
                    psum[bk][:, :],
                    bufs["F_ocT"][:, 128 * c:128 * c + 128],
                    bufs["F_woT"][:, 1024 * c + 512 * hf:1024 * c + 512 * hf + 512],
                    start=(c == 0), stop=False), waits=wt)
            ctx.op("tensor", lambda e, hf=hf, bk=bk: e.matmul(
                psum[bk][:, :], bufs["ones_sb"][0:1, 0:128],
                bufs["F_bo"][0:1, 512 * hf:512 * hf + 512], start=False, stop=True), inc="sT")
            smm = ctx.cnt["sT"]
            wv = [("sT", smm)]
            if st_tok[bs] > 0 and hf == 0:
                wv.append((f"d{2 + bs}", st_tok[bs]))
            s1_ev_half[(i, hf)] = ctx.op("vector", lambda e, hf=hf, bs=bs, bk=bk: e.tensor_copy(
                bufs["F_s1"][bs][:, 512 * hf:512 * hf + 512], psum[bk][:, :]),
                waits=wv, inc="sV", drain=True)
        s1_ev[bs] = s1_ev_half[(i, 1)]
        st_tok[bs] = ctx.op("sync", lambda e, i=i, bs=bs: e.dma_start(
            out=io["s1"][128 * i:128 * i + 128, :], in_=bufs["F_s1"][bs]),
            waits=[("sV", s1_ev_half[(i, 1)])], inc=f"d{2 + bs}", inc_n=16)


def alloc_stageG(nc, ar, bufs):
    bufs["G_w1"] = ar.alloc(8 * 4096, BF16)
    bufs["G_w2"] = ar.alloc(32 * 1024, BF16)
    bufs["G_b1"] = ar.alloc(4096, BF16)
    bufs["G_b2"] = ar.alloc(1024, BF16)
    bufs["G_xnb"] = ar.allocN(2, 8 * 256, BF16)
    bufs["G_h"] = ar.alloc(32 * 256, BF16)
    bufs["G_g"] = ar.alloc(1024, F32)
    bufs["G_x1"] = ar.allocN(2, 1024, F32)
    bufs["G_s1"] = ar.allocN(2, 1024, F32)
    bufs["G_o"] = ar.allocN(2, 1024, F32)
    bufs["G_scr"] = ar.alloc(1024, F32)


def emit_stageG(ctx, bufs, io):
    psum = bufs["psum"]
    cc = bufs["cconst"]

    ctx.op("gpsimd", lambda e: e.dma_start(
        out=bufs["G_w1"].rearrange("p (c n) -> p c n", c=8),
        in_=io["w1tT"].rearrange("(c p) n -> p c n", p=128)), inc="gld", inc_n=16)
    ctx.op("gpsimd", lambda e: e.dma_start(
        out=bufs["G_w2"].rearrange("p (c n) -> p c n", c=32),
        in_=io["w2tT"].rearrange("(c p) n -> p c n", p=128)), inc="gld", inc_n=16)
    ctx.op("gpsimd", lambda e: e.dma_start(out=bufs["G_b1"][0:1, :], in_=io["b1q"].unsqueeze(0)), inc="gld", inc_n=16)
    ctx.op("gpsimd", lambda e: e.dma_start(out=bufs["G_b2"][0:1, :], in_=io["b2q"].unsqueeze(0)), inc="gld", inc_n=16)
    gpre = ctx.cnt["gld"]

    xn1T_r = io["xn1T"].rearrange("(c p) n -> p c n", p=128)
    h_ev = {}
    y_ev = {}
    st_tok = [0, 0]
    ld_tok = [0, 0]
    x1_tok = [0, 0]
    s1_tok = [0, 0]
    gate_done = [0, 0]
    phase1_done = [0] * 16

    for nb in range(16):
        b = nb % 2
        w = [("sT", phase1_done[nb - 2])] if nb >= 2 else []
        ld_tok[b] = ctx.op("sync", lambda e, nb=nb, b=b: e.dma_start(
            out=bufs["G_xnb"][b].rearrange("p (c n) -> p c n", c=8),
            in_=xn1T_r[:, :, 256 * nb:256 * nb + 256]), waits=w, inc=f"d{b}", inc_n=16)
        for j in range(32):
            bk = j % 2
            for c in range(NCH):
                wt = []
                if c == 0:
                    wt.append((f"d{b}", ld_tok[b]))
                    if nb == 0 and j < 2:
                        wt.append(("gld", gpre))
                    if (nb, j - 2) in h_ev:
                        wt.append(("sC", h_ev[(nb, j - 2)]))
                    elif (nb - 1, j + 30) in h_ev:
                        wt.append(("sC", h_ev[(nb - 1, j + 30)]))
                ctx.op("tensor", lambda e, c=c, j=j, b=b, bk=bk: e.matmul(
                    psum[bk][:, 0:256],
                    bufs["G_w1"][:, 4096 * c + 128 * j:4096 * c + 128 * j + 128],
                    bufs["G_xnb"][b][:, 256 * c:256 * c + 256],
                    start=(c == 0), stop=False), waits=wt)
            ctx.op("tensor", lambda e, j=j, bk=bk: e.matmul(
                psum[bk][:, 0:256], bufs["G_b1"][0:1, 128 * j:128 * j + 128],
                bufs["ones_sb"][0:1, 0:256], start=False, stop=True), inc="sT")
            mmt = ctx.cnt["sT"]
            wg = [("sT", mmt)]
            if (nb, j - 2) in h_ev or (nb - 1, j + 30) in h_ev:
                pass
            h_ev[(nb, j)] = ctx.op("scalar", lambda e, j=j, bk=bk: e.activation(
                bufs["G_h"][:, 256 * j:256 * j + 256], psum[bk][:, 0:256], AF.Gelu,
                bias=cc[:, C_ZERO:C_ZERO + 1], scale=cc[:, C_ONE:C_ONE + 1]),
                waits=wg, inc="sC", drain=True)
        phase1_done[nb] = ctx.cnt["sT"]
        h_all = ctx.cnt["sC"]
        for t in range(2):
            i = 2 * nb + t
            pb_pair = (2, 3) if t == 0 else (4, 5)
            wl = [("sV", gate_done[b])] if gate_done[b] else []
            x1_tok[b] = ctx.op("sync", lambda e, i=i, b=b: e.dma_start(
                out=bufs["G_x1"][b], in_=io["x1"][128 * i:128 * i + 128, :]),
                waits=wl, inc=f"d{8 + b}", inc_n=16)
            s1_tok[b] = ctx.op("sync", lambda e, i=i, b=b: e.dma_start(
                out=bufs["G_s1"][b], in_=io["s1"][128 * i:128 * i + 128, :]),
                inc=f"d{13 + b}", inc_n=16)
            for dh in range(2):
                pk = pb_pair[dh]
                for j in range(32):
                    wt = []
                    if j == 0:
                        wt.append(("sC", h_all))
                        if (i - 2, dh) in y_ev:
                            wt.append(("sC", y_ev[(i - 2, dh)]))
                    ctx.op("tensor", lambda e, j=j, t=t, dh=dh, pk=pk: e.matmul(
                        psum[pk][:, :],
                        bufs["G_h"][:, 256 * j + 128 * t:256 * j + 128 * t + 128],
                        bufs["G_w2"][:, 1024 * j + 512 * dh:1024 * j + 512 * dh + 512],
                        start=(j == 0), stop=False), waits=wt)
                ctx.op("tensor", lambda e, dh=dh, pk=pk: e.matmul(
                    psum[pk][:, :], bufs["ones_sb"][0:1, 0:128],
                    bufs["G_b2"][0:1, 512 * dh:512 * dh + 512], start=False, stop=True), inc="sT")
                ymm = ctx.cnt["sT"]
                wsl = [("sT", ymm)]
                if dh == 0 and (gate_done[0] or gate_done[1]):
                    wsl.append(("sV", max(gate_done)))
                y_ev[(i, dh)] = ctx.op("scalar", lambda e, dh=dh, pk=pk: e.activation(
                    bufs["G_g"][:, 512 * dh:512 * dh + 512], psum[pk][:, :], AF.Silu,
                    bias=cc[:, C_ZERO:C_ZERO + 1], scale=cc[:, C_ONE:C_ONE + 1]),
                    waits=wsl, inc="sC", drain=True)
            wv = [("sC", y_ev[(i, 1)]), (f"d{8 + b}", x1_tok[b]), (f"d{13 + b}", s1_tok[b])]
            if st_tok[b] > 0:
                wv.append((f"d{2 + b}", st_tok[b]))
            ctx.op("vector", lambda e, b=b: e.tensor_tensor(
                bufs["G_scr"], bufs["G_g"], bufs["G_s1"][b], ALU.mult), waits=wv, drain=True)
            gate_done[b] = ctx.op("vector", lambda e, b=b: e.tensor_tensor(
                bufs["G_o"][b], bufs["G_scr"], bufs["G_x1"][b], ALU.add), inc="sV", drain=True)
            st_tok[b] = ctx.op("sync", lambda e, i=i, b=b: e.dma_start(
                out=io["out"][128 * i:128 * i + 128, :], in_=bufs["G_o"][b]),
                waits=[("sV", gate_done[b])], inc=f"d{2 + b}", inc_n=16)

def alloc_stageDE(nc, ar, bufs):
    alloc_stageD(nc, ar, bufs)
    alloc_stageE(nc, ar, bufs)


def emit_stageDE(ctx, bufs, io):
    emit_stageD(ctx, bufs, io)
    emit_stageE(ctx, bufs, io)


STAGES = {
    "A": (alloc_stageA, emit_stageA),
    "B": (alloc_stageB, emit_stageB),
    "C": (alloc_stageC, emit_stageC),
    "D": (alloc_stageD, emit_stageD),
    "E": (alloc_stageE, emit_stageE),
    "DE": (alloc_stageDE, emit_stageDE),
    "F": (alloc_stageF, emit_stageF),
    "G": (alloc_stageG, emit_stageG),
}
STAGE_IO["DE"] = (sorted(set(STAGE_IO["D"][0]) | set(STAGE_IO["E"][0])),
                  sorted(set(STAGE_IO["D"][1]) | set(STAGE_IO["E"][1])), [])


def build_group(stages, keep_outputs=None):
    table = dram_table()
    reads, writes, rws = set(), set(), set()
    for s in stages:
        r, w, rw = STAGE_IO[s]
        for name in r:
            if name not in writes and name not in rws:
                reads.add(name)
        for name in rw:
            if name not in writes and name not in rws:
                rws.add(name)
            writes.add(name)
        writes.update(w)
    if keep_outputs is None:
        keep_outputs = set(writes) | set(rws)
    nc = bass.Bass("TRN2", target_bir_lowering=False)
    io = {}
    in_names, out_names = [], []
    seed_pairs = []
    for name in sorted(rws):
        sh, dt = table[name]
        io[name + "__seed"] = nc.dram_tensor(name + "__seed", sh, dt, kind="ExternalInput").ap()
        in_names.append(name + "__seed")
        kind = "ExternalOutput" if name in keep_outputs else "Internal"
        io[name] = nc.dram_tensor(name, sh, dt, kind=kind).ap()
        if kind == "ExternalOutput":
            out_names.append(name)
        seed_pairs.append(name)
    for name in sorted(reads - rws):
        sh, dt = table[name]
        io[name] = nc.dram_tensor(name, sh, dt, kind="ExternalInput").ap()
        in_names.append(name)
    for name in sorted(writes - rws):
        sh, dt = table[name]
        kind = "ExternalOutput" if name in keep_outputs else "Internal"
        io[name] = nc.dram_tensor(name, sh, dt, kind=kind).ap()
        if kind == "ExternalOutput":
            out_names.append(name)

    ar = Arena(nc)
    bufs = {}
    bufs["cconst"] = ar.alloc(NCONST, F32)
    bufs["ident_sb"] = ar.alloc(128, BF16)
    bufs["ident_f"] = ar.alloc(128, F32)
    bufs["ones_sb"] = ar.alloc(512, BF16, parts=1)
    bufs["tri_lo_sb"] = ar.alloc(128, BF16)
    bufs["tri_up_sb"] = ar.alloc(128, BF16)
    bufs["tri_lo4_sb"] = ar.alloc(512, BF16)
    bufs["tri_up4_sb"] = ar.alloc(512, BF16)
    psum_cms = [nc.psum_tensor(f"pb{i}", [128, 512], F32) for i in range(8)]
    psums = [cm.__enter__() for cm in psum_cms]
    bufs["psum"] = [p[:, :] for p in psums]
    stage_base = ar.off
    hi = stage_base
    for s in stages:
        ar.off = stage_base
        STAGES[s][0](nc, ar, bufs)
        hi = max(hi, ar.off)
    ar.off = hi

    sems = {}
    sem_cms = {}
    for sn in SEM_NAMES:
        cm = nc.semaphore(sn)
        sem_cms[sn] = cm
        sems[sn] = cm.__enter__()

    with nc.Block() as blk:
        def body(eng_name):
            def run(eng):
                ctx = Ctx(sems, eng_name, eng)
                ctx.op("gpsimd", lambda e: e.dma_start(out=bufs["cconst"], in_=io["cconst"]), inc="gld", inc_n=16)
                if "ident" in io:
                    ctx.op("gpsimd", lambda e: e.dma_start(out=bufs["ident_sb"], in_=io["ident"]), inc="gld", inc_n=16)
                    ctx.op("sync", lambda e: e.dma_start(out=bufs["ident_f"], in_=io["ident"]), inc="qld", inc_n=16)
                if "ones_row" in io:
                    ctx.op("gpsimd", lambda e: e.dma_start(out=bufs["ones_sb"], in_=io["ones_row"]), inc="gld", inc_n=16)
                if "tri_lo" in io:
                    ctx.op("gpsimd", lambda e: e.dma_start(out=bufs["tri_lo_sb"], in_=io["tri_lo"]), inc="gld", inc_n=16)
                if "tri_up" in io:
                    ctx.op("gpsimd", lambda e: e.dma_start(out=bufs["tri_up_sb"], in_=io["tri_up"]), inc="gld", inc_n=16)
                if "tri_lo4" in io:
                    ctx.op("gpsimd", lambda e: e.dma_start(out=bufs["tri_lo4_sb"], in_=io["tri_lo4"]), inc="gld", inc_n=16)
                if "tri_up4" in io:
                    ctx.op("gpsimd", lambda e: e.dma_start(out=bufs["tri_up4_sb"], in_=io["tri_up4"]), inc="gld", inc_n=16)
                for name in seed_pairs:
                    ctx.op("sync", lambda e, name=name: e.dma_start(
                        out=io[name], in_=io[name + "__seed"]), inc="qld", inc_n=16)
                ctx.wait("tensor", ["gld", "qld"])
                ctx.wait("vector", ["gld", "qld"])
                ctx.wait("scalar", ["gld", "qld"])
                ctx.wait("sync", ["gld", "qld"])
                ctx.wait("gpsimd", ["gld", "qld"])
                for s in stages:
                    STAGES[s][1](ctx, bufs, io)
                    ctx.barrier()
                all_dma = ["qst", "gld", "qld"] + [f"d{i}" for i in range(20)]
                ctx.wait("sync", all_dma)
                ctx.barrier()
            return run
        blk.tensor(body("tensor"))
        blk.vector(body("vector"))
        blk.scalar(body("scalar"))
        blk.gpsimd(body("gpsimd"))
        blk.sync(body("sync"))

    return nc, in_names, out_names


GROUPS = [["A", "B", "C", "DE", "F", "G"]]


def kernel(**inputs):
    consts = build_consts()
    rename = {"wq1": "L1_attn_wq", "wk1": "L1_attn_wk", "wv1": "L1_attn_wv",
              "wo1": "L1_attn_wo", "bo1": "L1_attn_bo", "omega": "L1_attn_omega",
              "W1q": "L1_ffn_W1", "b1q": "L1_ffn_b1", "W2q": "L1_ffn_W2", "b2q": "L1_ffn_b2"}
    pool = {}
    for k, v in inputs.items():
        pool[k] = np.ascontiguousarray(v)
    for k in consts:
        pool[k] = consts[k]

    def fetch(name, core):
        base = name[:-6] if name.endswith("__seed") else name
        if base == "x":
            return np.ascontiguousarray(pool["L0_attn_x"][core])
        if base in consts:
            return consts[base]
        if base in rename:
            return pool[rename[base]]
        if base in ("Wq", "bq", "Wk", "bk", "Wvd", "bvd", "Wvu", "bvu"):
            return pool["L0_attn_" + base]
        if base in ("Wqkv", "q_scale", "k_scale", "Wo", "ln_g", "ln_b", "W1", "b1", "W2", "b2"):
            return pool["L0_ffn_" + base]
        return pool["__scratch__" + base][core]

    n_cores = 8
    for gi, stages in enumerate(GROUPS):
        needed = {"out"}
        for later in GROUPS[gi + 1:]:
            for s in later:
                r, w, rw = STAGE_IO[s]
                needed.update(r)
                needed.update(rw)
        nc, in_names, out_names = build_group(stages, needed)
        in_maps = []
        for core in range(n_cores):
            m = {name: fetch(name, core) for name in in_names}
            in_maps.append(m)
        res = run_bass_kernel_spmd(nc, in_maps, core_ids=list(range(n_cores)))
        for name in out_names:
            pool["__scratch__" + name] = np.stack([res.results[c][name] for c in range(n_cores)])

    return pool["__scratch__out"]


# version 12
# speedup vs baseline: 1.0425x; 1.0240x over previous
import math
import numpy as np
import ml_dtypes
import concourse.bass as bass
import concourse.mybir as mybir
from concourse.bass_utils import run_bass_kernel_spmd

F32 = mybir.dt.float32
BF16 = mybir.dt.bfloat16
AF = mybir.ActivationFunctionType
ALU = mybir.AluOpType
AX = mybir.AxisListType

N = 4096
D = 1024
NT = N // 128
NCH = D // 128

ENGS = ("tensor", "vector", "scalar", "gpsimd", "sync")

C_ZERO, C_ONE, C_NEGHALF = 0, 1, 2
NCONST = 4


def build_consts():
    c = {}
    c["ident"] = np.eye(128, dtype=np.float32)
    c["ones_row"] = np.ones((1, 512), np.float32)
    cc = np.zeros((128, NCONST), np.float32)
    cc[:, C_ONE] = 1.0
    cc[:, C_NEGHALF] = -0.5
    c["cconst"] = cc
    p = np.arange(128)[:, None]
    f = np.arange(128)[None, :]
    c["tri_lo"] = (p >= f).astype(np.float32)
    c["tri_up"] = (p <= f).astype(np.float32)
    c["tri_lo4"] = np.tile((p >= f), (1, 4)).astype(np.float32)
    c["tri_up4"] = np.tile((p <= f), (1, 4)).astype(np.float32)
    w17 = np.zeros((1, 1040), np.float32)
    w17[0, 64::65] = 1.0
    c["wvu17row"] = w17
    f2 = np.arange(512)[None, :]
    for k, off in enumerate((0, 128, 256, 384)):
        c[f"amask{k}"] = ((p + off) >= f2).astype(np.float32)
    DH, WIN = 64, 256
    half = np.arange(0, DH, 2, dtype=np.float64)
    inv_freq = 1.0 / 10000.0 ** (half / DH)
    t = np.arange(N, dtype=np.float64)
    freqs = t[:, None] * inv_freq[None, :]
    freqs = np.concatenate([freqs, freqs], -1)
    base_scale = (half + 0.4 * DH) / (1.4 * DH)
    power = (t - N // 2) / (WIN // 2)
    scale = base_scale[None, :] ** power[:, None]
    scale = np.concatenate([scale, scale], -1)
    cos, sin = np.cos(freqs), np.sin(freqs)
    sgn = np.concatenate([-np.ones(32), np.ones(32)])[None, :]
    c["rot_all"] = np.concatenate([
        cos * scale * 8.0, sin * scale * 8.0 * sgn,
        cos / scale, sin / scale * sgn], axis=1).astype(np.float32)
    return c


class Arena:

    def __init__(self, nc, nbytes=205 * 1024):
        self.nbytes = nbytes
        self._cm = nc.sbuf_tensor("arena", [128, nbytes], mybir.dt.uint8)
        self.t = self._cm.__enter__()
        self.off = 0
        self.marks = []

    def alloc(self, cols, dtype, parts=128):
        esz = 2 if dtype == BF16 else 4
        nb = cols * esz
        off = (self.off + 63) & ~63
        assert off + nb <= self.nbytes, f"arena overflow at {off}+{nb}"
        self.off = off + nb
        ap = self.t[:, off:off + nb].bitcast(dtype)
        if parts != 128:
            ap = ap[0:parts, :]
        return ap

    def allocN(self, n, cols, dtype, parts=128):
        return [self.alloc(cols, dtype, parts) for _ in range(n)]

    def push(self):
        self.marks.append(self.off)

    def pop(self):
        self.off = self.marks.pop()


class Ctx:
    def __init__(self, sems, eng_name, eng):
        self.sems = sems
        self.cnt = {k: 0 for k in sems}
        self.eng_name = eng_name
        self.eng = eng
        self._waited = {k: 0 for k in sems}

    def op(self, engine, fn, waits=(), inc=None, inc_n=1, drain=False):
        resolved = []
        for w in waits:
            if isinstance(w, str):
                resolved.append((w, self.cnt[w]))
            else:
                resolved.append(w)
        if self.eng_name == engine:
            for s, v in resolved:
                if v > 0 and self._waited[s] < v:
                    self.eng.wait_ge(self.sems[s], v)
                    self._waited[s] = v
            ins = fn(self.eng)
            if inc is not None:
                ins.then_inc(self.sems[inc], inc_n)
            if drain:
                self.eng.drain()
        if inc is not None:
            self.cnt[inc] += inc_n
            return self.cnt[inc]
        return None

    def wait(self, engine, waits):
        resolved = [(w, self.cnt[w]) if isinstance(w, str) else w for w in waits]
        if self.eng_name == engine:
            for s, v in resolved:
                if v > 0 and self._waited[s] < v:
                    self.eng.wait_ge(self.sems[s], v)
                    self._waited[s] = v

    def barrier(self):
        dma_sems = [k for k in self.sems if k in ("gld", "qld", "qst") or k.startswith("d")]
        if self.eng_name is not None:
            for s in dma_sems:
                v = self.cnt[s]
                if v > 0 and self._waited[s] < v:
                    self.eng.wait_ge(self.sems[s], v)
                    self._waited[s] = v
        self.cnt["bar"] += 5
        target = self.cnt["bar"]
        if self.eng_name is not None:
            self.eng.sem_inc(self.sems["bar"], 1)
            self.eng.wait_ge(self.sems["bar"], target)
            self._waited["bar"] = target


SEM_NAMES = ("gld", "qld", "qst", "sT", "sT2", "sV", "sV2", "sC", "sC2", "sG", "bar") + tuple(f"d{i}" for i in range(20))

def dram_table():
    return {
        "x": ([N, D], F32),
        "Wq": ([16, 1024, 64], F32), "bq": ([16, 64], F32),
        "Wk": ([16, 1024, 64], F32), "bk": ([16, 64], F32),
        "Wvd": ([16, 1024, 16], F32), "bvd": ([16, 16], F32),
        "Wvu": ([16, 16, 64], F32), "bvu": ([16, 64], F32),
        "Wqkv": ([6, 1024, 3072], F32), "q_scale": ([6, 64], F32), "k_scale": ([6, 64], F32),
        "Wo": ([6, 1024, 1024], F32), "ln_g": ([6, 1024], F32), "ln_b": ([6, 1024], F32),
        "W1": ([6, 1024, 5460], F32), "b1": ([6, 5460], F32),
        "W2": ([6, 2730, 1024], F32), "b2": ([6, 1024], F32),
        "omega": ([64, 64], F32),
        "wq1": ([1024, 1024], F32), "wk1": ([1024, 1024], F32),
        "wv1": ([1024, 1024], F32), "wo1": ([1024, 1024], F32), "bo1": ([1024], F32),
        "W1q": ([4096, 1024], F32), "b1q": ([4096], F32),
        "W2q": ([1024, 4096], F32), "b2q": ([1024], F32),
        "ident": ([128, 128], F32), "ones_row": ([1, 512], F32), "cconst": ([128, NCONST], F32),
        "tri_lo": ([128, 128], F32), "tri_up": ([128, 128], F32),
        "tri_lo4": ([128, 512], F32), "tri_up4": ([128, 512], F32),
        "wvu17row": ([1, 1040], F32),
        "amask0": ([128, 512], F32), "amask1": ([128, 512], F32),
        "amask2": ([128, 512], F32), "amask3": ([128, 512], F32),
        "rot_all": ([N, 256], F32),
        "r0": ([N, D], F32), "xnT": ([D, N], BF16),
        "s0": ([N, D], F32),
        "x1": ([N, D], F32), "xn1T": ([D, N], BF16),
        "s1": ([N, D], F32),
        "w1tT": ([D, 4096], BF16), "w2tT": ([4096, D], BF16),
        "escr1": ([1, 1], F32), "escr2": ([1, 1], F32),
        "out": ([N, D], F32),
    }


STAGE_IO = {
    "A": (["x", "cconst", "ident"], ["r0", "xnT"], []),
    "B": (["xnT", "Wq", "bq", "Wk", "bk", "Wvd", "bvd", "Wvu", "bvu", "ident", "wvu17row",
           "cconst", "ones_row", "amask0", "amask1", "amask2", "amask3"], ["s0"], []),
    "C": (["Wqkv", "q_scale", "k_scale", "Wo", "ln_g", "ln_b", "W1", "b1", "W2", "b2",
           "cconst", "ones_row", "ident", "tri_lo", "tri_up", "tri_lo4", "tri_up4",
           "rot_all"], [], ["r0"]),
    "D": (["x", "s0", "r0", "cconst", "ident"], ["x1", "xn1T"], []),
    "E": (["W1q", "W2q", "cconst", "ident", "ones_row"], ["w1tT", "w2tT", "escr1", "escr2"], []),
    "F": (["xn1T", "omega", "wq1", "wk1", "wv1", "wo1", "bo1",
           "cconst", "ones_row", "ident", "tri_up", "tri_up4"], ["s1"], []),
    "G": (["xn1T", "w1tT", "w2tT", "b1q", "b2q", "x1", "s1", "cconst", "ones_row"], ["out"], []),
}

CONST_NAMES = {"ident", "ones_row", "cconst", "tri_lo", "tri_up", "tri_lo4", "tri_up4",
               "amask0", "amask1", "amask2", "amask3", "rot_all", "wvu17row"}


def alloc_stageA(nc, ar, bufs):
    bufs["A_xa"] = ar.allocN(2, 1024, F32)
    bufs["A_xn"] = ar.allocN(2, 1024, F32)
    bufs["A_xnb"] = ar.allocN(2, 1024, BF16)
    bufs["A_xtv"] = ar.allocN(2, 1024, BF16)
    bufs["A_scr"] = ar.allocN(1, 1024, F32)
    bufs["A_st"] = ar.alloc(8, F32)


def emit_stageA(ctx, bufs, io):
    x, r0, xnT = io["x"], io["r0"], io["xnT"]
    xa, xn, xnb, xtv = bufs["A_xa"], bufs["A_xn"], bufs["A_xnb"], bufs["A_xtv"]
    st = bufs["A_st"]
    scr = bufs["A_scr"][0]
    cconst, psum = bufs["cconst"], bufs["psum"]
    xnT_r = xnT.rearrange("(c p) n -> p c n", p=128)

    ld = [0] * NT; vxn = [0] * NT; ccast = [0] * NT; ttr = [0] * NT
    vev = [0] * NT; st1 = [0] * NT; st2 = [0] * NT

    for i in range(NT):
        b = i % 2
        w = [("sV", vxn[i - 2])] if i >= 2 else []
        ld[i] = ctx.op("sync", lambda e, i=i, b=b: e.dma_start(
            out=xa[b], in_=x[i * 128:(i + 1) * 128, :]), waits=w, inc=f"d{b}", inc_n=16)

        ctx.op("vector", lambda e, b=b: e.tensor_scalar(
            scr, xa[b], 1.0, 0.0, ALU.mult, ALU.add, accum_out=st[:, 0:1]),
            waits=[(f"d{b}", ld[i])], drain=True)
        ctx.op("vector", lambda e, b=b: e.scalar_tensor_tensor(
            scr, xa[b], 1.0, xa[b], ALU.mult, ALU.mult, accum_out=st[:, 1:2]), drain=True)
        ctx.op("vector", lambda e: e.tensor_scalar(st[:, 2:3], st[:, 0:1], 1.0 / D, None, ALU.mult), drain=True)
        ctx.op("vector", lambda e: e.tensor_tensor(st[:, 3:4], st[:, 2:3], st[:, 2:3], ALU.mult), drain=True)
        ctx.op("vector", lambda e: e.tensor_scalar(st[:, 4:5], st[:, 1:2], 1.0 / D, None, ALU.mult), drain=True)
        ctx.op("vector", lambda e: e.tensor_tensor(st[:, 4:5], st[:, 4:5], st[:, 3:4], ALU.subtract), drain=True)
        ctx.op("vector", lambda e: e.tensor_scalar(st[:, 4:5], st[:, 4:5], 1e-5, None, ALU.add),
               inc="sV2", drain=True)
        ctx.op("scalar", lambda e: e.activation(st[:, 5:6], st[:, 4:5], AF.Ln,
               bias=cconst[:, C_ZERO:C_ZERO + 1], scale=cconst[:, C_ONE:C_ONE + 1]),
               waits=["sV2"], drain=True)
        ctx.op("scalar", lambda e: e.activation(st[:, 6:7], st[:, 5:6], AF.Exp,
               bias=cconst[:, C_ZERO:C_ZERO + 1], scale=cconst[:, C_NEGHALF:C_NEGHALF + 1]),
               inc="sC", drain=True)
        wv = [("sC", ctx.cnt["sC"])]
        if i >= 2:
            wv.append((f"d{2 + b}", st1[i - 2]))
        vxn[i] = ctx.op("vector", lambda e, b=b: e.tensor_scalar(
            xn[b], xa[b], st[:, 2:3], st[:, 6:7], ALU.subtract, ALU.mult),
            waits=wv, inc="sV", drain=True)
        wc = [("sV", vxn[i])]
        if i >= 2:
            wc.append(("sT", ttr[i - 2]))
        ccast[i] = ctx.op("scalar", lambda e, b=b: e.copy(xnb[b], xn[b]),
                          waits=wc, inc="sC2")
        st1[i] = ctx.op("sync", lambda e, i=i, b=b: e.dma_start(
            out=r0[i * 128:(i + 1) * 128, :], in_=xn[b]), waits=[("sV", vxn[i])],
            inc=f"d{2 + b}", inc_n=16)

        pb = psum[4 + b][:, :].bitcast(BF16)
        for c in range(NCH):
            wt = [("sC2", ccast[i])]
            if i >= 2 and c == 0:
                wt.append(("sV", vev[i - 2]))
            ctx.op("tensor", lambda e, b=b, c=c, pb=pb: e.transpose(
                pb[:, c * 128:(c + 1) * 128], xnb[b][:, c * 128:(c + 1) * 128], bufs["ident_sb"]),
                waits=wt, inc=("sT" if c == NCH - 1 else None))
        ttr[i] = ctx.cnt["sT"]
        we = [("sT", ttr[i])]
        if i >= 2:
            we.append((f"d{15 + b}", st2[i - 2]))
        vev[i] = ctx.op("vector", lambda e, b=b, pb=pb: e.tensor_copy(xtv[b], pb[:, 0:1024]),
                        waits=we, inc="sV", drain=True)
        st2[i] = ctx.op("sync", lambda e, i=i, b=b: e.dma_start(
            out=xnT_r[:, :, i * 128:(i + 1) * 128],
            in_=xtv[b].rearrange("p (c n) -> p c n", c=8)),
            waits=[("sV", vev[i])], inc=f"d{15 + b}", inc_n=16)


def alloc_stageB(nc, ar, bufs):
    bufs["B_xnT"] = ar.alloc(8 * 4096, BF16)
    bufs["B_vdw"] = ar.alloc(8 * 256, BF16)
    bufs["B_vdo"] = ar.alloc(32 * 272, BF16)
    bufs["B_bvdb"] = ar.alloc(256, F32)
    bufs["B_bvub"] = ar.alloc(1024, F32)
    bufs["B_bqki"] = ar.alloc(128, F32)
    bufs["B_bqk"] = ar.alloc(16, F32)
    bufs["B_qkw"] = ar.allocN(2, 2 * 8 * 128, BF16)
    bufs["B_qT"] = ar.allocN(2, 4096, BF16)
    bufs["B_kT"] = ar.allocN(2, 4096, BF16)
    bufs["B_pt"] = ar.allocN(4, 512, BF16)
    bufs["B_am"] = [ar.alloc(512, BF16) for _ in range(4)]
    bufs["B_avs"] = ar.allocN(4, 512, BF16)
    bufs["B_wvu"] = ar.alloc(1040, BF16)
    bufs["B_sst"] = ar.allocN(2, 512, F32)
    bufs["B_rcp"] = ar.alloc(8, F32)


def emit_stageB(ctx, bufs, io):
    xnT = bufs["B_xnT"]
    psum = bufs["psum"]
    cc = bufs["cconst"]
    identb = bufs["ident_sb"]

    xnT_r = io["xnT"].rearrange("(c p) n -> p c n", p=128)
    ctx.op("gpsimd", lambda e: e.dma_start(
        out=xnT.rearrange("p (c n) -> p c n", c=8), in_=xnT_r), inc="gld", inc_n=16)
    for h in range(16):
        ctx.op("gpsimd", lambda e, h=h: e.dma_start(
            out=bufs["B_vdw"].rearrange("p (c x) -> p c x", c=8)[:, :, 16 * h:16 * h + 16],
            in_=io["Wvd"][h].rearrange("(c p) r -> p c r", p=128)), inc="gld", inc_n=16)
    for k in range(4):
        ctx.op("gpsimd", lambda e, k=k: e.dma_start(
            out=bufs["B_am"][k], in_=io["amask" + str(k)]), inc="gld", inc_n=16)
    wvu_z = ctx.op("vector", lambda e: e.memset(bufs["B_wvu"][0:32, :], 0.0), inc="sV", drain=True)
    for h in range(16):
        ctx.op("gpsimd", lambda e, h=h: e.dma_start(
            out=bufs["B_wvu"][0:16, 65 * h:65 * h + 64], in_=io["Wvu"][h]),
            waits=([("sV", wvu_z)] if h == 0 else []), inc="gld", inc_n=16)
    ctx.op("gpsimd", lambda e: e.dma_start(
        out=bufs["B_wvu"][16:17, :], in_=io["wvu17row"]), inc="gld", inc_n=16)
    ld_pre = ctx.cnt["gld"]
    wvu_ones = 0
    ctx.op("sync", lambda e: e.dma_start(
        out=bufs["B_bvdb"], in_=io["bvd"].rearrange("h r -> (h r)").unsqueeze(0).partition_broadcast(128).squeeze(1)),
        inc="qld", inc_n=16)
    ctx.op("sync", lambda e: e.dma_start(
        out=bufs["B_bvub"], in_=io["bvu"].rearrange("h d -> (h d)").unsqueeze(0).partition_broadcast(128).squeeze(1)),
        inc="qld", inc_n=16)
    ctx.op("sync", lambda e: e.dma_start(out=bufs["B_bqki"][0:8, :],
        in_=io["bq"].rearrange("(j g) d -> j (g d)", g=2)), inc="qld", inc_n=16)
    ld_bqk = ctx.op("sync", lambda e: e.dma_start(out=bufs["B_bqki"][8:16, :],
        in_=io["bk"].rearrange("(j g) d -> j (g d)", g=2)), inc="qld", inc_n=16)
    ld_sync_all = ctx.cnt["qld"]

    ctx.op("tensor", lambda e: e.transpose(
        psum[2][:, 0:16], bufs["B_bqki"][0:16, :], bufs["ident_f"][0:16, 0:16]),
        waits=[("qld", ld_bqk)], inc="sT")
    _t = ctx.cnt["sT"]
    ctx.op("vector", lambda e: e.tensor_copy(bufs["B_bqk"], psum[2][:, 0:16]),
           waits=[("sT", _t)], inc="sV", drain=True)
    bqk_ready = ctx.cnt["sV"]

    vd_ev = [0] * NT
    for t in range(NT):
        bk = t % 2
        for c in range(NCH):
            w = [("gld", ld_pre)]
            if t >= 2 and c == 0:
                w.append(("sV", vd_ev[t - 2]))
            ctx.op("tensor", lambda e, t=t, c=c, bk=bk: e.matmul(
                psum[bk][:, 0:256], xnT[:, 4096 * c + 128 * t: 4096 * c + 128 * t + 128],
                bufs["B_vdw"][:, 256 * c:256 * c + 256],
                start=(c == 0), stop=(c == NCH - 1)), waits=w,
                inc=("sT" if c == NCH - 1 else None))
        mm_tok = ctx.cnt["sT"]
        ctx.op("vector", lambda e, t=t: e.memset(
            bufs["B_vdo"][:, 272 * t:272 * t + 272].rearrange("p (h x) -> p h x", h=16)[:, :, 16:17], 1.0),
            waits=[("sT", mm_tok), ("qld", ld_sync_all)], drain=True)
        vd_ev[t] = ctx.op("vector", lambda e, t=t, bk=bk: e.scalar_tensor_tensor(
            bufs["B_vdo"][:, 272 * t:272 * t + 272].rearrange("p (h x) -> p h x", h=16)[:, :, 0:16],
            psum[bk][:, 0:256].rearrange("p (h r) -> p h r", h=16),
            1.0, bufs["B_bvdb"].rearrange("p (h r) -> p h r", h=16),
            ALU.mult, ALU.add), inc="sV", drain=True)
    vd_done = ctx.cnt["sV"]

    qkw_tok = [0, 0]
    attn_done = [0, 0]
    proj_mm_done = [0, 0]

    def load_qkw(j):
        b = j % 2
        w = [("sT", proj_mm_done[b])] if j >= 2 else []
        for hh in range(2):
            h = 2 * j + hh
            ctx.op("gpsimd", lambda e, h=h, hh=hh, b=b: e.dma_start(
                out=bufs["B_qkw"][b].rearrange("p (g c x) -> p g c x", g=2, c=8)[:, 0, :, 64 * hh:64 * hh + 64],
                in_=io["Wq"][h].rearrange("(c p) d -> p c d", p=128)),
                waits=(w if hh == 0 else []), inc=f"d{10 + b}", inc_n=16)
            qkw_tok[b] = ctx.op("gpsimd", lambda e, h=h, hh=hh, b=b: e.dma_start(
                out=bufs["B_qkw"][b].rearrange("p (g c x) -> p g c x", g=2, c=8)[:, 1, :, 64 * hh:64 * hh + 64],
                in_=io["Wk"][h].rearrange("(c p) d -> p c d", p=128)), inc=f"d{10 + b}", inc_n=16)

    load_qkw(0)
    exp_tok = {}
    pv_tok = {}
    norm_tok = {}
    proj_ev = {}
    sst_st = [0, 0]
    out_ev = [0, 0]
    attvd_wr_guard = 0
    avt_ev_done = 0
    bank_exp = {0: 0, 1: 0, 2: 0, 3: 0}
    pt_guard = {0: 0, 1: 0, 2: 0, 3: 0}
    avs_ev = {}
    out_norm = {}

    for j in range(8):
        b = j % 2
        if j < 7:
            load_qkw(j + 1)
        for qk in range(2):
            for s in range(8):
                pbank = 6 + (s % 2)
                for c in range(NCH):
                    w = []
                    if c == 0:
                        w.append((f"d{10 + b}", qkw_tok[b]))
                        if s >= 2:
                            w.append(("sV", proj_ev[(j, qk, s - 2)]))
                        elif qk == 1:
                            w.append(("sV", proj_ev[(j, 0, 6 + s)]))
                        elif j > 0:
                            w.append(("sV", proj_ev[(j - 1, 1, 6 + s)]))
                        if j > 0 and qk == 0 and s < 2 and out_norm.get((s, 7), 0) >= 0 and out_norm.get((s % 2, 7), 0) > 0:
                            w.append(("sV", out_norm[(s % 2, 7)]))
                        if s == 0 and qk == 0 and j >= 2:
                            w.append(("sT", attn_done[b]))
                    ctx.op("tensor", lambda e, qk=qk, s=s, c=c, b=b, pbank=pbank: e.matmul(
                        psum[pbank][:, :],
                        bufs["B_qkw"][b][:, 1024 * qk + 128 * c:1024 * qk + 128 * c + 128],
                        xnT[:, 4096 * c + 512 * s:4096 * c + 512 * s + 512],
                        start=(c == 0), stop=(c == NCH - 1)), waits=w,
                        inc=("sT" if c == NCH - 1 else None))
                mm_tok = ctx.cnt["sT"]
                if qk == 1 and s == 7:
                    proj_mm_done[b] = mm_tok
                wv = [("sT", mm_tok), ("sV", bqk_ready)]
                if j >= 2 and qk == 0 and s == 0:
                    wv.append(("sT", attn_done[b]))
                if qk == 0:
                    proj_ev[(j, 0, s)] = ctx.op("vector", lambda e, s=s, b=b, j=j, pbank=pbank: e.tensor_scalar(
                        bufs["B_qT"][b][:, 512 * s:512 * s + 512],
                        psum[pbank][:, :], bufs["B_bqk"][:, j:j + 1], 1.0 / 32,
                        ALU.add, ALU.mult), waits=wv, inc="sV", drain=True)
                else:
                    proj_ev[(j, 1, s)] = ctx.op("vector", lambda e, s=s, b=b, j=j, pbank=pbank: e.tensor_scalar(
                        bufs["B_kT"][b][:, 512 * s:512 * s + 512],
                        psum[pbank][:, :], bufs["B_bqk"][:, 8 + j:8 + j + 1], None,
                        ALU.add), waits=wv, inc="sV", drain=True)
        proj_all_ev = ctx.cnt["sV"]

        for s in range(8):
            c0 = 4 * s
            nchunks = 32 - c0
            for ci in range(nchunks):
                c = c0 + ci
                for hh in range(2):
                    base = 64 * hh
                    h = 2 * j + hh
                    obank = 4 + hh
                    sbank = (2 * ci + hh) % 4
                    ptb = sbank
                    w = [("sV", proj_all_ev)]
                    if bank_exp[sbank] > 0:
                        w.append(("sC", bank_exp[sbank]))
                    if ci == 0:
                        if s >= 1:
                            w.append(("sV", avs_ev[(hh, s - 1)]))
                        elif j > 0:
                            w.append(("sV", avs_ev[(hh, 7)]))
                        if j == 0 and s == 0:
                            w.append(("sV", vd_done))
                    ctx.op("tensor", lambda e, s=s, c=c, b=b, base=base, sbank=sbank: e.matmul(
                        psum[sbank][:, :],
                        bufs["B_kT"][b][base:base + 64, 128 * c:128 * c + 128],
                        bufs["B_qT"][b][base:base + 64, 512 * s:512 * s + 512],
                        start=True, stop=True), waits=w, inc="sT")
                    st_tok_mm = ctx.cnt["sT"]
                    we = [("sT", st_tok_mm)]
                    if pt_guard[ptb]:
                        we.append(("sT", pt_guard[ptb]))
                    exp_tok[(hh, s, ci)] = ctx.op("scalar", lambda e, ptb=ptb, sbank=sbank: e.activation(
                        bufs["B_pt"][ptb], psum[sbank][:, :], AF.Exp,
                        bias=cc[:, C_ZERO:C_ZERO + 1], scale=cc[:, C_ONE:C_ONE + 1]),
                        waits=we, inc="sC")
                    bank_exp[sbank] = exp_tok[(hh, s, ci)]
                    pvw, pvt = "sC", exp_tok[(hh, s, ci)]
                    if ci < 4:
                        ctx.op("vector", lambda e, ptb=ptb, ci=ci: e.tensor_tensor(
                            bufs["B_pt"][ptb], bufs["B_pt"][ptb], bufs["B_am"][ci], ALU.mult),
                            waits=[("sC", exp_tok[(hh, s, ci)])], inc="sV", drain=True)
                        pvw, pvt = "sV", ctx.cnt["sV"]
                    ctx.op("tensor", lambda e, c=c, h=h, ptb=ptb, obank=obank, ci=ci, nch=nchunks: e.matmul(
                        psum[obank][0:17, :],
                        bufs["B_vdo"][:, 272 * c + 17 * h:272 * c + 17 * h + 17],
                        bufs["B_pt"][ptb],
                        start=(ci == 0), stop=(ci == nch - 1)), waits=[(pvw, pvt)], inc="sT")
                    pt_guard[ptb] = ctx.cnt["sT"]
            for hh in range(2):
                base = 64 * hh
                h = 2 * j + hh
                obank = 4 + hh
                avb = 2 * (s % 2) + hh
                wv = [("sT", pt_guard[(2 * (nchunks - 1) + hh) % 4])]
                avs_ev[(hh, s)] = ctx.op("vector", lambda e, avb=avb, obank=obank: e.tensor_copy(
                    bufs["B_avs"][avb][0:17, :], psum[obank][0:17, :]), waits=wv, inc="sV", drain=True)
                ob = 6 + hh
                for t in range(4):
                    w2 = []
                    if t == 0:
                        w2.append(("sV", avs_ev[(hh, s)]))
                        key = (hh, s - 1) if s >= 1 else (hh, 7)
                        if key in out_norm and out_norm[key] > 0:
                            w2.append(("sV", out_norm[key]))
                    ctx.op("tensor", lambda e, t=t, h=h, avb=avb, ob=ob: e.matmul(
                        psum[ob][:, 128 * t:128 * t + 65],
                        bufs["B_avs"][avb][0:17, 128 * t:128 * t + 128],
                        bufs["B_wvu"][0:17, 65 * h:65 * h + 65],
                        start=(t == 0), stop=(t == 3)), waits=w2,
                        inc=("sT" if t == 3 else None))
                omm = ctx.cnt["sT"]
                wv3 = [("sT", omm)]
                if sst_st[s % 2] > 0 and hh == 0:
                    wv3.append((f"d{2 + s % 2}", sst_st[s % 2]))
                ctx.op("vector", lambda e, ob=ob: e.reciprocal(
                    bufs["B_rcp"][:, 0:4].unsqueeze(2),
                    psum[ob][:, :].rearrange("p (t x) -> p t x", t=4)[:, :, 64:65]),
                    waits=wv3, drain=True)
                ctx.op("vector", lambda e, s=s, hh=hh, ob=ob: e.scalar_tensor_tensor(
                    bufs["B_sst"][s % 2].rearrange("p (t x) -> p t x", t=4)[:, :, 64 * hh:64 * hh + 64],
                    psum[ob][:, :].rearrange("p (t x) -> p t x", t=4)[:, :, 0:64],
                    1.0,
                    bufs["B_rcp"][:, 0:4].unsqueeze(2).broadcast_to((128, 4, 64)),
                    ALU.mult, ALU.mult), drain=True)
                out_norm[(hh, s)] = ctx.op("vector", lambda e, hh=hh, s=s, j=j: e.scalar_tensor_tensor(
                    bufs["B_sst"][s % 2].rearrange("p (t x) -> p t x", t=4)[:, :, 64 * hh:64 * hh + 64],
                    bufs["B_sst"][s % 2].rearrange("p (t x) -> p t x", t=4)[:, :, 64 * hh:64 * hh + 64],
                    1.0,
                    bufs["B_bvub"][:, 128 * j + 64 * hh:128 * j + 64 * hh + 64].unsqueeze(1).broadcast_to((128, 4, 64)),
                    ALU.mult, ALU.add), inc="sV", drain=True)
            sst_st[s % 2] = ctx.op("sync", lambda e, s=s, j=j: e.dma_start(
                out=io["s0"][512 * s:512 * s + 512, 128 * j:128 * j + 128].rearrange(
                    "(t p) c -> p t c", p=128),
                in_=bufs["B_sst"][s % 2].rearrange("p (t x) -> p t x", t=4)),
                waits=[("sV", out_norm[(1, s)])], inc=f"d{2 + s % 2}", inc_n=16)
        attn_done[b] = ctx.cnt["sT"]



def alloc_stageC(nc, ar, bufs):
    bufs["C_tbl"] = ar.allocN(2, 256, F32)
    ar.push()
    bufs["C_wqkv"] = ar.alloc(8 * 3072, BF16)
    bufs["C_wo"] = ar.alloc(8 * 1024, BF16)
    bufs["C_qsb"] = ar.alloc(4 * 64, F32)
    bufs["C_ra"] = ar.allocN(2, 1024, F32)
    bufs["C_rb"] = ar.alloc(1024, BF16)
    bufs["C_rT"] = ar.allocN(2, 8 * 129, BF16)
    bufs["C_qkv"] = ar.allocN(2, 3072, BF16)
    bufs["C_scr"] = ar.alloc(2048, BF16)
    bufs["C_nrm"] = ar.alloc(48, F32)
    bufs["C_rn"] = ar.alloc(32, F32)
    bufs["C_qh"] = ar.alloc(2048, BF16)
    bufs["C_qsw"] = ar.alloc(2048, BF16)
    bufs["C_qrot"] = ar.alloc(2048, BF16)
    bufs["C_AB"] = ar.alloc(256, F32)
    bufs["C_qT"] = ar.allocN(2, 1024, BF16)
    bufs["C_kTr"] = ar.alloc(4 * 1024, BF16)
    bufs["C_vr"] = ar.alloc(4 * 1040, BF16)
    bufs["C_pt"] = ar.allocN(4, 512, BF16)
    bufs["C_ocb"] = ar.alloc(1024, BF16)
    bufs["C_ocT"] = ar.alloc(1024, BF16)
    bufs["C_rcp"] = ar.alloc(4, F32)
    bufs["C_rout"] = ar.allocN(2, 1024, F32)
    attn_hi = ar.off
    ar.pop()
    ar.push()
    bufs["C_hT"] = ar.alloc(22 * 2048, BF16)
    bufs["C_U"] = ar.alloc(22 * 1024, BF16)
    bufs["C_w1a"] = ar.allocN(2, 8 * 128, BF16)
    bufs["C_w1g"] = ar.allocN(2, 8 * 128, BF16)
    bufs["C_b1row"] = ar.alloc(5504, BF16)
    bufs["C_b2row"] = ar.alloc(1024, BF16)
    bufs["C_gl"] = ar.allocN(2, 512, F32)
    bufs["C_xs"] = ar.allocN(2, 1024, F32)
    bufs["C_xn"] = ar.alloc(1024, F32)
    bufs["C_xnb"] = ar.alloc(1024, BF16)
    bufs["C_rr"] = ar.allocN(2, 1024, F32)
    bufs["C_ro2"] = ar.allocN(2, 1024, F32)
    bufs["C_st"] = ar.alloc(16, F32)
    bufs["C_gb"] = ar.alloc(1024, F32)
    bufs["C_bb"] = ar.alloc(1024, F32)
    bufs["C_bnd"] = ar.alloc(512, F32, parts=1)
    ffn_hi = ar.off
    ar.pop()
    ar.off = max(attn_hi, ffn_hi)


C_LAYERS = 6
C_ATTN = True
C_FFN = True
C_CUT = 99
C_SUB = 99

def emit_stageC(ctx, bufs, io):
    psum = bufs["psum"]
    cc = bufs["cconst"]
    identb = bufs["ident_sb"]
    r0 = io["r0"]
    rot = io["rot_all"].rearrange("(t p) d -> p t d", p=128)

    tbl_tok = [0, 0]
    tbl_user = [0, 0]

    for l in range(C_LAYERS):
        ctx.op("gpsimd", lambda e, l=l: e.dma_start(
            out=bufs["C_wqkv"].rearrange("p (c f) -> p c f", c=8),
            in_=io["Wqkv"][l].rearrange("(c p) f -> p c f", p=128)), inc="d6", inc_n=16)
        ctx.op("gpsimd", lambda e, l=l: e.dma_start(
            out=bufs["C_wo"].rearrange("p (c f) -> p c f", c=8),
            in_=io["Wo"][l].rearrange("(c p) f -> p c f", p=128)), inc="d6", inc_n=16)
        qs_ap = io["q_scale"][l]
        ks_ap = io["k_scale"][l]
        ctx.op("sync", lambda e, qs_ap=qs_ap: e.dma_start(
            out=bufs["C_qsb"][:, 0:64], in_=qs_ap.unsqueeze(0).partition_broadcast(128).squeeze(1)), inc="d12", inc_n=16)
        ctx.op("sync", lambda e, qs_ap=qs_ap: e.dma_start(
            out=bufs["C_qsb"][:, 64:96], in_=qs_ap[32:64].unsqueeze(0).partition_broadcast(128).squeeze(1)), inc="d12", inc_n=16)
        ctx.op("sync", lambda e, qs_ap=qs_ap: e.dma_start(
            out=bufs["C_qsb"][:, 96:128], in_=qs_ap[0:32].unsqueeze(0).partition_broadcast(128).squeeze(1)), inc="d12", inc_n=16)
        ctx.op("sync", lambda e, ks_ap=ks_ap: e.dma_start(
            out=bufs["C_qsb"][:, 128:192], in_=ks_ap.unsqueeze(0).partition_broadcast(128).squeeze(1)), inc="d12", inc_n=16)
        ctx.op("sync", lambda e, ks_ap=ks_ap: e.dma_start(
            out=bufs["C_qsb"][:, 192:224], in_=ks_ap[32:64].unsqueeze(0).partition_broadcast(128).squeeze(1)), inc="d12", inc_n=16)
        ctx.op("sync", lambda e, ks_ap=ks_ap: e.dma_start(
            out=bufs["C_qsb"][:, 224:256], in_=ks_ap[0:32].unsqueeze(0).partition_broadcast(128).squeeze(1)), inc="d12", inc_n=16)
        wq_tok = ctx.cnt["d6"]
        wq_tok2 = ctx.cnt["d12"]

        ctx.op("vector", lambda e: e.memset(
            bufs["C_vr"].rearrange("p (s h x) -> p s h x", s=4, h=16)[:, :, :, 64:65], 1.0), drain=True)

        ld_tok = [0] * NT
        cast_tok = [0] * NT
        rT_ev = [0] * NT
        rtr_done = [0] * NT
        qkv_ev = {}
        qkv_done = [0] * NT
        prep_done = [0] * NT
        ring_wr = [0] * NT
        pv_last = [0] * NT
        attn_sT_end = [0] * NT
        res_ev = [0] * NT
        st_tok = [0, 0]
        oc_guard = [0]
        qT_ev = [0] * NT
        bank_exp = {4: 0, 5: 0}
        pt_guard = {0: 0, 1: 0, 2: 0, 3: 0}
        norm_toks = {g: 0 for g in range(4)}

        qh_toks = [0] * NT
        rv_toks = [0] * NT
        ocT_evs = [0] * NT
        qrot_toks = [0] * NT
        prep_dones = [0] * NT

        def front(i):
            b = i % 2
            w = [("sV", res_ev[i - 2])] if i >= 2 else []
            ld_tok[i] = ctx.op("sync", lambda e, i=i, b=b: e.dma_start(
                out=bufs["C_ra"][b], in_=r0[128 * i:128 * i + 128, :]), waits=w,
                inc=f"d{b}", inc_n=16)
            wtb = [("sV", tbl_user[b])] if tbl_tok[b] > 0 else []
            tbl_tok[b] = ctx.op("sync", lambda e, i=i, b=b: e.dma_start(
                out=bufs["C_tbl"][b], in_=rot[:, i, :]), waits=wtb, inc=f"d{8 + b}", inc_n=16)
            wc = [(f"d{b}", ld_tok[i])]
            if i >= 1:
                wc.append(("sT", rtr_done[i - 1]))
            cast_tok[i] = ctx.op("gpsimd", lambda e, b=b: e.tensor_copy(bufs["C_rb"], bufs["C_ra"][b]),
                                 waits=wc, inc="sG", drain=True)
            pb = psum[2 + b][:, :].bitcast(BF16)
            for c in range(NCH):
                wt = [("sG", cast_tok[i])]
                if c == 0:
                    if i >= 2:
                        wt.append(("sV", ocT_evs[i - 2]))
                        wt.append(("sT", qkv_done[i - 2]))
                ctx.op("tensor", lambda e, c=c, pb=pb: e.transpose(
                    pb[:, 128 * c:128 * c + 128], bufs["C_rb"][:, 128 * c:128 * c + 128], identb),
                    waits=wt, inc=("sT" if c == NCH - 1 else None))
            rtr_done[i] = ctx.cnt["sT"]
            rT_ev[i] = ctx.op("vector", lambda e, b=b, pb=pb: e.tensor_copy(
                bufs["C_rT"][b].rearrange("p (c x) -> p c x", c=8)[:, :, 1:129],
                pb[:, 0:1024].rearrange("p (c x) -> p c x", c=8)),
                waits=[("sT", rtr_done[i])], inc="sV", drain=True)
            if i == 0:
                gv = ctx.op("vector", lambda e, b=b: e.memset(
                    bufs["C_rT"][b].rearrange("p (c x) -> p c x", c=8)[:, 4:8, 0:1], 0.0),
                    waits=[("sV", rT_ev[i])], inc="sV", drain=True)
            else:
                gv = ctx.op("vector", lambda e, b=b: e.tensor_copy(
                    bufs["C_rT"][b].rearrange("p (c x) -> p c x", c=8)[:, 4:8, 0:1],
                    bufs["C_rT"][1 - b].rearrange("p (c x) -> p c x", c=8)[:, 4:8, 128:129]),
                    waits=[("sV", rT_ev[i])], inc="sV", drain=True)
            rT_ready = gv
            for g in range(6):
                gb = g % 2
                for c in range(NCH):
                    wt = []
                    if c == 0:
                        wt.append(("sV", rT_ready))
                        if g == 0 and i == 0:
                            wt.append(("d6", wq_tok))
                        if g >= 2:
                            wt.append(("sC2", qkv_ev[(i, g - 2)]))
                        elif i >= 1:
                            wt.append(("sC2", qkv_ev[(i - 1, 4 + g)]))
                    off = 1 if c < 4 else 0
                    ctx.op("tensor", lambda e, c=c, g=g, b=b, gb=gb, off=off: e.matmul(
                        psum[gb][:, :],
                        bufs["C_rT"][b][:, 129 * c + off:129 * c + off + 128],
                        bufs["C_wqkv"][:, 3072 * c + 512 * g:3072 * c + 512 * g + 512],
                        start=(c == 0), stop=(c == NCH - 1)), waits=wt,
                        inc=("sT" if c == NCH - 1 else None))
                mmt = ctx.cnt["sT"]
                if g == 5:
                    qkv_done[i] = mmt
                wsc = [("sT", mmt)]
                if g == 0 and i >= 2:
                    wsc.append(("sV", qh_toks[i - 2]))
                    wsc.append(("sG", rv_toks[i - 2]))
                qkv_ev[(i, g)] = ctx.op("scalar", lambda e, g=g, gb=gb, b=b: e.copy(
                    bufs["C_qkv"][b][:, 512 * g:512 * g + 512], psum[gb][:, :]),
                    waits=wsc, inc="sC2", drain=True)

        def mid(i):
            b = i % 2
            qkv = bufs["C_qkv"][b]
            ctx.op("vector", lambda e, qkv=qkv: e.tensor_tensor(
                bufs["C_scr"], qkv[:, 0:2048], qkv[:, 0:2048], ALU.mult),
                waits=[("sC2", qkv_ev[(i, 3)])], drain=True)
            ctx.op("vector", lambda e: e.reduce_sum(
                bufs["C_nrm"][:, 0:32], bufs["C_scr"].rearrange("p (h x) -> p h x", h=32), axis=AX.X),
                drain=True)
            ctx.op("vector", lambda e: e.tensor_scalar(
                bufs["C_nrm"][:, 0:32], bufs["C_nrm"][:, 0:32], 1e-24, None, ALU.max),
                inc="sV", drain=True)
            nrm_tok = ctx.cnt["sV"]
            ctx.op("scalar", lambda e: e.activation(bufs["C_rn"][:, 0:32], bufs["C_nrm"][:, 0:32],
                   AF.Ln, bias=cc[:, C_ZERO:C_ZERO + 1], scale=cc[:, C_ONE:C_ONE + 1]),
                   waits=[("sV", nrm_tok)], drain=True)
            ctx.op("scalar", lambda e: e.activation(bufs["C_rn"][:, 0:32], bufs["C_rn"][:, 0:32],
                   AF.Exp, bias=cc[:, C_ZERO:C_ZERO + 1], scale=cc[:, C_NEGHALF:C_NEGHALF + 1]),
                   inc="sC", drain=True)
            rn_tok = ctx.cnt["sC"]
            wv = [(f"d{8 + b}", tbl_tok[b]), ("d12", wq_tok2)]
            for z in range(4):
                ctx.op("vector", lambda e, z=z, b=b: e.tensor_tensor(
                    bufs["C_AB"][:, 64 * z:64 * z + 64], bufs["C_tbl"][b][:, 64 * z:64 * z + 64],
                    bufs["C_qsb"][:, 64 * z:64 * z + 64], ALU.mult),
                    waits=(wv if z == 0 else []), inc=("sV" if z == 3 else None), drain=True)
            tbl_user[b] = ctx.cnt["sV"]
            qh_toks[i] = ctx.op("vector", lambda e, qkv=qkv: e.tensor_tensor(
                bufs["C_qh"].rearrange("p (h x) -> p h x", h=32),
                qkv[:, 0:2048].rearrange("p (h x) -> p h x", h=32),
                bufs["C_rn"][:, 0:32].unsqueeze(2).broadcast_to((128, 32, 64)), ALU.mult),
                waits=[("sC", rn_tok)], inc="sV", drain=True)
            ctx.op("gpsimd", lambda e: e.tensor_copy(
                bufs["C_qsw"].rearrange("p (h g x) -> p h g x", h=32, g=2)[:, :, 0, :],
                bufs["C_qh"].rearrange("p (h g x) -> p h g x", h=32, g=2)[:, :, 1, :]),
                waits=[("sV", qh_toks[i])], drain=True)
            ctx.op("gpsimd", lambda e: e.tensor_copy(
                bufs["C_qsw"].rearrange("p (h g x) -> p h g x", h=32, g=2)[:, :, 1, :],
                bufs["C_qh"].rearrange("p (h g x) -> p h g x", h=32, g=2)[:, :, 0, :]),
                inc="sG", drain=True)
            qsw_tok = ctx.cnt["sG"]
            ctx.op("vector", lambda e: e.tensor_tensor(
                bufs["C_scr"].rearrange("p (g h x) -> p g h x", g=2, h=16),
                bufs["C_qh"].rearrange("p (g h x) -> p g h x", g=2, h=16),
                bufs["C_AB"].rearrange("p (g two x) -> p g two x", g=2, two=2)[:, :, 0:1, :].broadcast_to((128, 2, 16, 64)),
                ALU.mult), drain=True)
            ctx.op("vector", lambda e: e.tensor_tensor(
                bufs["C_qsw"].rearrange("p (g h x) -> p g h x", g=2, h=16),
                bufs["C_qsw"].rearrange("p (g h x) -> p g h x", g=2, h=16),
                bufs["C_AB"].rearrange("p (g two x) -> p g two x", g=2, two=2)[:, :, 1:2, :].broadcast_to((128, 2, 16, 64)),
                ALU.mult), waits=[("sG", qsw_tok)], drain=True)
            ctx.op("vector", lambda e: e.tensor_tensor(
                bufs["C_qrot"], bufs["C_scr"], bufs["C_qsw"], ALU.add), inc="sV", drain=True)
            qrot_toks[i] = ctx.cnt["sV"]
            sl = i % 4
            wv2 = [("sC2", qkv_ev[(i, 5)])]
            if i >= 2:
                wv2.append(("sT", pv_last[i - 2]))
            ctx.op("gpsimd", lambda e, sl=sl, qkv=qkv: e.tensor_copy(
                bufs["C_vr"][:, 1040 * sl:1040 * sl + 1040].rearrange("p (h x) -> p h x", h=16)[:, :, 0:64],
                qkv[:, 2048:3072].rearrange("p (h x) -> p h x", h=16)),
                waits=wv2, inc="sG", drain=True)
            rv_toks[i] = ctx.cnt["sG"]

        def back(i):
            b = i % 2
            sl = i % 4
            pb = psum[2 + b][:, :].bitcast(BF16)
            for c in range(8):
                wt = [("sV", qrot_toks[i])]
                if c == 0:
                    wt.append(("sV", rT_ev[i]))
                ctx.op("tensor", lambda e, c=c, pb=pb: e.transpose(
                    pb[:, 128 * c:128 * c + 128], bufs["C_qrot"][:, 0:1024][:, 128 * c:128 * c + 128], identb),
                    waits=wt, inc=("sT" if c == 7 else None))
            qtr = ctx.cnt["sT"]
            wv3 = [("sT", qtr)]
            if i >= 2:
                wv3.append(("sT", attn_sT_end[i - 2]))
            qT_ev[i] = ctx.op("vector", lambda e, b=b, pb=pb: e.tensor_copy(
                bufs["C_qT"][b], pb[:, 0:1024]), waits=wv3, inc="sV", drain=True)
            for c in range(8):
                wt = [("sV", qT_ev[i])]
                ctx.op("tensor", lambda e, c=c, pb=pb: e.transpose(
                    pb[:, 128 * c:128 * c + 128], bufs["C_qrot"][:, 1024:2048][:, 128 * c:128 * c + 128], identb),
                    waits=(wt if c == 0 else []), inc=("sT" if c == 7 else None))
            ktr = ctx.cnt["sT"]
            wv4 = [("sT", ktr)]
            if i >= 2:
                wv4.append(("sT", pv_last[i - 2]))
            ring_wr[i] = ctx.op("vector", lambda e, sl=sl, pb=pb: e.tensor_copy(
                bufs["C_kTr"][:, 1024 * sl:1024 * sl + 1024], pb[:, 0:1024]),
                waits=wv4, inc="sV", drain=True)
            prep_dones[i] = ctx.cnt["sV"]
            cs = [c for c in (i - 2, i - 1, i) if c >= 0]
            exp_toks = {}
            for g in range(4):
                obank = 6 + (g % 2)
                for ci, c in enumerate(cs):
                    idx = g * len(cs) + ci
                    sbank = 4 + (idx % 2)
                    ptb = idx % 4
                    for m in range(4):
                        h = g + 4 * m
                        hp, par = h // 2, h % 2
                        base = 64 * par
                        wt = []
                        if m == 0:
                            wt.append(("sV", prep_dones[i]))
                            wt.append(("sG", rv_toks[i]))
                            if bank_exp[sbank] > 0:
                                wt.append(("sC", bank_exp[sbank]))
                        ctx.op("tensor", lambda e, c=c, hp=hp, base=base, b=b, m=m, sbank=sbank: e.matmul(
                            psum[sbank][:, 128 * m:128 * m + 128],
                            bufs["C_kTr"][base:base + 64, 1024 * (c % 4) + 128 * hp:1024 * (c % 4) + 128 * hp + 128],
                            bufs["C_qT"][b][base:base + 64, 128 * hp:128 * hp + 128],
                            start=True, stop=True), waits=wt,
                            inc=("sT" if m == 3 else None))
                    smm = ctx.cnt["sT"]
                    we = [("sT", smm)]
                    if pt_guard[ptb]:
                        we.append(("sT", pt_guard[ptb]))
                    exp_toks[(g, ci)] = ctx.op("scalar", lambda e, ptb=ptb, sbank=sbank: e.activation(
                        bufs["C_pt"][ptb], psum[sbank][:, :], AF.Exp,
                        bias=cc[:, C_ZERO:C_ZERO + 1], scale=cc[:, C_ONE:C_ONE + 1]),
                        waits=we, inc="sC")
                    bank_exp[sbank] = exp_toks[(g, ci)]
                    pvw, pvt = "sC", exp_toks[(g, ci)]
                    if c == i - 2:
                        ctx.op("vector", lambda e, ptb=ptb: e.tensor_tensor(
                            bufs["C_pt"][ptb], bufs["C_pt"][ptb], bufs["tri_lo4_sb"], ALU.mult),
                            waits=[("sC", exp_toks[(g, ci)])], inc="sV", drain=True)
                        pvw, pvt = "sV", ctx.cnt["sV"]
                    elif c == i:
                        ctx.op("vector", lambda e, ptb=ptb: e.tensor_tensor(
                            bufs["C_pt"][ptb], bufs["C_pt"][ptb], bufs["tri_up4_sb"], ALU.mult),
                            waits=[("sC", exp_toks[(g, ci)])], inc="sV", drain=True)
                        pvw, pvt = "sV", ctx.cnt["sV"]
                    for m in range(4):
                        h = g + 4 * m
                        wpv = [(pvw, pvt)] if m == 0 else []
                        if m == 0 and ci == 0 and norm_toks[(g - 2) % 4] > 0:
                            wpv.append(("sV", norm_toks[(g - 2) % 4]))
                        if m == 0 and ci == 0 and obank == 7 and i >= 1:
                            wpv.append(("sV", res_ev[i - 1]))
                        ctx.op("tensor", lambda e, m=m, c=c, h=h, ptb=ptb, obank=obank, ci=ci, ncs=len(cs): e.matmul(
                            psum[obank][:, 128 * m:128 * m + 65],
                            bufs["C_pt"][ptb][:, 128 * m:128 * m + 128],
                            bufs["C_vr"][:, 1040 * (c % 4) + 65 * h:1040 * (c % 4) + 65 * h + 65],
                            start=(ci == 0 and m == 0), stop=(ci == ncs - 1 and m == 3)), waits=wpv,
                            inc=("sT" if m == 3 else None))
                    pt_guard[ptb] = ctx.cnt["sT"]
                pv_last[i] = ctx.cnt["sT"]
                wn = [("sT", pv_last[i])]
                if oc_guard[0]:
                    wn.append(("sT", oc_guard[0]))
                ctx.op("vector", lambda e, obank=obank: e.reciprocal(
                    bufs["C_rcp"][:, 0:4].unsqueeze(2),
                    psum[obank][:, :].rearrange("p (m x) -> p m x", m=4)[:, :, 64:65]),
                    waits=wn, drain=True)
                norm_toks[g] = ctx.op("vector", lambda e, g=g, obank=obank: e.scalar_tensor_tensor(
                    bufs["C_ocb"].rearrange("p (m x) -> p m x", m=4)[:, :, 64 * g:64 * g + 64],
                    psum[obank][:, :].rearrange("p (m x) -> p m x", m=4)[:, :, 0:64],
                    1.0,
                    bufs["C_rcp"][:, 0:4].unsqueeze(2).broadcast_to((128, 4, 64)),
                    ALU.mult, ALU.mult), inc="sV", drain=True)
            attn_sT_end[i] = ctx.cnt["sT"]
            all_norm = max(norm_toks.values())
            for c in range(8):
                wt = [("sV", all_norm), ("sV", ring_wr[i])]
                ctx.op("tensor", lambda e, c=c, pb=pb: e.transpose(
                    pb[:, 128 * c:128 * c + 128], bufs["C_ocb"][:, 128 * c:128 * c + 128], identb),
                    waits=(wt if c == 0 else []), inc=("sT" if c == 7 else None))
            octr = ctx.cnt["sT"]
            ocT_evs[i] = ctx.op("vector", lambda e, pb=pb: e.tensor_copy(
                bufs["C_ocT"], pb[:, 0:1024]), waits=[("sT", octr)], inc="sV", drain=True)
            oc_guard[0] = octr
            for half in range(2):
                for c in range(NCH):
                    wt = []
                    if c == 0:
                        wt.append(("sV", ocT_evs[i]))
                        if half == 0 and i >= 1:
                            wt.append(("sV", res_ev[i - 1]))
                        if half == 1:
                            wt.append(("sV", ctx.cnt["sV"]))
                    ctx.op("tensor", lambda e, c=c, half=half: e.matmul(
                        psum[7][:, :],
                        bufs["C_ocT"][:, 128 * c:128 * c + 128],
                        bufs["C_wo"][:, 1024 * c + 512 * half:1024 * c + 512 * half + 512],
                        start=(c == 0), stop=(c == NCH - 1)), waits=wt,
                        inc=("sT" if c == NCH - 1 else None))
                womm = ctx.cnt["sT"]
                wv6 = [("sT", womm)]
                if half == 0 and st_tok[b] > 0:
                    wv6.append((f"d{2 + b}", st_tok[b]))
                res_ev[i] = ctx.op("vector", lambda e, half=half, b=b: e.tensor_tensor(
                    bufs["C_rout"][b][:, 512 * half:512 * half + 512], psum[7][:, :],
                    bufs["C_ra"][b][:, 512 * half:512 * half + 512], ALU.add),
                    waits=wv6, inc="sV", drain=True)
            st_tok[b] = ctx.op("sync", lambda e, i=i, b=b: e.dma_start(
                out=r0[128 * i:128 * i + 128, :], in_=bufs["C_rout"][b]),
                waits=[("sV", res_ev[i])], inc=f"d{2 + b}", inc_n=16)

        if C_ATTN:
            front(0)
            for i in range(NT):
                if i + 1 < NT:
                    front(i + 1)
                mid(i)
                back(i)
        ctx.barrier()
        ffn2_done = 0

        ctx.op("gpsimd", lambda e, l=l: e.dma_start(
            out=bufs["C_b1row"][0:1, 0:5460], in_=io["b1"][l].unsqueeze(0)), inc="d6", inc_n=16)
        ctx.op("gpsimd", lambda e, l=l: e.dma_start(
            out=bufs["C_b2row"][0:1, :], in_=io["b2"][l].unsqueeze(0)), inc="d6", inc_n=16)
        ctx.op("sync", lambda e: e.dma_start(
            out=bufs["C_bnd"], in_=r0[2047:2048, 512:1024]), inc="d12", inc_n=16)
        ctx.op("sync", lambda e, l=l: e.dma_start(
            out=bufs["C_gb"], in_=io["ln_g"][l].unsqueeze(0).partition_broadcast(128).squeeze(1)), inc="d12", inc_n=16)
        ctx.op("sync", lambda e, l=l: e.dma_start(
            out=bufs["C_bb"], in_=io["ln_b"][l].unsqueeze(0).partition_broadcast(128).squeeze(1)), inc="d12", inc_n=16)
        brow_tok = ctx.cnt["d6"]
        brow_tok2 = ctx.cnt["d12"]

        for half in range(2 if C_FFN else 0):
            n0 = 2048 * half
            we2 = [("sT", ffn2_done)] if (half == 1 and ffn2_done) else []
            ctx.op("gpsimd", lambda e, l=l: e.dma_start(
                out=bufs["C_U"].rearrange("p (c x) -> p c x", c=22)[:, 0:6, :],
                in_=io["W2"][l][0:768].rearrange("(c p) d -> p c d", p=128)),
                waits=we2, inc="d7", inc_n=16)
            w2_early = ctx.cnt["d7"]
            xs_tok = [0] * 16
            xn_ev = [0] * 16
            xnT_ev = [0] * 16
            for t in range(16):
                i = 16 * half + t
                bt = t % 2
                w = [("sV", xn_ev[t - 2])] if t >= 2 else []
                ctx.op("sync", lambda e, i=i, bt=bt: e.dma_start(
                    out=bufs["C_xs"][bt][:, 0:512], in_=r0[128 * i:128 * i + 128, 0:512]),
                    waits=w, inc=f"d{bt}", inc_n=16)
                if i == 0:
                    ctx.op("vector", lambda e, bt=bt: e.memset(bufs["C_xs"][bt][0:1, 512:1024], 0.0), drain=True)
                    xs_tok[t] = ctx.op("sync", lambda e, i=i, bt=bt: e.dma_start(
                        out=bufs["C_xs"][bt][1:128, 512:1024], in_=r0[0:127, 512:1024]),
                        inc=f"d{bt}", inc_n=16)
                elif t == 0:
                    ctx.op("sync", lambda e, bt=bt: e.dma_start(
                        out=bufs["C_xs"][bt][0:1, 512:1024], in_=bufs["C_bnd"]),
                        waits=[("d12", brow_tok2)], inc=f"d{bt}", inc_n=16)
                    xs_tok[t] = ctx.op("sync", lambda e, i=i, bt=bt: e.dma_start(
                        out=bufs["C_xs"][bt][1:128, 512:1024], in_=r0[128 * i:128 * i + 127, 512:1024]),
                        inc=f"d{bt}", inc_n=16)
                else:
                    xs_tok[t] = ctx.op("sync", lambda e, i=i, bt=bt: e.dma_start(
                        out=bufs["C_xs"][bt][:, 512:1024], in_=r0[128 * i - 1:128 * i + 127, 512:1024]),
                        inc=f"d{bt}", inc_n=16)
                ctx.op("vector", lambda e, bt=bt: e.tensor_scalar(
                    bufs["C_xn"], bufs["C_xs"][bt], 1.0, 0.0, ALU.mult, ALU.add, accum_out=bufs["C_st"][:, 0:1]),
                    waits=[(f"d{bt}", xs_tok[t]), ("d6", brow_tok), ("d12", brow_tok2)], drain=True)
                ctx.op("vector", lambda e, bt=bt: e.scalar_tensor_tensor(
                    bufs["C_xn"], bufs["C_xs"][bt], 1.0, bufs["C_xs"][bt], ALU.mult, ALU.mult,
                    accum_out=bufs["C_st"][:, 1:2]), drain=True)
                ctx.op("vector", lambda e: e.tensor_scalar(bufs["C_st"][:, 2:3], bufs["C_st"][:, 0:1], 1.0 / D, None, ALU.mult), drain=True)
                ctx.op("vector", lambda e: e.tensor_tensor(bufs["C_st"][:, 3:4], bufs["C_st"][:, 2:3], bufs["C_st"][:, 2:3], ALU.mult), drain=True)
                ctx.op("vector", lambda e: e.tensor_scalar(bufs["C_st"][:, 4:5], bufs["C_st"][:, 1:2], 1.0 / D, None, ALU.mult), drain=True)
                ctx.op("vector", lambda e: e.tensor_tensor(bufs["C_st"][:, 4:5], bufs["C_st"][:, 4:5], bufs["C_st"][:, 3:4], ALU.subtract), drain=True)
                ctx.op("vector", lambda e: e.tensor_scalar(bufs["C_st"][:, 4:5], bufs["C_st"][:, 4:5], 1e-5, None, ALU.add), inc="sV", drain=True)
                stv = ctx.cnt["sV"]
                ctx.op("scalar", lambda e: e.activation(bufs["C_st"][:, 5:6], bufs["C_st"][:, 4:5], AF.Ln,
                       bias=cc[:, C_ZERO:C_ZERO + 1], scale=cc[:, C_ONE:C_ONE + 1]),
                       waits=[("sV", stv)], drain=True)
                ctx.op("scalar", lambda e: e.activation(bufs["C_st"][:, 6:7], bufs["C_st"][:, 5:6], AF.Exp,
                       bias=cc[:, C_ZERO:C_ZERO + 1], scale=cc[:, C_NEGHALF:C_NEGHALF + 1]),
                       inc="sC", drain=True)
                rstdt = ctx.cnt["sC"]
                ctx.op("vector", lambda e, bt=bt: e.scalar_tensor_tensor(
                    bufs["C_xn"], bufs["C_xs"][bt], bufs["C_st"][:, 2:3], bufs["C_gb"],
                    ALU.subtract, ALU.mult), waits=[("sC", rstdt)], drain=True)
                xn_ev[t] = ctx.op("vector", lambda e: e.scalar_tensor_tensor(
                    bufs["C_xn"], bufs["C_xn"], bufs["C_st"][:, 6:7], bufs["C_bb"],
                    ALU.mult, ALU.add), inc="sV", drain=True)
                ctx.op("gpsimd", lambda e: e.tensor_copy(bufs["C_xnb"], bufs["C_xn"]),
                       waits=[("sV", xn_ev[t])], inc="sG", drain=True)
                csg = ctx.cnt["sG"]
                pbx = psum[bt][:, :].bitcast(BF16)
                for c in range(NCH):
                    wt = [("sG", csg)]
                    if t >= 2 and c == 0:
                        wt.append(("sV", xnT_ev[t - 2]))
                    ctx.op("tensor", lambda e, c=c, pbx=pbx: e.transpose(
                        pbx[:, 128 * c:128 * c + 128], bufs["C_xnb"][:, 128 * c:128 * c + 128], identb),
                        waits=wt, inc=("sT" if c == NCH - 1 else None))
                xtr = ctx.cnt["sT"]
                xnT_ev[t] = ctx.op("vector", lambda e, t=t, pbx=pbx: e.tensor_copy(
                    bufs["C_U"][:, 6144:22528].rearrange("p (c x) -> p c x", c=8)[:, :, 128 * t:128 * t + 128],
                    pbx[:, 0:1024].rearrange("p (c x) -> p c x", c=8)),
                    waits=[("sT", xtr)], inc="sV", drain=True)
            xnT_done = ctx.cnt["sV"]

            w1_tok = [0, 0]
            w1_guard = [0, 0]
            jws = [128] * 21 + [42]
            h_ev = {}
            for j in range(22):
                js = j % 2
                jw = jws[j]
                wld = [("sT", w1_guard[js])] if w1_guard[js] else []
                ctx.op("gpsimd", lambda e, l=l, j=j, js=js, jw=jw: e.dma_start(
                    out=bufs["C_w1a"][js].rearrange("p (c x) -> p c x", c=8)[:, :, 0:jw],
                    in_=io["W1"][l].rearrange("(c p) f -> p c f", p=128)[:, :, 128 * j:128 * j + jw]),
                    waits=wld, inc=f"d{4 + js}", inc_n=16)
                w1_tok[js] = ctx.op("gpsimd", lambda e, l=l, j=j, js=js, jw=jw: e.dma_start(
                    out=bufs["C_w1g"][js].rearrange("p (c x) -> p c x", c=8)[:, :, 0:jw],
                    in_=io["W1"][l].rearrange("(c p) f -> p c f", p=128)[:, :, 2730 + 128 * j:2730 + 128 * j + jw]),
                    inc=f"d{4 + js}", inc_n=16)
                for ng in range(4):
                    ab = 2 + (ng % 2)
                    gb_ = 4 + (ng % 2)
                    for c in range(NCH):
                        wt = []
                        if c == 0:
                            wt.append((f"d{4 + js}", w1_tok[js]))
                            if j == 0:
                                wt.append(("sV", xnT_done))
                            if (j, ng - 2) in h_ev:
                                wt.append(("sV", h_ev[(j, ng - 2)]))
                            elif (j - 1, ng + 2) in h_ev:
                                wt.append(("sV", h_ev[(j - 1, ng + 2)]))
                        ctx.op("tensor", lambda e, c=c, ng=ng, js=js, jw=jw, ab=ab: e.matmul(
                            psum[ab][0:jw, :],
                            bufs["C_w1a"][js][:, 128 * c:128 * c + jw],
                            bufs["C_U"][:, 6144:22528][:, 2048 * c + 512 * ng:2048 * c + 512 * ng + 512],
                            start=(c == 0), stop=False), waits=wt)
                        ctx.op("tensor", lambda e, c=c, ng=ng, js=js, jw=jw, gb_=gb_: e.matmul(
                            psum[gb_][0:jw, :],
                            bufs["C_w1g"][js][:, 128 * c:128 * c + jw],
                            bufs["C_U"][:, 6144:22528][:, 2048 * c + 512 * ng:2048 * c + 512 * ng + 512],
                            start=(c == 0), stop=False))
                    ctx.op("tensor", lambda e, j=j, jw=jw, ab=ab: e.matmul(
                        psum[ab][0:jw, :], bufs["C_b1row"][0:1, 128 * j:128 * j + jw],
                        bufs["ones_sb"][0:1, 0:512], start=False, stop=True))
                    ctx.op("tensor", lambda e, j=j, jw=jw, gb_=gb_: e.matmul(
                        psum[gb_][0:jw, :], bufs["C_b1row"][0:1, 2730 + 128 * j:2730 + 128 * j + jw],
                        bufs["ones_sb"][0:1, 0:512], start=False, stop=True), inc="sT")
                    mmt = ctx.cnt["sT"]
                    glb = ng % 2
                    wgl = [("sT", mmt)]
                    if (j, ng - 2) in h_ev:
                        wgl.append(("sV", h_ev[(j, ng - 2)]))
                    elif (j - 1, ng + 2) in h_ev:
                        wgl.append(("sV", h_ev[(j - 1, ng + 2)]))
                    ctx.op("scalar", lambda e, jw=jw, gb_=gb_, glb=glb: e.activation(
                        bufs["C_gl"][glb][0:jw, :], psum[gb_][0:jw, :], AF.Gelu,
                        bias=cc[0:jw, C_ZERO:C_ZERO + 1], scale=cc[0:jw, C_ONE:C_ONE + 1]),
                        waits=wgl, inc="sC", drain=True)
                    glt = ctx.cnt["sC"]
                    h_ev[(j, ng)] = ctx.op("vector", lambda e, j=j, ng=ng, jw=jw, ab=ab, glb=glb: e.tensor_tensor(
                        bufs["C_hT"][0:jw, 2048 * j + 512 * ng:2048 * j + 512 * ng + 512],
                        psum[ab][0:jw, :], bufs["C_gl"][glb][0:jw, :], ALU.mult),
                        waits=[("sC", glt)], inc="sV", drain=True)
                w1_guard[js] = ctx.cnt["sT"]
            h_done = ctx.cnt["sV"]

            ctx.op("gpsimd", lambda e, l=l: e.dma_start(
                out=bufs["C_U"].rearrange("p (c x) -> p c x", c=22)[:, 6:21, :],
                in_=io["W2"][l][768:2688].rearrange("(c p) d -> p c d", p=128)),
                waits=[("sV", h_done), ("d7", w2_early)], inc="d7", inc_n=16)
            ctx.op("gpsimd", lambda e, l=l: e.dma_start(
                out=bufs["C_U"][0:42, :].rearrange("p (c x) -> p c x", c=22)[:, 21, :],
                in_=io["W2"][l][2688:2730]), inc="d7", inc_n=16)
            w2_tok = ctx.cnt["d7"]
            rr_tok = [0, 0]
            y_ev = {}
            st2_tok = [0, 0]
            for t in range(16):
                i = 16 * half + t
                bt = t % 2
                wr = []
                if t >= 2:
                    wr.append(("sV", y_ev[(t - 2, 1)]))
                rr_tok[bt] = ctx.op("sync", lambda e, i=i, bt=bt: e.dma_start(
                    out=bufs["C_rr"][bt], in_=r0[128 * i:128 * i + 128, :]),
                    waits=wr, inc=f"d{13 + bt}", inc_n=16)
                pbank = [6, 7] if t % 2 == 0 else [4, 5]
                for dh in range(2):
                    pk = pbank[dh]
                    for j in range(22):
                        jw = jws[j]
                        wt = []
                        if j == 0:
                            wt.append(("d7", w2_early))
                            wt.append(("sV", h_done))
                            if t >= 2:
                                wt.append(("sV", y_ev[(t - 2, dh)]))
                        elif j == 6:
                            wt.append(("d7", w2_tok))
                        ctx.op("tensor", lambda e, t=t, j=j, jw=jw, dh=dh, pk=pk: e.matmul(
                            psum[pk][:, :],
                            bufs["C_hT"][0:jw, 2048 * j + 128 * t:2048 * j + 128 * t + 128],
                            bufs["C_U"][0:jw, 1024 * j + 512 * dh:1024 * j + 512 * dh + 512],
                            start=(j == 0), stop=False), waits=wt)
                    ctx.op("tensor", lambda e, dh=dh, pk=pk: e.matmul(
                        psum[pk][:, :], bufs["ones_sb"][0:1, 0:128],
                        bufs["C_b2row"][0:1, 512 * dh:512 * dh + 512],
                        start=False, stop=True), inc="sT")
                    ymm = ctx.cnt["sT"]
                    wv7 = [("sT", ymm), (f"d{13 + bt}", rr_tok[bt])]
                    if st2_tok[bt] > 0 and dh == 0:
                        wv7.append((f"d{2 + bt}", st2_tok[bt]))
                    y_ev[(t, dh)] = ctx.op("vector", lambda e, dh=dh, bt=bt, pk=pk: e.tensor_tensor(
                        bufs["C_ro2"][bt][:, 512 * dh:512 * dh + 512], psum[pk][:, :],
                        bufs["C_rr"][bt][:, 512 * dh:512 * dh + 512], ALU.add),
                        waits=wv7, inc="sV", drain=True)
                st2_tok[bt] = ctx.op("sync", lambda e, i=i, bt=bt: e.dma_start(
                    out=r0[128 * i:128 * i + 128, :], in_=bufs["C_ro2"][bt]),
                    waits=[("sV", y_ev[(t, 1)])], inc=f"d{2 + bt}", inc_n=16)
            ffn2_done = ctx.cnt["sT"]
            ctx.barrier()


def alloc_stageD(nc, ar, bufs):
    bufs["D_x"] = ar.allocN(2, 1024, F32)
    bufs["D_s"] = ar.allocN(2, 1024, F32)
    bufs["D_f"] = ar.allocN(2, 1024, F32)
    bufs["D_g"] = ar.alloc(1024, F32)
    bufs["D_x1"] = ar.allocN(2, 1024, F32)
    bufs["D_xn"] = ar.allocN(2, 1024, F32)
    bufs["D_xnb"] = ar.alloc(1024, BF16)
    bufs["D_xtv"] = ar.allocN(2, 1024, BF16)
    bufs["D_st"] = ar.alloc(8, F32)
    bufs["D_scr"] = ar.alloc(1024, F32)


def emit_stageD(ctx, bufs, io):
    psum = bufs["psum"]
    cc = bufs["cconst"]
    identb = bufs["ident_sb"]
    x, s0, r0 = io["x"], io["s0"], io["r0"]
    x1, xn1T = io["x1"], io["xn1T"]
    xn1T_r = xn1T.rearrange("(c p) n -> p c n", p=128)
    st = bufs["D_st"]

    ldx = [0] * NT; lds = [0] * NT; ldf = [0] * NT
    gdone = [0] * NT; x1done = [0] * NT; xndone = [0] * NT
    stx1 = [0, 0]; stxn = [0, 0]; ttr = [0] * NT; vev = [0] * NT
    cast_t = [0] * NT

    for i in range(NT):
        b = i % 2
        wx = [("sV", x1done[i - 2])] if i >= 2 else []
        ldx[i] = ctx.op("sync", lambda e, i=i, b=b: e.dma_start(
            out=bufs["D_x"][b], in_=x[128 * i:128 * i + 128, :]), waits=wx, inc=f"d{b}", inc_n=16)
        lds[i] = ctx.op("sync", lambda e, i=i, b=b: e.dma_start(
            out=bufs["D_s"][b], in_=s0[128 * i:128 * i + 128, :]), waits=wx, inc=f"d{8 + b}", inc_n=16)
        wf = [("sC", gdone[i - 2])] if i >= 2 else []
        ldf[i] = ctx.op("sync", lambda e, i=i, b=b: e.dma_start(
            out=bufs["D_f"][b], in_=r0[128 * i:128 * i + 128, :]), waits=wf, inc=f"d{13 + b}", inc_n=16)
        wg = [(f"d{13 + b}", ldf[i])]
        if i >= 1:
            wg.append(("sV", x1done[i - 1]))
        gdone[i] = ctx.op("scalar", lambda e, b=b: e.activation(
            bufs["D_g"], bufs["D_f"][b], AF.Silu,
            bias=cc[:, C_ZERO:C_ZERO + 1], scale=cc[:, C_ONE:C_ONE + 1]),
            waits=wg, inc="sC", drain=True)
        wv = [("sC", gdone[i]), (f"d{b}", ldx[i]), (f"d{8 + b}", lds[i])]
        if i >= 2:
            wv.append((f"d{2 + b}", stx1[b]))
        ctx.op("vector", lambda e, b=b: e.tensor_tensor(
            bufs["D_scr"], bufs["D_g"], bufs["D_s"][b], ALU.mult), waits=wv, drain=True)
        x1done[i] = ctx.op("vector", lambda e, b=b: e.tensor_tensor(
            bufs["D_x1"][b], bufs["D_scr"], bufs["D_x"][b], ALU.add), inc="sV", drain=True)
        stx1[b] = ctx.op("sync", lambda e, i=i, b=b: e.dma_start(
            out=x1[128 * i:128 * i + 128, :], in_=bufs["D_x1"][b]),
            waits=[("sV", x1done[i])], inc=f"d{2 + b}", inc_n=16)
        ctx.op("vector", lambda e, b=b: e.tensor_scalar(
            bufs["D_scr"], bufs["D_x1"][b], 1.0, 0.0, ALU.mult, ALU.add, accum_out=st[:, 0:1]), drain=True)
        ctx.op("vector", lambda e, b=b: e.scalar_tensor_tensor(
            bufs["D_scr"], bufs["D_x1"][b], 1.0, bufs["D_x1"][b], ALU.mult, ALU.mult,
            accum_out=st[:, 1:2]), drain=True)
        ctx.op("vector", lambda e: e.tensor_scalar(st[:, 2:3], st[:, 0:1], 1.0 / D, None, ALU.mult), drain=True)
        ctx.op("vector", lambda e: e.tensor_tensor(st[:, 3:4], st[:, 2:3], st[:, 2:3], ALU.mult), drain=True)
        ctx.op("vector", lambda e: e.tensor_scalar(st[:, 4:5], st[:, 1:2], 1.0 / D, None, ALU.mult), drain=True)
        ctx.op("vector", lambda e: e.tensor_tensor(st[:, 4:5], st[:, 4:5], st[:, 3:4], ALU.subtract), drain=True)
        ctx.op("vector", lambda e: e.tensor_scalar(st[:, 4:5], st[:, 4:5], 1e-5, None, ALU.add), inc="sV", drain=True)
        stv = ctx.cnt["sV"]
        ctx.op("scalar", lambda e: e.activation(st[:, 5:6], st[:, 4:5], AF.Ln,
               bias=cc[:, C_ZERO:C_ZERO + 1], scale=cc[:, C_ONE:C_ONE + 1]),
               waits=[("sV", stv)], drain=True)
        ctx.op("scalar", lambda e: e.activation(st[:, 6:7], st[:, 5:6], AF.Exp,
               bias=cc[:, C_ZERO:C_ZERO + 1], scale=cc[:, C_NEGHALF:C_NEGHALF + 1]),
               inc="sC", drain=True)
        rst = ctx.cnt["sC"]
        wv2 = [("sC", rst)]
        if i >= 2:
            wv2.append(("sG", cast_t[i - 2]))
        xndone[i] = ctx.op("vector", lambda e, b=b: e.tensor_scalar(
            bufs["D_xn"][b], bufs["D_x1"][b], st[:, 2:3], st[:, 6:7], ALU.subtract, ALU.mult),
            waits=wv2, inc="sV", drain=True)
        wcast = [("sV", xndone[i])]
        if i >= 1:
            wcast.append(("sT", ttr[i - 1]))
        cast_t[i] = ctx.op("gpsimd", lambda e, b=b: e.tensor_copy(bufs["D_xnb"], bufs["D_xn"][b]),
                           waits=wcast, inc="sG", drain=True)
        pb = psum[4 + b][:, :].bitcast(BF16)
        for c in range(NCH):
            wt = [("sG", cast_t[i])]
            if i >= 2 and c == 0:
                wt.append(("sV", vev[i - 2]))
            ctx.op("tensor", lambda e, c=c, pb=pb: e.transpose(
                pb[:, 128 * c:128 * c + 128], bufs["D_xnb"][:, 128 * c:128 * c + 128], identb),
                waits=wt, inc=("sT" if c == NCH - 1 else None))
        ttr[i] = ctx.cnt["sT"]
        we = [("sT", ttr[i])]
        if i >= 2:
            we.append((f"d{15 + b}", stxn[b]))
        vev[i] = ctx.op("vector", lambda e, b=b, pb=pb: e.tensor_copy(bufs["D_xtv"][b], pb[:, 0:1024]),
                        waits=we, inc="sV", drain=True)
        stxn[b] = ctx.op("sync", lambda e, i=i, b=b: e.dma_start(
            out=xn1T_r[:, :, 128 * i:128 * i + 128],
            in_=bufs["D_xtv"][b].rearrange("p (c n) -> p c n", c=8)),
            waits=[("sV", vev[i])], inc=f"d{15 + b}", inc_n=16)


def alloc_stageE(nc, ar, bufs):
    bufs["E_w"] = ar.allocN(2, 4096, F32)
    bufs["E_acc"] = ar.alloc(4, F32)
    bufs["E_scr"] = ar.alloc(128, F32)
    bufs["E_thr"] = ar.alloc(4, F32)
    bufs["E_ge"] = ar.alloc(4096, F32)
    bufs["E_le"] = ar.alloc(4096, F32)
    bufs["E_wt"] = ar.alloc(4096, BF16)
    bufs["E_tv"] = ar.allocN(2, 4096, BF16)


def _stageE_one(ctx, bufs, io, wname, oname, rows, cols, scr_name):
    psum = bufs["psum"]
    cc = bufs["cconst"]
    identb = bufs["ident_sb"]
    W = io[wname]
    OT = io[oname].rearrange("(c p) n -> p c n", p=128)
    ntile = rows // 128
    nch = cols // 128
    numel = rows * cols
    ld = [0] * ntile
    used = [0] * ntile
    for t in range(ntile):
        b = t % 2
        w = [("sV", used[t - 2])] if t >= 2 else []
        ld[t] = ctx.op("gpsimd", lambda e, t=t, b=b, W=W, cols=cols: e.dma_start(
            out=bufs["E_w"][b][:, 0:cols], in_=W[128 * t:128 * t + 128, :]),
            waits=w, inc=f"d{10 + b}", inc_n=16)
        ctx.op("vector", lambda e, b=b, cols=cols: e.tensor_reduce(
            bufs["E_scr"][:, 0:1], bufs["E_w"][b][:, 0:cols], AX.X, ALU.add,
            apply_absolute_value=True), waits=[(f"d{10 + b}", ld[t])], drain=True)
        if t == 0:
            used[t] = ctx.op("vector", lambda e: e.tensor_copy(
                bufs["E_acc"][:, 0:1], bufs["E_scr"][:, 0:1]), inc="sV", drain=True)
        else:
            used[t] = ctx.op("vector", lambda e: e.tensor_tensor(
                bufs["E_acc"][:, 0:1], bufs["E_acc"][:, 0:1], bufs["E_scr"][:, 0:1], ALU.add),
                inc="sV", drain=True)
    ctx.op("tensor", lambda e: e.transpose(
        psum[0][0:1, 0:128], bufs["E_acc"][:, 0:1], bufs["ident_f"]),
        waits=[("sV", used[ntile - 1])], inc="sT")
    tt = ctx.cnt["sT"]
    ctx.op("vector", lambda e: e.reduce_sum(
        bufs["E_thr"][0:1, 0:1], psum[0][0:1, 0:128], axis=AX.X), waits=[("sT", tt)], inc="sV", drain=True)
    ctx.op("vector", lambda e, numel=numel: e.tensor_scalar(
        bufs["E_thr"][0:1, 1:2], bufs["E_thr"][0:1, 0:1], 1.0 / numel, 1e-8, ALU.mult, ALU.max), drain=True)
    ctx.op("vector", lambda e: e.tensor_scalar(
        bufs["E_thr"][0:1, 2:3], bufs["E_thr"][0:1, 1:2], 0.5, None, ALU.mult), inc="sV", drain=True)
    thr_tok = ctx.cnt["sV"]
    ctx.op("sync", lambda e, scr_name=scr_name: e.dma_start(
        out=io[scr_name], in_=bufs["E_thr"][0:1, 2:3]), waits=[("sV", thr_tok)], inc="d19", inc_n=16)
    st8 = ctx.cnt["d19"]
    ctx.op("sync", lambda e, scr_name=scr_name: e.dma_start(
        out=bufs["E_thr"][:, 3:4], in_=io[scr_name].partition_broadcast(128).squeeze(1)),
        waits=[("d19", st8)], inc="d19", inc_n=16)
    bcast_tok = ctx.cnt["d19"]
    st_tok = [0, 0]
    tern_done = [0] * ntile
    for t in range(ntile):
        b = t % 2
        w = [("d19", bcast_tok)]
        if t >= 2:
            w.append(("sV", tern_done[t - 2]))
        ld2 = ctx.op("gpsimd", lambda e, t=t, b=b, W=W, cols=cols: e.dma_start(
            out=bufs["E_w"][b][:, 0:cols], in_=W[128 * t:128 * t + 128, :]),
            waits=w, inc=f"d{10 + b}", inc_n=16)
        wv = [(f"d{10 + b}", ld2)]
        ctx.op("vector", lambda e, b=b, cols=cols: e.tensor_scalar(
            bufs["E_ge"][:, 0:cols], bufs["E_w"][b][:, 0:cols], bufs["E_thr"][:, 3:4], None, ALU.is_ge),
            waits=wv, drain=True)
        ctx.op("vector", lambda e, b=b, cols=cols: e.tensor_scalar(
            bufs["E_le"][:, 0:cols], bufs["E_w"][b][:, 0:cols],
            -1.0, bufs["E_thr"][:, 3:4], ALU.mult, ALU.is_ge), drain=True)
        tern_done[t] = ctx.op("vector", lambda e, cols=cols: e.tensor_sub(
            bufs["E_wt"][:, 0:cols], bufs["E_ge"][:, 0:cols], bufs["E_le"][:, 0:cols]),
            inc="sV", drain=True)
        for rnd in range((nch + 7) // 8):
            pb = psum[2 + (t + rnd) % 2][:, :].bitcast(BF16)
            ncthis = min(8, nch - 8 * rnd)
            for c in range(ncthis):
                wt = [("sV", tern_done[t])]
                if t * 8 + rnd >= 2 and c == 0:
                    wt.append(("sV", ev_guard[(t + rnd) % 2]))
                ctx.op("tensor", lambda e, c=c, rnd=rnd, pb=pb: e.transpose(
                    pb[:, 128 * c:128 * c + 128],
                    bufs["E_wt"][:, 1024 * rnd + 128 * c:1024 * rnd + 128 * c + 128], identb),
                    waits=wt, inc=("sT" if c == ncthis - 1 else None))
            trt = ctx.cnt["sT"]
            we = [("sT", trt)]
            if st_tok[(t + rnd) % 2] > 0:
                we.append((f"d{17 + (t + rnd) % 2}", st_tok[(t + rnd) % 2]))
            b2 = (t + rnd) % 2
            ev = ctx.op("vector", lambda e, b2=b2, ncthis=ncthis, pb=pb: e.tensor_copy(
                bufs["E_tv"][b2][:, 0:128 * ncthis], pb[:, 0:128 * ncthis]),
                waits=we, inc="sV", drain=True)
            ev_guard[b2] = ev
            st_tok[b2] = ctx.op("sync", lambda e, t=t, rnd=rnd, ncthis=ncthis,
                                b2=b2, OT=OT: e.dma_start(
                out=OT[:, 8 * rnd:8 * rnd + ncthis, 128 * t:128 * t + 128],
                in_=bufs["E_tv"][b2][:, 0:128 * ncthis].rearrange("p (c n) -> p c n", n=128)),
                waits=[("sV", ev)], inc=f"d{17 + b2}", inc_n=16)


def emit_stageE(ctx, bufs, io):
    global ev_guard
    ev_guard = {0: 0, 1: 0}
    _stageE_one(ctx, bufs, io, "W1q", "w1tT", 4096, 1024, "escr1")
    ctx.barrier()
    ev_guard = {0: 0, 1: 0}
    _stageE_one(ctx, bufs, io, "W2q", "w2tT", 1024, 4096, "escr2")


def alloc_stageF(nc, ar, bufs):
    bufs["F_xnT"] = ar.alloc(8 * 4096, BF16)
    bufs["F_wst"] = ar.alloc(8 * 1024, BF16)
    bufs["F_wqT"] = ar.alloc(8 * 1024, BF16)
    bufs["F_wkT"] = ar.alloc(8 * 1024, BF16)
    bufs["F_wvT"] = ar.alloc(8 * 1024, BF16)
    bufs["F_woT"] = ar.alloc(8 * 1024, BF16)
    bufs["F_om"] = ar.alloc(64, BF16)
    bufs["F_omst"] = ar.alloc(64, BF16)
    bufs["F_bo"] = ar.alloc(1024, BF16)
    bufs["F_q"] = ar.alloc(1024, F32)
    bufs["F_k"] = ar.alloc(1024, F32)
    bufs["F_qb"] = ar.alloc(1024, BF16)
    bufs["F_kb"] = ar.alloc(1024, BF16)
    bufs["F_v"] = ar.alloc(16 * 65, BF16)
    bufs["F_scr"] = ar.alloc(1024, F32)
    bufs["F_nu"] = ar.alloc(32, F32)
    bufs["F_phq"] = ar.alloc(1024, BF16)
    bufs["F_phk"] = ar.alloc(1024, BF16)
    bufs["F_pqT"] = ar.alloc(1024, BF16)
    bufs["F_pkT"] = ar.alloc(1024, BF16)
    bufs["F_AT"] = ar.allocN(2, 512, BF16)
    bufs["F_Sf"] = ar.alloc(8 * 65, F32)
    bufs["F_Sb"] = ar.alloc(8 * 65, BF16)
    bufs["F_oc"] = ar.alloc(1024, BF16)
    bufs["F_ocT"] = ar.alloc(1024, BF16)
    bufs["F_s1"] = ar.allocN(2, 1024, F32)
    bufs["F_z"] = ar.alloc(4, F32)
    bufs["F_z2"] = ar.alloc(4, F32)


def emit_stageF(ctx, bufs, io):
    psum = bufs["psum"]
    cc = bufs["cconst"]
    identb = bufs["ident_sb"]
    LN8 = math.log(8.0)
    wev = {}
    at_guard = {0: 0, 1: 0}
    s1_ev_half = {}

    ctx.op("gpsimd", lambda e: e.dma_start(
        out=bufs["F_xnT"].rearrange("p (c n) -> p c n", c=8),
        in_=io["xn1T"].rearrange("(c p) n -> p c n", p=128)), inc="gld", inc_n=16)
    ctx.op("gpsimd", lambda e: e.dma_start(out=bufs["F_omst"][0:64, :], in_=io["omega"]),
           inc="gld", inc_n=16)
    ctx.op("gpsimd", lambda e: e.dma_start(out=bufs["F_bo"][0:1, :], in_=io["bo1"].unsqueeze(0)),
           inc="gld", inc_n=16)
    gld_pre = ctx.cnt["gld"]
    wT_ready = {}
    for wi, (wn, dst) in enumerate([("wq1", "F_wqT"), ("wk1", "F_wkT"),
                                    ("wv1", "F_wvT"), ("wo1", "F_woT")]):
        ldt = ctx.op("gpsimd", lambda e, wn=wn: e.dma_start(
            out=bufs["F_wst"].rearrange("p (c n) -> p c n", c=8),
            in_=io[wn].rearrange("(c p) n -> p c n", p=128)),
            waits=([("sT", wT_ready[wi - 1])] if wi > 0 else []), inc="d10", inc_n=16)
        for cf in range(8):
            pbw = psum[cf % 2][:, :].bitcast(BF16)
            for c in range(NCH):
                wt = []
                if c == 0:
                    wt.append(("d10", ldt))
                    if wi > 0 or cf >= 2:
                        wt.append(("sV", wev.get((wi, cf - 2), wev.get((wi - 1, cf + 6), 0))))
                ctx.op("tensor", lambda e, c=c, cf=cf, pbw=pbw: e.transpose(
                    pbw[:, 128 * c:128 * c + 128],
                    bufs["F_wst"][:, 1024 * cf + 128 * c:1024 * cf + 128 * c + 128], identb),
                    waits=wt, inc=("sT" if c == NCH - 1 else None))
            trt = ctx.cnt["sT"]
            wev[(wi, cf)] = ctx.op("vector", lambda e, cf=cf, dst=dst, pbw=pbw: e.tensor_copy(
                bufs[dst].rearrange("p (c n) -> p c n", c=8)[:, :, 128 * cf:128 * cf + 128],
                pbw[:, 0:1024].rearrange("p (c n) -> p c n", c=8)),
                waits=[("sT", trt)], inc="sV", drain=True)
        wT_ready[wi] = ctx.cnt["sT"]
    wT_all = ctx.cnt["sV"]
    ctx.op("tensor", lambda e: e.transpose(
        psum[2][:, :].bitcast(BF16)[0:64, 0:64], bufs["F_omst"][0:64, :], identb[0:64, 0:64]),
        waits=[("gld", gld_pre), ("sV", wT_all)], inc="sT")
    omt = ctx.cnt["sT"]
    omev = ctx.op("vector", lambda e: e.tensor_copy(
        bufs["F_om"][0:64, :], psum[2][:, :].bitcast(BF16)[0:64, 0:64]),
        waits=[("sT", omt)], inc="sV", drain=True)
    omrep = ctx.op("gpsimd", lambda e: e.dma_start(out=bufs["F_om"][64:128, :], in_=bufs["F_om"][0:64, :]),
                   waits=[("sV", omev)], inc="gld", inc_n=16)
    pre_all = ctx.cnt["gld"]
    ctx.op("vector", lambda e: e.memset(bufs["F_Sf"], 0.0), drain=True)
    ctx.op("vector", lambda e: e.memset(bufs["F_Sb"], 0.0), drain=True)
    zs = ctx.op("vector", lambda e: e.memset(
        bufs["F_v"].rearrange("p (h x) -> p h x", h=16)[:, :, 64:65], 1.0), inc="sV", drain=True)

    qkv_ev = {}
    qkv_mm_done = [0] * NT
    exp_done = {}
    bank67_guard = {6: 0, 7: 0}
    bank45_state = {4: 0, 5: 0}
    st_tok = [0, 0]
    s1_ev = [0, 0]
    scan_done = [0] * NT
    proj_done = [0] * NT
    cast_done = [0] * NT
    phT_ev = [0] * NT
    oct_ev = [0] * NT
    norm_tok = {g: 0 for g in range(4)}
    mask_tok = {g: 0 for g in range(4)}
    state_tok = {g: 0 for g in range(4)}

    for i in range(NT):
        for gi, (dstb, bks) in enumerate([("F_q", (0, 1)), ("F_k", (2, 3)), (None, (4, 5))]):
            for hf in range(2):
                bk = bks[hf]
                wsel = [bufs["F_wqT"], bufs["F_wkT"], bufs["F_wvT"]][gi]
                for c in range(NCH):
                    wt = []
                    if c == 0:
                        wt.append(("gld", pre_all))
                        wt.append(("sV", zs))
                        if i > 0:
                            wt.append(("sV", qkv_ev[(i - 1, gi, hf)]))
                        if gi == 0 and i > 0:
                            wt.append(("sV", oct_ev[i - 1]))
                        if gi == 1 and i > 0:
                            wt.append(("sV", s1_ev[(i - 1) % 2]))
                    ctx.op("tensor", lambda e, c=c, i=i, hf=hf, bk=bk, wsel=wsel: e.matmul(
                        psum[bk][:, :],
                        bufs["F_xnT"][:, 4096 * c + 128 * i:4096 * c + 128 * i + 128],
                        wsel[:, 1024 * c + 512 * hf:1024 * c + 512 * hf + 512],
                        start=(c == 0), stop=(c == NCH - 1)), waits=wt,
                        inc=("sT" if c == NCH - 1 else None))
                mmt = ctx.cnt["sT"]
                if gi < 2:
                    qkv_ev[(i, gi, hf)] = ctx.op("vector", lambda e, gi=gi, hf=hf, bk=bk, dstb=dstb: e.tensor_copy(
                        bufs[dstb][:, 512 * hf:512 * hf + 512], psum[bk][:, :]),
                        waits=[("sT", mmt)], inc="sV", drain=True)
                else:
                    wv = [("sT", mmt)]
                    if i > 0:
                        wv.append(("sT", scan_done[i - 1]))
                    qkv_ev[(i, gi, hf)] = ctx.op("vector", lambda e, hf=hf, bk=bk: e.tensor_copy(
                        bufs["F_v"].rearrange("p (h x) -> p h x", h=16)[:, 8 * hf:8 * hf + 8, 0:64],
                        psum[bk][:, :].rearrange("p (h x) -> p h x", h=8)),
                        waits=wv, inc="sV", drain=True)
        qkv_mm_done[i] = ctx.cnt["sT"]
        ctx.op("vector", lambda e: e.tensor_tensor(
            bufs["F_scr"], bufs["F_q"], bufs["F_q"], ALU.mult),
            waits=[("sV", qkv_ev[(i, 0, 1)])], drain=True)
        ctx.op("vector", lambda e: e.reduce_sum(
            bufs["F_nu"][:, 0:16], bufs["F_scr"].rearrange("p (h x) -> p h x", h=16), axis=AX.X), drain=True)
        ctx.op("vector", lambda e: e.tensor_scalar(
            bufs["F_nu"][:, 0:16], bufs["F_nu"][:, 0:16], -0.5, -LN8, ALU.mult, ALU.add), inc="sV", drain=True)
        ctx.op("vector", lambda e: e.tensor_tensor(
            bufs["F_scr"], bufs["F_k"], bufs["F_k"], ALU.mult),
            waits=[("sV", qkv_ev[(i, 1, 1)])], drain=True)
        ctx.op("vector", lambda e: e.reduce_sum(
            bufs["F_nu"][:, 16:32], bufs["F_scr"].rearrange("p (h x) -> p h x", h=16), axis=AX.X), drain=True)
        ctx.op("vector", lambda e: e.tensor_scalar(
            bufs["F_nu"][:, 16:32], bufs["F_nu"][:, 16:32], -0.5, -LN8, ALU.mult, ALU.add), inc="sV", drain=True)
        nu_tok = ctx.cnt["sV"]
        wcast = [("sV", qkv_ev[(i, 0, 1)])]
        if i > 0:
            wcast.append(("sT", proj_done[i - 1]))
        ctx.op("gpsimd", lambda e: e.tensor_copy(bufs["F_qb"], bufs["F_q"]), waits=wcast, drain=True)
        cast_done[i] = ctx.op("gpsimd", lambda e: e.tensor_copy(bufs["F_kb"], bufs["F_k"]),
                              waits=[("sV", qkv_ev[(i, 1, 1)])], inc="sG", drain=True)
        pb0 = psum[0][:, :].bitcast(BF16)
        pb1 = psum[1][:, :].bitcast(BF16)
        for c in range(8):
            wt = [("sG", cast_done[i])]
            if c == 0:
                wt.append(("sV", qkv_ev[(i, 0, 1)]))
            ctx.op("tensor", lambda e, c=c, pb0=pb0: e.transpose(
                pb0[:, 128 * c:128 * c + 128], bufs["F_qb"][:, 128 * c:128 * c + 128], identb),
                waits=wt, inc=("sT" if c == 7 else None))
        qtr = ctx.cnt["sT"]
        qT_ev = ctx.op("vector", lambda e, pb0=pb0: e.tensor_copy(bufs["F_pqT"], pb0[:, 0:1024]),
                       waits=[("sT", qtr)], inc="sV", drain=True)
        for c in range(8):
            ctx.op("tensor", lambda e, c=c, pb1=pb1: e.transpose(
                pb1[:, 128 * c:128 * c + 128], bufs["F_kb"][:, 128 * c:128 * c + 128], identb),
                waits=([("sG", cast_done[i])] if c == 0 else []), inc=("sT" if c == 7 else None))
        ktr = ctx.cnt["sT"]
        kT_ev = ctx.op("vector", lambda e, pb1=pb1: e.tensor_copy(bufs["F_pkT"], pb1[:, 0:1024]),
                       waits=[("sT", ktr)], inc="sV", drain=True)
        for qk in range(2):
            srcT = bufs["F_pqT"] if qk == 0 else bufs["F_pkT"]
            dstP = bufs["F_phq"] if qk == 0 else bufs["F_phk"]
            for h in range(16):
                hp, par = h // 2, h % 2
                base = 64 * par
                pk = 6 + ((qk * 16 + h) % 2)
                wt = [("sV", qT_ev if qk == 0 else kT_ev)]
                if bank67_guard[pk]:
                    wt.append(("sC", bank67_guard[pk]))
                gk = pk - 6
                if norm_tok[gk] > 0 or norm_tok[gk + 2] > 0:
                    wt.append(("sV", max(norm_tok[gk], norm_tok[gk + 2])))
                ctx.op("tensor", lambda e, hp=hp, base=base, srcT=srcT, pk=pk: e.matmul(
                    psum[pk][:, 0:64],
                    srcT[base:base + 64, 128 * hp:128 * hp + 128],
                    bufs["F_om"][base:base + 64, :],
                    start=True, stop=True), waits=wt, inc="sT")
                pmm = ctx.cnt["sT"]
                bcol = h if qk == 0 else 16 + h
                bank67_guard[pk] = ctx.op("scalar", lambda e, h=h, bcol=bcol, dstP=dstP, pk=pk: e.activation(
                    dstP[:, 64 * h:64 * h + 64], psum[pk][:, 0:64], AF.Exp,
                    bias=bufs["F_nu"][:, bcol:bcol + 1], scale=cc[:, C_ONE:C_ONE + 1]),
                    waits=[("sT", pmm), ("sV", nu_tok)], inc="sC")
        proj_done[i] = ctx.cnt["sT"]
        exp_phi_done = ctx.cnt["sC"]
        for c in range(8):
            wt = [("sC", exp_phi_done)]
            if c == 0:
                wt.append(("sV", qT_ev))
                wt.append(("sT", proj_done[i]))
            ctx.op("tensor", lambda e, c=c, pb0=pb0: e.transpose(
                pb0[:, 128 * c:128 * c + 128], bufs["F_phq"][:, 128 * c:128 * c + 128], identb),
                waits=wt, inc=("sT" if c == 7 else None))
        pqtr = ctx.cnt["sT"]
        pqT_ev = ctx.op("vector", lambda e, pb0=pb0: e.tensor_copy(bufs["F_pqT"], pb0[:, 0:1024]),
                        waits=[("sT", pqtr)], inc="sV", drain=True)
        for c in range(8):
            wt = [("sC", exp_phi_done)]
            if c == 0:
                wt.append(("sV", kT_ev))
            ctx.op("tensor", lambda e, c=c, pb1=pb1: e.transpose(
                pb1[:, 128 * c:128 * c + 128], bufs["F_phk"][:, 128 * c:128 * c + 128], identb),
                waits=wt, inc=("sT" if c == 7 else None))
        pktr = ctx.cnt["sT"]
        pkT_ev = ctx.op("vector", lambda e, pb1=pb1: e.tensor_copy(bufs["F_pkT"], pb1[:, 0:1024]),
                        waits=[("sT", pktr)], inc="sV", drain=True)
        phT_ev[i] = ctx.cnt["sV"]

        H0 = [0, 8, 1, 9]
        for g in range(4):
            h0 = H0[g]
            par = h0 % 2
            base = 64 * par
            ab = 4 + (g % 2)
            ob = 6 + (g % 2)
            atb = g % 2
            for m in range(4):
                h = h0 + 2 * m
                hp = h // 2
                wt = []
                if m == 0:
                    wt.append(("sV", pkT_ev))
                    wt.append(("sV", pqT_ev))
                    if state_tok[(g - 2) % 4] > 0:
                        wt.append(("sV", state_tok[(g - 2) % 4]))
                ctx.op("tensor", lambda e, hp=hp, base=base, ab=ab, m=m: e.matmul(
                    psum[ab][:, 128 * m:128 * m + 128],
                    bufs["F_pkT"][base:base + 64, 128 * hp:128 * hp + 128],
                    bufs["F_pqT"][base:base + 64, 128 * hp:128 * hp + 128],
                    start=True, stop=True), waits=wt, inc=("sT" if m == 3 else None))
            atmm = ctx.cnt["sT"]
            wm = [("sT", atmm)]
            if at_guard[atb] > 0:
                wm.append(("sT", at_guard[atb]))
            mask_tok[g] = ctx.op("vector", lambda e, atb=atb, ab=ab: e.tensor_tensor(
                bufs["F_AT"][atb], psum[ab][:, :], bufs["tri_up4_sb"], ALU.mult),
                waits=wm, inc="sV", drain=True)
            for m in range(4):
                h = h0 + 2 * m
                hp = h // 2
                wo1 = []
                if m == 0:
                    wo1.append(("sV", mask_tok[g]))
                    wo1.append(("sV", qkv_ev[(i, 2, 1)]))
                    if norm_tok[(g - 2) % 4] > 0:
                        wo1.append(("sV", norm_tok[(g - 2) % 4]))
                    if state_tok[g] > 0:
                        wo1.append(("sV", state_tok[g]))
                ctx.op("tensor", lambda e, h=h, m=m, atb=atb, ob=ob: e.matmul(
                    psum[ob][:, 128 * m:128 * m + 65], bufs["F_AT"][atb][:, 128 * m:128 * m + 128],
                    bufs["F_v"][:, 65 * h:65 * h + 65], start=(m == 0), stop=False), waits=wo1)
                ctx.op("tensor", lambda e, h=h, hp=hp, m=m, base=base, ob=ob: e.matmul(
                    psum[ob][:, 128 * m:128 * m + 65],
                    bufs["F_pqT"][base:base + 64, 128 * hp:128 * hp + 128],
                    bufs["F_Sb"][base:base + 64, 65 * hp:65 * hp + 65],
                    start=False, stop=(m == 3)), inc=("sT" if m == 3 else None))
            omm = ctx.cnt["sT"]
            at_guard[atb] = omm
            for m in range(4):
                h = h0 + 2 * m
                ctx.op("tensor", lambda e, h=h, m=m, base=base, ab=ab: e.matmul(
                    psum[ab][base:base + 64, 128 * m:128 * m + 65],
                    bufs["F_phk"][:, 64 * h:64 * h + 64],
                    bufs["F_v"][:, 65 * h:65 * h + 65], start=True, stop=True),
                    inc=("sT" if m == 3 else None))
            umm = ctx.cnt["sT"]
            ctx.op("vector", lambda e, ob=ob: e.tensor_scalar(
                bufs["F_z"][:, 0:4].unsqueeze(2),
                psum[ob][:, :].rearrange("p (m x) -> p m x", m=4)[:, :, 64:65],
                1e-6, None, ALU.add), waits=[("sT", omm)], drain=True)
            ctx.op("vector", lambda e: e.reciprocal(
                bufs["F_z2"][:, 0:4].unsqueeze(2), bufs["F_z"][:, 0:4].unsqueeze(2)), drain=True)
            norm_tok[g] = ctx.op("vector", lambda e, h0=h0, ob=ob: e.scalar_tensor_tensor(
                bufs["F_oc"].rearrange("p (m x) -> p m x", x=128)[:, h0 // 2:h0 // 2 + 4,
                                                                 64 * (h0 % 2):64 * (h0 % 2) + 64],
                psum[ob][:, :].rearrange("p (m x) -> p m x", m=4)[:, :, 0:64],
                1.0,
                bufs["F_z2"][:, 0:4].unsqueeze(2).broadcast_to((128, 4, 64)),
                ALU.mult, ALU.mult), inc="sV", drain=True)
            hp0 = h0 // 2
            ctx.op("vector", lambda e, hp0=hp0, base=base, ab=ab: e.tensor_tensor(
                bufs["F_Sf"][base:base + 64, 65 * hp0:65 * hp0 + 260],
                bufs["F_Sf"][base:base + 64, 65 * hp0:65 * hp0 + 260],
                psum[ab][base:base + 64, :].rearrange("p (m x) -> p m x", m=4)[:, :, 0:65],
                ALU.add), waits=[("sT", umm)], drain=True)
            state_tok[g] = ctx.op("vector", lambda e, hp0=hp0, base=base: e.tensor_copy(
                bufs["F_Sb"][base:base + 64, 65 * hp0:65 * hp0 + 260],
                bufs["F_Sf"][base:base + 64, 65 * hp0:65 * hp0 + 260]), inc="sV", drain=True)
        scan_done[i] = ctx.cnt["sT"]
        all_norm = max(norm_tok.values())
        for c in range(8):
            wt = [("sV", all_norm)]
            if c == 0:
                wt.append(("sV", pqT_ev))
            ctx.op("tensor", lambda e, c=c, pb0=pb0: e.transpose(
                pb0[:, 128 * c:128 * c + 128], bufs["F_oc"][:, 128 * c:128 * c + 128], identb),
                waits=wt, inc=("sT" if c == 7 else None))
        octr = ctx.cnt["sT"]
        ocT_evt = ctx.op("vector", lambda e, pb0=pb0: e.tensor_copy(bufs["F_ocT"], pb0[:, 0:1024]),
                         waits=[("sT", octr)], inc="sV", drain=True)
        oct_ev[i] = ocT_evt
        bs = i % 2
        for hf in range(2):
            bk = 2 + hf
            for c in range(NCH):
                wt = []
                if c == 0:
                    wt.append(("sV", ocT_evt))
                    if (i - 1, hf) in s1_ev_half:
                        wt.append(("sV", s1_ev_half[(i - 1, hf)]))
                ctx.op("tensor", lambda e, c=c, hf=hf, bk=bk: e.matmul(
                    psum[bk][:, :],
                    bufs["F_ocT"][:, 128 * c:128 * c + 128],
                    bufs["F_woT"][:, 1024 * c + 512 * hf:1024 * c + 512 * hf + 512],
                    start=(c == 0), stop=False), waits=wt)
            ctx.op("tensor", lambda e, hf=hf, bk=bk: e.matmul(
                psum[bk][:, :], bufs["ones_sb"][0:1, 0:128],
                bufs["F_bo"][0:1, 512 * hf:512 * hf + 512], start=False, stop=True), inc="sT")
            smm = ctx.cnt["sT"]
            wv = [("sT", smm)]
            if st_tok[bs] > 0 and hf == 0:
                wv.append((f"d{2 + bs}", st_tok[bs]))
            s1_ev_half[(i, hf)] = ctx.op("vector", lambda e, hf=hf, bs=bs, bk=bk: e.tensor_copy(
                bufs["F_s1"][bs][:, 512 * hf:512 * hf + 512], psum[bk][:, :]),
                waits=wv, inc="sV", drain=True)
        s1_ev[bs] = s1_ev_half[(i, 1)]
        st_tok[bs] = ctx.op("sync", lambda e, i=i, bs=bs: e.dma_start(
            out=io["s1"][128 * i:128 * i + 128, :], in_=bufs["F_s1"][bs]),
            waits=[("sV", s1_ev_half[(i, 1)])], inc=f"d{2 + bs}", inc_n=16)


def alloc_stageG(nc, ar, bufs):
    bufs["G_w1"] = ar.alloc(8 * 4096, BF16)
    bufs["G_w2"] = ar.alloc(32 * 1024, BF16)
    bufs["G_b1"] = ar.alloc(4096, BF16)
    bufs["G_b2"] = ar.alloc(1024, BF16)
    bufs["G_xnb"] = ar.allocN(2, 8 * 256, BF16)
    bufs["G_h"] = ar.alloc(32 * 256, BF16)
    bufs["G_g"] = ar.alloc(1024, F32)
    bufs["G_x1"] = ar.allocN(2, 1024, F32)
    bufs["G_s1"] = ar.allocN(2, 1024, F32)
    bufs["G_o"] = ar.allocN(2, 1024, F32)
    bufs["G_scr"] = ar.alloc(1024, F32)


def emit_stageG(ctx, bufs, io):
    psum = bufs["psum"]
    cc = bufs["cconst"]

    ctx.op("gpsimd", lambda e: e.dma_start(
        out=bufs["G_w1"].rearrange("p (c n) -> p c n", c=8),
        in_=io["w1tT"].rearrange("(c p) n -> p c n", p=128)), inc="gld", inc_n=16)
    ctx.op("gpsimd", lambda e: e.dma_start(
        out=bufs["G_w2"].rearrange("p (c n) -> p c n", c=32),
        in_=io["w2tT"].rearrange("(c p) n -> p c n", p=128)), inc="gld", inc_n=16)
    ctx.op("gpsimd", lambda e: e.dma_start(out=bufs["G_b1"][0:1, :], in_=io["b1q"].unsqueeze(0)), inc="gld", inc_n=16)
    ctx.op("gpsimd", lambda e: e.dma_start(out=bufs["G_b2"][0:1, :], in_=io["b2q"].unsqueeze(0)), inc="gld", inc_n=16)
    gpre = ctx.cnt["gld"]

    xn1T_r = io["xn1T"].rearrange("(c p) n -> p c n", p=128)
    h_ev = {}
    y_ev = {}
    st_tok = [0, 0]
    ld_tok = [0, 0]
    x1_tok = [0, 0]
    s1_tok = [0, 0]
    gate_done = [0, 0]
    phase1_done = [0] * 16

    for nb in range(16):
        b = nb % 2
        w = [("sT", phase1_done[nb - 2])] if nb >= 2 else []
        ld_tok[b] = ctx.op("sync", lambda e, nb=nb, b=b: e.dma_start(
            out=bufs["G_xnb"][b].rearrange("p (c n) -> p c n", c=8),
            in_=xn1T_r[:, :, 256 * nb:256 * nb + 256]), waits=w, inc=f"d{b}", inc_n=16)
        for j in range(32):
            bk = j % 2
            for c in range(NCH):
                wt = []
                if c == 0:
                    wt.append((f"d{b}", ld_tok[b]))
                    if nb == 0 and j < 2:
                        wt.append(("gld", gpre))
                    if (nb, j - 2) in h_ev:
                        wt.append(("sC", h_ev[(nb, j - 2)]))
                    elif (nb - 1, j + 30) in h_ev:
                        wt.append(("sC", h_ev[(nb - 1, j + 30)]))
                ctx.op("tensor", lambda e, c=c, j=j, b=b, bk=bk: e.matmul(
                    psum[bk][:, 0:256],
                    bufs["G_w1"][:, 4096 * c + 128 * j:4096 * c + 128 * j + 128],
                    bufs["G_xnb"][b][:, 256 * c:256 * c + 256],
                    start=(c == 0), stop=False), waits=wt)
            ctx.op("tensor", lambda e, j=j, bk=bk: e.matmul(
                psum[bk][:, 0:256], bufs["G_b1"][0:1, 128 * j:128 * j + 128],
                bufs["ones_sb"][0:1, 0:256], start=False, stop=True), inc="sT")
            mmt = ctx.cnt["sT"]
            wg = [("sT", mmt)]
            if (nb, j - 2) in h_ev or (nb - 1, j + 30) in h_ev:
                pass
            h_ev[(nb, j)] = ctx.op("scalar", lambda e, j=j, bk=bk: e.activation(
                bufs["G_h"][:, 256 * j:256 * j + 256], psum[bk][:, 0:256], AF.Gelu,
                bias=cc[:, C_ZERO:C_ZERO + 1], scale=cc[:, C_ONE:C_ONE + 1]),
                waits=wg, inc="sC", drain=True)
        phase1_done[nb] = ctx.cnt["sT"]
        h_all = ctx.cnt["sC"]
        for t in range(2):
            i = 2 * nb + t
            pb_pair = (2, 3) if t == 0 else (4, 5)
            wl = [("sV", gate_done[b])] if gate_done[b] else []
            x1_tok[b] = ctx.op("sync", lambda e, i=i, b=b: e.dma_start(
                out=bufs["G_x1"][b], in_=io["x1"][128 * i:128 * i + 128, :]),
                waits=wl, inc=f"d{8 + b}", inc_n=16)
            s1_tok[b] = ctx.op("sync", lambda e, i=i, b=b: e.dma_start(
                out=bufs["G_s1"][b], in_=io["s1"][128 * i:128 * i + 128, :]),
                inc=f"d{13 + b}", inc_n=16)
            for dh in range(2):
                pk = pb_pair[dh]
                for j in range(32):
                    wt = []
                    if j == 0:
                        wt.append(("sC", h_all))
                        if (i - 2, dh) in y_ev:
                            wt.append(("sC", y_ev[(i - 2, dh)]))
                    ctx.op("tensor", lambda e, j=j, t=t, dh=dh, pk=pk: e.matmul(
                        psum[pk][:, :],
                        bufs["G_h"][:, 256 * j + 128 * t:256 * j + 128 * t + 128],
                        bufs["G_w2"][:, 1024 * j + 512 * dh:1024 * j + 512 * dh + 512],
                        start=(j == 0), stop=False), waits=wt)
                ctx.op("tensor", lambda e, dh=dh, pk=pk: e.matmul(
                    psum[pk][:, :], bufs["ones_sb"][0:1, 0:128],
                    bufs["G_b2"][0:1, 512 * dh:512 * dh + 512], start=False, stop=True), inc="sT")
                ymm = ctx.cnt["sT"]
                wsl = [("sT", ymm)]
                if dh == 0 and (gate_done[0] or gate_done[1]):
                    wsl.append(("sV", max(gate_done)))
                y_ev[(i, dh)] = ctx.op("scalar", lambda e, dh=dh, pk=pk: e.activation(
                    bufs["G_g"][:, 512 * dh:512 * dh + 512], psum[pk][:, :], AF.Silu,
                    bias=cc[:, C_ZERO:C_ZERO + 1], scale=cc[:, C_ONE:C_ONE + 1]),
                    waits=wsl, inc="sC", drain=True)
            wv = [("sC", y_ev[(i, 1)]), (f"d{8 + b}", x1_tok[b]), (f"d{13 + b}", s1_tok[b])]
            if st_tok[b] > 0:
                wv.append((f"d{2 + b}", st_tok[b]))
            ctx.op("vector", lambda e, b=b: e.tensor_tensor(
                bufs["G_scr"], bufs["G_g"], bufs["G_s1"][b], ALU.mult), waits=wv, drain=True)
            gate_done[b] = ctx.op("vector", lambda e, b=b: e.tensor_tensor(
                bufs["G_o"][b], bufs["G_scr"], bufs["G_x1"][b], ALU.add), inc="sV", drain=True)
            st_tok[b] = ctx.op("sync", lambda e, i=i, b=b: e.dma_start(
                out=io["out"][128 * i:128 * i + 128, :], in_=bufs["G_o"][b]),
                waits=[("sV", gate_done[b])], inc=f"d{2 + b}", inc_n=16)

def alloc_stageDE(nc, ar, bufs):
    alloc_stageD(nc, ar, bufs)
    alloc_stageE(nc, ar, bufs)


def emit_stageDE(ctx, bufs, io):
    emit_stageD(ctx, bufs, io)
    emit_stageE(ctx, bufs, io)


STAGES = {
    "A": (alloc_stageA, emit_stageA),
    "B": (alloc_stageB, emit_stageB),
    "C": (alloc_stageC, emit_stageC),
    "D": (alloc_stageD, emit_stageD),
    "E": (alloc_stageE, emit_stageE),
    "DE": (alloc_stageDE, emit_stageDE),
    "F": (alloc_stageF, emit_stageF),
    "G": (alloc_stageG, emit_stageG),
}
STAGE_IO["DE"] = (sorted(set(STAGE_IO["D"][0]) | set(STAGE_IO["E"][0])),
                  sorted(set(STAGE_IO["D"][1]) | set(STAGE_IO["E"][1])), [])


def build_group(stages, keep_outputs=None):
    table = dram_table()
    reads, writes, rws = set(), set(), set()
    for s in stages:
        r, w, rw = STAGE_IO[s]
        for name in r:
            if name not in writes and name not in rws:
                reads.add(name)
        for name in rw:
            if name not in writes and name not in rws:
                rws.add(name)
            writes.add(name)
        writes.update(w)
    if keep_outputs is None:
        keep_outputs = set(writes) | set(rws)
    nc = bass.Bass("TRN2", target_bir_lowering=False)
    io = {}
    in_names, out_names = [], []
    seed_pairs = []
    for name in sorted(rws):
        sh, dt = table[name]
        io[name + "__seed"] = nc.dram_tensor(name + "__seed", sh, dt, kind="ExternalInput").ap()
        in_names.append(name + "__seed")
        kind = "ExternalOutput" if name in keep_outputs else "Internal"
        io[name] = nc.dram_tensor(name, sh, dt, kind=kind).ap()
        if kind == "ExternalOutput":
            out_names.append(name)
        seed_pairs.append(name)
    for name in sorted(reads - rws):
        sh, dt = table[name]
        io[name] = nc.dram_tensor(name, sh, dt, kind="ExternalInput").ap()
        in_names.append(name)
    for name in sorted(writes - rws):
        sh, dt = table[name]
        kind = "ExternalOutput" if name in keep_outputs else "Internal"
        io[name] = nc.dram_tensor(name, sh, dt, kind=kind).ap()
        if kind == "ExternalOutput":
            out_names.append(name)

    ar = Arena(nc)
    bufs = {}
    bufs["cconst"] = ar.alloc(NCONST, F32)
    bufs["ident_sb"] = ar.alloc(128, BF16)
    bufs["ident_f"] = ar.alloc(128, F32)
    bufs["ones_sb"] = ar.alloc(512, BF16, parts=1)
    bufs["tri_lo_sb"] = ar.alloc(128, BF16)
    bufs["tri_up_sb"] = ar.alloc(128, BF16)
    bufs["tri_lo4_sb"] = ar.alloc(512, BF16)
    bufs["tri_up4_sb"] = ar.alloc(512, BF16)
    psum_cms = [nc.psum_tensor(f"pb{i}", [128, 512], F32) for i in range(8)]
    psums = [cm.__enter__() for cm in psum_cms]
    bufs["psum"] = [p[:, :] for p in psums]
    stage_base = ar.off
    hi = stage_base
    for s in stages:
        ar.off = stage_base
        STAGES[s][0](nc, ar, bufs)
        hi = max(hi, ar.off)
    ar.off = hi

    sems = {}
    sem_cms = {}
    for sn in SEM_NAMES:
        cm = nc.semaphore(sn)
        sem_cms[sn] = cm
        sems[sn] = cm.__enter__()

    with nc.Block() as blk:
        def body(eng_name):
            def run(eng):
                ctx = Ctx(sems, eng_name, eng)
                ctx.op("gpsimd", lambda e: e.dma_start(out=bufs["cconst"], in_=io["cconst"]), inc="gld", inc_n=16)
                if "ident" in io:
                    ctx.op("gpsimd", lambda e: e.dma_start(out=bufs["ident_sb"], in_=io["ident"]), inc="gld", inc_n=16)
                    ctx.op("sync", lambda e: e.dma_start(out=bufs["ident_f"], in_=io["ident"]), inc="qld", inc_n=16)
                if "ones_row" in io:
                    ctx.op("gpsimd", lambda e: e.dma_start(out=bufs["ones_sb"], in_=io["ones_row"]), inc="gld", inc_n=16)
                if "tri_lo" in io:
                    ctx.op("gpsimd", lambda e: e.dma_start(out=bufs["tri_lo_sb"], in_=io["tri_lo"]), inc="gld", inc_n=16)
                if "tri_up" in io:
                    ctx.op("gpsimd", lambda e: e.dma_start(out=bufs["tri_up_sb"], in_=io["tri_up"]), inc="gld", inc_n=16)
                if "tri_lo4" in io:
                    ctx.op("gpsimd", lambda e: e.dma_start(out=bufs["tri_lo4_sb"], in_=io["tri_lo4"]), inc="gld", inc_n=16)
                if "tri_up4" in io:
                    ctx.op("gpsimd", lambda e: e.dma_start(out=bufs["tri_up4_sb"], in_=io["tri_up4"]), inc="gld", inc_n=16)
                for name in seed_pairs:
                    ctx.op("sync", lambda e, name=name: e.dma_start(
                        out=io[name], in_=io[name + "__seed"]), inc="qld", inc_n=16)
                ctx.wait("tensor", ["gld", "qld"])
                ctx.wait("vector", ["gld", "qld"])
                ctx.wait("scalar", ["gld", "qld"])
                ctx.wait("sync", ["gld", "qld"])
                ctx.wait("gpsimd", ["gld", "qld"])
                for s in stages:
                    STAGES[s][1](ctx, bufs, io)
                    ctx.barrier()
                all_dma = ["qst", "gld", "qld"] + [f"d{i}" for i in range(20)]
                ctx.wait("sync", all_dma)
                ctx.barrier()
            return run
        blk.tensor(body("tensor"))
        blk.vector(body("vector"))
        blk.scalar(body("scalar"))
        blk.gpsimd(body("gpsimd"))
        blk.sync(body("sync"))

    return nc, in_names, out_names


GROUPS = [["A", "B", "C", "DE", "F", "G"]]


def kernel(**inputs):
    consts = build_consts()
    rename = {"wq1": "L1_attn_wq", "wk1": "L1_attn_wk", "wv1": "L1_attn_wv",
              "wo1": "L1_attn_wo", "bo1": "L1_attn_bo", "omega": "L1_attn_omega",
              "W1q": "L1_ffn_W1", "b1q": "L1_ffn_b1", "W2q": "L1_ffn_W2", "b2q": "L1_ffn_b2"}
    pool = {}
    for k, v in inputs.items():
        pool[k] = np.ascontiguousarray(v)
    for k in consts:
        pool[k] = consts[k]

    def fetch(name, core):
        base = name[:-6] if name.endswith("__seed") else name
        if base == "x":
            return np.ascontiguousarray(pool["L0_attn_x"][core])
        if base in consts:
            return consts[base]
        if base in rename:
            return pool[rename[base]]
        if base in ("Wq", "bq", "Wk", "bk", "Wvd", "bvd", "Wvu", "bvu"):
            return pool["L0_attn_" + base]
        if base in ("Wqkv", "q_scale", "k_scale", "Wo", "ln_g", "ln_b", "W1", "b1", "W2", "b2"):
            return pool["L0_ffn_" + base]
        return pool["__scratch__" + base][core]

    n_cores = 8
    for gi, stages in enumerate(GROUPS):
        needed = {"out"}
        for later in GROUPS[gi + 1:]:
            for s in later:
                r, w, rw = STAGE_IO[s]
                needed.update(r)
                needed.update(rw)
        nc, in_names, out_names = build_group(stages, needed)
        in_maps = []
        for core in range(n_cores):
            m = {name: fetch(name, core) for name in in_names}
            in_maps.append(m)
        res = run_bass_kernel_spmd(nc, in_maps, core_ids=list(range(n_cores)))
        for name in out_names:
            pool["__scratch__" + name] = np.stack([res.results[c][name] for c in range(n_cores)])

    return pool["__scratch__out"]
